# version 03
# baseline (speedup 1.0000x reference)
import os
import numpy as np
import ml_dtypes
import concourse.bass as bass
import concourse.mybir as mybir
from concourse.bass import ds
from concourse.bass_utils import run_bass_kernel_spmd
from contextlib import ExitStack

F32 = mybir.dt.float32
BF16 = mybir.dt.bfloat16
AF = mybir.ActivationFunctionType
ALU = mybir.AluOpType
AX = mybir.AxisListType.X

D = 1024
HF = 4096
NH = 16
TFULL = 8192
NB = 16
TB = 512
EPS = 1e-5
EXPC = 12.0
CONSTS = [EPS, -EXPC, 0.125, 1.0 / 1024.0, 1.0 / 4096.0, 0.5]


STOP_AFTER = os.environ.get("KSTOP", "")
KDBG = os.environ.get("KDBG", "") == "1"



def build_nc():
    nc = bass.Bass("TRN2", target_bir_lowering=False, debug=False)

    xin = nc.dram_tensor("xin", [TFULL, D], F32, kind="ExternalInput").ap()
    w1T = nc.dram_tensor("w1T", [D, HF], F32, kind="ExternalInput").ap()
    w2T = nc.dram_tensor("w2T", [HF, D], F32, kind="ExternalInput").ap()
    wqT = nc.dram_tensor("wqT", [D, D], F32, kind="ExternalInput").ap()
    wkT = nc.dram_tensor("wkT", [D, D], F32, kind="ExternalInput").ap()
    wvT = nc.dram_tensor("wvT", [D, D], F32, kind="ExternalInput").ap()
    woT = nc.dram_tensor("woT", [D, D], F32, kind="ExternalInput").ap()
    b1i = nc.dram_tensor("b1", [HF], F32, kind="ExternalInput").ap()
    b2i = nc.dram_tensor("b2", [D], F32, kind="ExternalInput").ap()
    bqi = nc.dram_tensor("bq", [D], F32, kind="ExternalInput").ap()
    bki = nc.dram_tensor("bk", [D], F32, kind="ExternalInput").ap()
    bvi = nc.dram_tensor("bv", [D], F32, kind="ExternalInput").ap()
    boi = nc.dram_tensor("bo", [D], F32, kind="ExternalInput").ap()
    rci = nc.dram_tensor("rotc", [32], F32, kind="ExternalInput").ap()
    rsi = nc.dram_tensor("rots", [32], F32, kind="ExternalInput").ap()
    out = nc.dram_tensor("out", [2048, D], F32, kind="ExternalOutput").ap()
    if KDBG:
        dbg_pp = nc.dram_tensor("dbg_pp", [128, 4096], F32, kind="ExternalOutput").ap()
        dbg_bw1 = nc.dram_tensor("dbg_bw1", [128, 512], BF16, kind="ExternalOutput").ap()
        dbg_bw2 = nc.dram_tensor("dbg_bw2", [128, 512], BF16, kind="ExternalOutput").ap()
        dbg_x1 = nc.dram_tensor("dbg_x1", [512, D], F32, kind="ExternalOutput").ap()
        dbg_k = nc.dram_tensor("dbg_k", [128, 512], BF16, kind="ExternalOutput").ap()
        dbg_q = nc.dram_tensor("dbg_q", [128, 1024], BF16, kind="ExternalOutput").ap()
        dbg_v = nc.dram_tensor("dbg_v", [128, 1040], BF16, kind="ExternalOutput").ap()
        dbg_o = nc.dram_tensor("dbg_o", [128, 2048], BF16, kind="ExternalOutput").ap()

    bw1d = nc.dram_tensor("bw1d", [D, HF], BF16, kind="Internal").ap()
    bw2d = nc.dram_tensor("bw2d", [HF, D], BF16, kind="Internal").ap()
    kd = nc.dram_tensor("kd", [D, TFULL], BF16, kind="Internal").ap()
    vd = nc.dram_tensor("vd", [64, 128, 16 * 65], BF16, kind="Internal").ap()
    qd = nc.dram_tensor("qd", [NH, 128, 8 * 128], BF16, kind="Internal").ap()
    x1d = nc.dram_tensor("x1d", [TFULL, D], F32, kind="Internal").ap()
    xnqd = nc.dram_tensor("xnqd", [4, 128, D], BF16, kind="Internal").ap()
    rowb = nc.dram_tensor("rowb", [1024], F32, kind="Internal").ap()
    al1d = nc.dram_tensor("al1d", [HF], F32, kind="Internal").ap()
    al2d = nc.dram_tensor("al2d", [D], F32, kind="Internal").ap()
    recb = nc.dram_tensor("recb", [512], F32, kind="Internal").ap()

    gctx = ExitStack()
    gctx.__enter__()

    def mksb(ctx, name, shape, dt):
        return ctx.enter_context(nc.sbuf_tensor(name, shape, dt))

    def mkps(ctx, name, shape, dt):
        return ctx.enter_context(nc.psum_tensor(name, shape, dt))

    class Ctr:
        def __init__(self, name):
            self.h = gctx.enter_context(nc.semaphore(name))
            self.v = 0

        def inc(self, inst, n=1):
            inst.then_inc(self.h, n)
            self.v += n
            return self.v

    sLD = Ctr("sLD")
    sSC = Ctr("sSC")
    sGP = Ctr("sGP")
    sPE = Ctr("sPE")
    sAC = Ctr("sAC")
    sDV = Ctr("sDV")
    sST = Ctr("sST")
    sBR = Ctr("sBR")

    identb = mksb(gctx, "identb", [128, 128], BF16)
    onesb = mksb(gctx, "onesb", [1, 512], BF16)
    ones64b = mksb(gctx, "ones64b", [128, 64], BF16)
    b1h = mksb(gctx, "b1h", [1, HF], BF16)
    b1l = mksb(gctx, "b1l", [1, HF], BF16)
    b2h = mksb(gctx, "b2h", [1, D], BF16)
    b2l = mksb(gctx, "b2l", [1, D], BF16)
    bvh = mksb(gctx, "bvh", [1, D], BF16)
    bvl = mksb(gctx, "bvl", [1, D], BF16)
    boh = mksb(gctx, "boh", [1, D], BF16)
    bol = mksb(gctx, "bol", [1, D], BF16)
    bqh = mksb(gctx, "bqh", [1, D], BF16)
    bql = mksb(gctx, "bql", [1, D], BF16)
    bkh = mksb(gctx, "bkh", [1, D], BF16)
    bkl = mksb(gctx, "bkl", [1, D], BF16)

    blk = gctx.enter_context(nc.Block())

    @blk.sync
    def _(sync):
        te, ve, sc, gp, sy = nc.tensor, nc.vector, nc.scalar, nc.gpsimd, nc.sync
        pid = sy.partition_id()
        ccv = sy.compute_val(pid % 4)
        ccv128 = sy.compute_val(pid % 4 * 128)

        def wpe(e):
            e.wait_ge(sPE.h, sPE.v)

        def wac(e):
            e.wait_ge(sAC.h, sAC.v)

        def wdv(e):
            e.wait_ge(sDV.h, sDV.v)

        def wld(e):
            e.wait_ge(sLD.h, sLD.v)

        def wsc(e):
            e.wait_ge(sSC.h, sSC.v)

        def wst(e):
            e.wait_ge(sST.h, sST.v)

        def wgp(e):
            e.wait_ge(sGP.h, sGP.v)

        def barrier():
            for e in (te, ve, sc, gp, sy):
                sBR.inc(e.drain(), 1)
            tgt = sBR.v
            for e in (te, ve, sc, gp, sy):
                e.wait_ge(sBR.h, tgt)

        actx = ExitStack()
        actx.__enter__()
        wqRT = mksb(actx, "wqRT", [128, 8 * D], BF16)
        wkRT = mksb(actx, "wkRT", [128, 8 * D], BF16)
        wvTb = mksb(actx, "wvTb", [128, 8 * D], BF16)

        pctx = ExitStack()
        pctx.__enter__()
        crow = mksb(pctx, "crow", [1, 128], F32)
        srow = mksb(pctx, "srow", [1, 128], F32)
        ssrow = mksb(pctx, "ssrow", [1, 128], F32)
        signp = mksb(pctx, "signp", [1, 128], F32)
        cbt = mksb(pctx, "cbt", [128, 128], F32)
        ssbt = mksb(pctx, "ssbt", [128, 128], F32)
        brow_f = mksb(pctx, "brow_f", [1, 6 * D + HF], F32)
        btmp = mksb(pctx, "btmp", [1, 2 * D], F32)
        wld_t = mksb(pctx, "wld_t", [128, D], F32)
        wtmp = mksb(pctx, "wtmp", [128, 256], F32)
        wsl = mksb(pctx, "wsl", [128, 8 * 512], F32)
        wab = mksb(pctx, "wab", [128, 512], BF16)
        alc = mksb(pctx, "alc", [128, 4], F32)
        abca = mksb(pctx, "abca", [128, 512], F32)
        wsg = mksb(pctx, "wsg", [128, 8 * 512], BF16)
        psP2 = mkps(pctx, "psP2", [128, 512], F32)

        gp.memset(onesb[:], 1.0)
        gp.memset(ones64b[:], 1.0)
        gp.memset(identb[:], 0.0)
        gp.affine_select(out=identb[:], in_=identb[:], compare_op=ALU.not_equal,
                         fill=1.0, base=0, pattern=[[-1, 128]], channel_multiplier=1)
        gp.memset(signp[:], 1.0)
        sGP.inc(gp.drain(), 1)
        wgp(ve)
        spv = signp[:].rearrange("p (i e) -> p i e", e=2)
        ve.tensor_single_scalar(spv[:, :, 0], spv[:, :, 0], -1.0, ALU.mult)
        sDV.inc(ve.drain(), 1)
        for e in (te, ve, sc, sy):
            wgp(e)

        OB1, OB2, OBV, OBO, OBQ, OBK = 0, HF, HF + D, HF + 2 * D, HF + 3 * D, HF + 4 * D
        for off, t in ((OB1, b1i), (OB2, b2i), (OBV, bvi), (OBO, boi),
                       (OBQ, bqi), (OBK, bki)):
            sLD.inc(sy.dma_start(out=brow_f[0:1, off:off + t.shape[0]],
                                 in_=t[None, :]), 16)
        rr = btmp
        sLD.inc(sy.dma_start(out=rr[0:1, 0:32], in_=rci[None, :]), 16)
        sLD.inc(sy.dma_start(out=rr[0:1, 32:64], in_=rsi[None, :]), 16)
        wld(ve)
        for b in range(2):
            for e in range(2):
                cv = crow[0:1, b * 64:(b + 1) * 64].rearrange("p (i e) -> p i e", e=2)
                sv = srow[0:1, b * 64:(b + 1) * 64].rearrange("p (i e) -> p i e", e=2)
                ve.tensor_copy(cv[:, :, e], rr[0:1, 0:32])
                ve.tensor_copy(sv[:, :, e], rr[0:1, 32:64])
        ve.tensor_mul(ssrow[:], srow[:], signp[:])
        sDV.inc(ve.drain(), 1)
        wdv(sy)
        sST.inc(sy.dma_start(out=rowb[None, 0:128], in_=crow[:]), 16)
        sST.inc(sy.dma_start(out=rowb[None, 128:256], in_=ssrow[:]), 16)
        wst(sy)
        sLD.inc(sy.dma_start(out=cbt[:], in_=bass.AP(rowb.tensor, 0, [[0, 128], [1, 128]]),
                             allow_slow_non_contiguous=True), 16)
        sLD.inc(sy.dma_start(out=ssbt[:], in_=bass.AP(rowb.tensor, 128, [[0, 128], [1, 128]]),
                             allow_slow_non_contiguous=True), 16)

        wld(ve)
        for off in (OBQ, OBK):
            for ch in range(8):
                s0 = brow_f[0:1, off + ch * 128: off + (ch + 1) * 128]
                t0 = btmp[0:1, ch * 128:(ch + 1) * 128]
                sv = s0.rearrange("p (i e) -> p i e", e=2)
                tv = t0.rearrange("p (i e) -> p i e", e=2)
                ssv = ssrow[0:1, :].rearrange("p (i e) -> p i e", e=2)
                ve.tensor_tensor(tv[:, :, 0], sv[:, :, 1], ssv[:, :, 0], ALU.mult)
                ve.tensor_tensor(tv[:, :, 1], sv[:, :, 0], ssv[:, :, 1], ALU.mult)
                ve.tensor_tensor(s0, s0, crow[0:1, :], ALU.mult)
                ve.tensor_tensor(s0, s0, t0, ALU.add)
        SPLITS = ((b1h, b1l, OB1, HF), (b2h, b2l, OB2, D),
                  (bvh, bvl, OBV, D), (boh, bol, OBO, D),
                  (bqh, bql, OBQ, D), (bkh, bkl, OBK, D))
        for hi, lo, off, ln in SPLITS:
            ve.tensor_copy(hi[:], brow_f[0:1, off:off + ln])
        sDV.inc(ve.drain(), 1)
        wdv(ve)
        for hi, lo, off, ln in SPLITS:
            ve.tensor_tensor(brow_f[0:1, off:off + ln],
                             brow_f[0:1, off:off + ln], hi[:], ALU.subtract)
        sDV.inc(ve.drain(), 1)
        wdv(ve)
        for hi, lo, off, ln in SPLITS:
            ve.tensor_copy(lo[:], brow_f[0:1, off:off + ln])
        sDV.inc(ve.drain(), 1)

        for wsrc, wdst, roped in ((wqT, wqRT, True), (wkT, wkRT, True),
                                  (wvT, wvTb, False)):
            for j in range(8):
                wdv(sy)
                sLD.inc(sy.dma_start(out=wld_t[:], in_=wsrc[j * 128:(j + 1) * 128, :]), 16)
                wld(ve)
                if roped:
                    for ch in range(8):
                        s0 = wld_t[:, ch * 128:(ch + 1) * 128]
                        sv = s0.rearrange("p (i e) -> p i e", e=2)
                        t1 = wtmp[:, 0:128]
                        t2 = wtmp[:, 128:256]
                        t2v = t2.rearrange("p (i e) -> p i e", e=2)
                        ssv = ssbt[:].rearrange("p (i e) -> p i e", e=2)
                        ve.tensor_tensor(t2v[:, :, 0], sv[:, :, 1], ssv[:, :, 0], ALU.mult)
                        ve.tensor_tensor(t2v[:, :, 1], sv[:, :, 0], ssv[:, :, 1], ALU.mult)
                        ve.tensor_tensor(t1, s0, cbt[:], ALU.mult)
                        ve.tensor_tensor(wdst[:, j * D + ch * 128: j * D + (ch + 1) * 128],
                                         t1, t2, ALU.add)
                else:
                    ve.tensor_copy(wdst[:, j * D:(j + 1) * D], wld_t[:])
                sDV.inc(ve.drain(), 1)

        if STOP_AFTER == "prep3":
            if KDBG:
                sST.inc(sy.dma_start(out=dbg_pp[0:1, 0:3072], in_=brow_f[0:1, HF:HF + 3 * D]), 16)
                sST.inc(sy.dma_start(out=dbg_pp[1:2, 0:128], in_=crow[:]), 16)
                sST.inc(sy.dma_start(out=dbg_pp[1:2, 128:256], in_=ssrow[:]), 16)
                sST.inc(sy.dma_start(out=dbg_pp[2:34, 0:128], in_=cbt[0:32, :]), 16)
                wdv(ve)

                ve.tensor_copy(brow_f[0:1, 0:1024], bass.AP(wqRT, 0, [[8192, 1], [1, 1024]]))
                sDV.inc(ve.drain(), 1)
                wdv(sy)
                sST.inc(sy.dma_start(out=dbg_pp[40:41, 0:1024], in_=brow_f[0:1, 0:1024]), 16)
                sy.wait_ge(sST.h, sST.v)
            pctx.__exit__(None, None, None)
            actx.__exit__(None, None, None)
            return
        for slab in range(8):
            wst(sy)
            wdv(sy)
            for j in range(8):
                sLD.inc(sy.dma_start(
                    out=wsl[:, j * 512:(j + 1) * 512],
                    in_=w1T[j * 128:(j + 1) * 128, slab * 512:(slab + 1) * 512]), 16)
            wld(sc)
            wdv(te)
            for hs in range(4):
                for j in range(8):
                    wpe(sc)
                    sc.activation(wab[:, 0:128], wsl[:, j * 512 + hs * 128: j * 512 + (hs + 1) * 128], AF.Abs)
                    sAC.inc(sc.drain(), 1)
                    wac(te)
                    te.matmul(psP2[:, hs * 64:(hs + 1) * 64], wab[:, 0:128], ones64b[:],
                              start=(hs == 0 and j == 0), stop=(hs == 3 and j == 7),
                              skip_group_check=True)
                    sPE.inc(te.drain(), 1)
            wpe(sc)
            sc.activation(alc[:], psP2[:, 0:256].rearrange("p (h s) -> p h s", h=4)[:, :, 0],
                          AF.Identity, scale=np.float32(1.0 / 1024.0))
            sAC.inc(sc.drain(), 1)
            wac(sy)
            sST.inc(sy.dma_start(
                out=bass.AP(al1d.tensor, slab * 512, [[1, 128], [128, 4]]),
                in_=alc[:], allow_slow_non_contiguous=True), 16)
            wst(sy)
            sLD.inc(sy.dma_start(
                out=abca[:], in_=bass.AP(al1d.tensor, slab * 512, [[0, 128], [1, 512]]),
                allow_slow_non_contiguous=True), 16)
            wst(sc)
            for j in range(8):
                sc.sign(wsg[:, j * 512:(j + 1) * 512], wsl[:, j * 512:(j + 1) * 512])
            sAC.inc(sc.drain(), 1)
            wac(ve)
            wld(ve)
            for j in range(8):
                ve.tensor_tensor(wsg[:, j * 512:(j + 1) * 512],
                                 wsg[:, j * 512:(j + 1) * 512], abca[:], ALU.mult)
            sDV.inc(ve.drain(), 1)
            wdv(sy)
            for j in range(8):
                sST.inc(sy.dma_start(
                    out=bw1d[j * 128:(j + 1) * 128, slab * 512:(slab + 1) * 512],
                    in_=wsg[:, j * 512:(j + 1) * 512]), 16)

        for slab in range(2):
            for half in range(4):
                wst(sy)
                wdv(sy)
                for j in range(8):
                    hc = half * 8 + j
                    sLD.inc(sy.dma_start(
                        out=wsl[:, j * 512:(j + 1) * 512],
                        in_=w2T[hc * 128:(hc + 1) * 128, slab * 512:(slab + 1) * 512]), 16)
                wld(sc)
                wdv(te)
                for ds_ in range(4):
                    for j in range(8):
                        wpe(sc)
                        sc.activation(wab[:, 0:128],
                                      wsl[:, j * 512 + ds_ * 128: j * 512 + (ds_ + 1) * 128],
                                      AF.Abs)
                        sAC.inc(sc.drain(), 1)
                        wac(te)
                        te.matmul(psP2[:, ds_ * 64:(ds_ + 1) * 64], wab[:, 0:128], ones64b[:],
                                  start=(half == 0 and ds_ == 0 and j == 0),
                                  stop=(half == 3 and ds_ == 3 and j == 7),
                                  skip_group_check=True)
                        sPE.inc(te.drain(), 1)
                wst(sc)
                for j in range(8):
                    sc.sign(wsg[:, j * 512:(j + 1) * 512], wsl[:, j * 512:(j + 1) * 512])
                sAC.inc(sc.drain(), 1)
                wac(sy)
                for j in range(8):
                    hc = half * 8 + j
                    sST.inc(sy.dma_start(
                        out=bw2d[hc * 128:(hc + 1) * 128, slab * 512:(slab + 1) * 512],
                        in_=wsg[:, j * 512:(j + 1) * 512]), 16)
            wpe(sc)
            sc.activation(alc[:], psP2[:, 0:256].rearrange("p (h s) -> p h s", h=4)[:, :, 0],
                          AF.Identity, scale=np.float32(1.0 / 4096.0))
            sAC.inc(sc.drain(), 1)
            wac(sy)
            sST.inc(sy.dma_start(
                out=bass.AP(al2d.tensor, slab * 512, [[1, 128], [128, 4]]),
                in_=alc[:], allow_slow_non_contiguous=True), 16)
            wst(sy)
            sLD.inc(sy.dma_start(
                out=abca[:], in_=bass.AP(al2d.tensor, slab * 512, [[0, 128], [1, 512]]),
                allow_slow_non_contiguous=True), 16)
            for half in range(4):
                wst(sy)
                wdv(sy)
                for j in range(8):
                    hc = half * 8 + j
                    sLD.inc(sy.dma_start(
                        out=wsg[:, j * 512:(j + 1) * 512],
                        in_=bw2d[hc * 128:(hc + 1) * 128, slab * 512:(slab + 1) * 512]), 16)
                wld(ve)
                for j in range(8):
                    ve.tensor_tensor(wsg[:, j * 512:(j + 1) * 512],
                                     wsg[:, j * 512:(j + 1) * 512], abca[:], ALU.mult)
                sDV.inc(ve.drain(), 1)
                wdv(sy)
                for j in range(8):
                    hc = half * 8 + j
                    sST.inc(sy.dma_start(
                        out=bw2d[hc * 128:(hc + 1) * 128, slab * 512:(slab + 1) * 512],
                        in_=wsg[:, j * 512:(j + 1) * 512]), 16)
        wst(sy)
        barrier()
        pctx.__exit__(None, None, None)
        if STOP_AFTER == "prep":
            if KDBG:
                wst(sy)
                sLD.inc(sy.dma_start(out=wab[:, 0:512], in_=bw1d[0:128, 0:512]), 16)
                wld(ve)
                ve.tensor_copy(wsl[:, 0:512], wab[:, 0:512])
                sDV.inc(ve.drain(), 1)
                wdv(sy)
                sST.inc(sy.dma_start(out=dbg_pp[0:128, 1024:1536], in_=wsl[:, 0:512]), 16)
                wst(sy)
                sLD.inc(sy.dma_start(out=wab[:, 0:512], in_=bw2d[384:512, 512:1024]), 16)
                wld(ve)
                ve.tensor_copy(wsl[:, 0:512], wab[:, 0:512])
                sDV.inc(ve.drain(), 1)
                wdv(sy)
                sST.inc(sy.dma_start(out=dbg_pp[0:128, 1536:2048], in_=wsl[:, 0:512]), 16)
                sy.wait_ge(sST.h, sST.v)
            pctx.__exit__(None, None, None)
            actx.__exit__(None, None, None)
            return

        def make_ws(ctx, tag):
            w = {}
            w["xb"] = mksb(ctx, f"xb{tag}", [128, 4 * D], F32)
            w["sq"] = mksb(ctx, f"sq{tag}", [128, D], F32)
            w["st_s"] = mksb(ctx, f"sts{tag}", [128, 4], F32)
            w["st_q"] = mksb(ctx, f"stq{tag}", [128, 4], F32)
            w["mean4"] = mksb(ctx, f"mean{tag}", [128, 4], F32)
            w["var4"] = mksb(ctx, f"var{tag}", [128, 4], F32)
            w["rstd4"] = mksb(ctx, f"rstd{tag}", [128, 4], F32)
            w["nmr4"] = mksb(ctx, f"nmr{tag}", [128, 4], F32)
            w["nt1"] = mksb(ctx, f"nt1{tag}", [128, 4], F32)
            w["nt2"] = mksb(ctx, f"nt2{tag}", [128, 4], F32)
            w["xn_tm"] = mksb(ctx, f"xntm{tag}", [128, 4 * D], BF16)
            w["xn_fm"] = mksb(ctx, f"xnfm{tag}", [128, 8 * 512], BF16)
            w["h_fm"] = mksb(ctx, f"hfm{tag}", [128, 32 * 512], BF16)
            w["w1s"] = mksb(ctx, f"w1s{tag}", [128, 2, 8 * 512], BF16)
            w["w2s"] = mksb(ctx, f"w2s{tag}", [128, 2, 512], BF16)
            w["psT"] = mkps(ctx, f"psT{tag}", [128, 1024], BF16)
            w["psH"] = [mkps(ctx, f"psH{tag}a", [128, 512], F32),
                        mkps(ctx, f"psH{tag}b", [128, 512], F32)]
            w["psY"] = [mkps(ctx, f"psY{tag}{i}", [128, 512], F32) for i in range(4)]
            return w

        def emit_ln(w, src, dst):
            for t in range(4):
                xt = src[:, t * D:(t + 1) * D]
                ve.reduce_sum(w["st_s"][:, t:t + 1], xt, axis=AX)
                wdv(sc)
                sc.square(w["sq"][:], xt)
                sAC.inc(sc.drain(), 1)
                wac(ve)
                ve.reduce_sum(w["st_q"][:, t:t + 1], w["sq"][:], axis=AX)
                sDV.inc(ve.drain(), 1)
            ve.tensor_scalar_mul(w["mean4"][:], w["st_s"][:], 1.0 / D)
            sDV.inc(ve.drain(), 1)
            wdv(sc)
            sc.square(w["nt1"][:], w["mean4"][:])
            sAC.inc(sc.drain(), 1)
            wac(ve)
            ve.scalar_tensor_tensor(w["var4"][:], w["st_q"][:], 1.0 / D, w["nt1"][:],
                                    ALU.mult, ALU.subtract)
            sDV.inc(ve.drain(), 1)
            wdv(sc)
            sc.activation(w["rstd4"][:], w["var4"][:], AF.Sqrt, bias=np.float32(EPS))
            sAC.inc(sc.drain(), 1)
            wac(ve)
            ve.reciprocal(w["nt2"][:], w["rstd4"][:])
            sDV.inc(ve.drain(), 1)
            wdv(ve)
            ve.scalar_tensor_tensor(w["nmr4"][:], w["mean4"][:], -1.0, w["nt2"][:],
                                    ALU.mult, ALU.mult)
            sDV.inc(ve.drain(), 1)
            wdv(ve)
            wpe(ve)
            for t in range(4):
                ve.tensor_scalar(dst[:, t * D:(t + 1) * D], src[:, t * D:(t + 1) * D],
                                 w["nt2"][:, t:t + 1], w["nmr4"][:, t:t + 1],
                                 ALU.mult, ALU.add)
            sDV.inc(ve.drain(), 1)

        def emit_tp(w, src, dst, ntile=4):
            for t in range(ntile):
                wdv(te)
                for j in range(8):
                    te.transpose(w["psT"][:, j * 128:(j + 1) * 128],
                                 src[:, t * D + j * 128: t * D + (j + 1) * 128],
                                 identb[:])
                sPE.inc(te.drain(), 1)
                wpe(ve)
                dv = dst[:].rearrange("p (j n) -> p j n", j=8)
                sv = w["psT"][:].rearrange("p (j n) -> p j n", j=8)
                ve.tensor_copy(dv[:, :, t * 128:(t + 1) * 128], sv[:, :, :])
                sDV.inc(ve.drain(), 1)

        def emit_ffn(w, xres, yout):
            rec_pe_w1 = [0, 0]
            rec_ac_h = [0, 0]
            wdv(te)
            for hg in range(8):
                par = hg % 2
                sc.wait_ge(sPE.h, rec_pe_w1[par])
                for j in range(8):
                    sSC.inc(sc.dma_start(
                        out=w["w1s"][:, par, j * 512:(j + 1) * 512],
                        in_=bw1d[j * 128:(j + 1) * 128, hg * 512:(hg + 1) * 512]), 16)
                wsc(te)
                for ht in range(4):
                    hpar = ht % 2
                    hidx = hg * 4 + ht
                    te.wait_ge(sAC.h, rec_ac_h[hpar])
                    for j in range(8):
                        te.matmul(w["psH"][hpar][:],
                                  w["w1s"][:, par, j * 512 + ht * 128: j * 512 + (ht + 1) * 128],
                                  w["xn_fm"][:, j * 512:(j + 1) * 512],
                                  start=(j == 0), stop=False, skip_group_check=True)
                    te.matmul(w["psH"][hpar][:], b1h[0:1, hidx * 128:(hidx + 1) * 128],
                              onesb[0:1, :], start=False, stop=False, skip_group_check=True)
                    te.matmul(w["psH"][hpar][:], b1l[0:1, hidx * 128:(hidx + 1) * 128],
                              onesb[0:1, :], start=False, stop=True, skip_group_check=True)
                    sPE.inc(te.drain(), 1)
                    wpe(sc)
                    sc.activation(w["h_fm"][:, hidx * 512:(hidx + 1) * 512],
                                  w["psH"][hpar][:], AF.Gelu)
                    rec_ac_h[hpar] = sAC.inc(sc.drain(), 1)
                rec_pe_w1[par] = sPE.v
            wac(te)
            rec_pe_w2 = [0, 0]
            for dh in range(2):
                for hc in range(32):
                    par = hc % 2
                    sc.wait_ge(sPE.h, rec_pe_w2[par])
                    sSC.inc(sc.dma_start(
                        out=w["w2s"][:, par, :],
                        in_=bw2d[hc * 128:(hc + 1) * 128, dh * 512:(dh + 1) * 512]), 16)
                    wsc(te)
                    for tt in range(4):
                        te.matmul(w["psY"][tt][:],
                                  w["h_fm"][:, hc * 512 + tt * 128: hc * 512 + (tt + 1) * 128],
                                  w["w2s"][:, par, :],
                                  start=(hc == 0), stop=False, skip_group_check=True)
                    rec_pe_w2[par] = sPE.inc(te.drain(), 1)
                for tt in range(4):
                    te.matmul(w["psY"][tt][:], onesb[0:1, 0:128],
                              b2h[0:1, dh * 512:(dh + 1) * 512],
                              start=False, stop=False, skip_group_check=True)
                    te.matmul(w["psY"][tt][:], onesb[0:1, 0:128],
                              b2l[0:1, dh * 512:(dh + 1) * 512],
                              start=False, stop=True, skip_group_check=True)
                sPE.inc(te.drain(), 1)
                wpe(ve)
                for tt in range(4):
                    ve.scalar_tensor_tensor(
                        yout[:, tt * D + dh * 512: tt * D + (dh + 1) * 512],
                        w["psY"][tt][:], 0.5,
                        xres[:, tt * D + dh * 512: tt * D + (dh + 1) * 512],
                        ALU.mult, ALU.add)
                sDV.inc(ve.drain(), 1)
                wdv(te)

        wsA = make_ws(actx, "A")
        x1b = mksb(actx, "x1bA", [128, 4 * D], F32)
        kstage = mksb(actx, "kstA", [128, 2, 512], BF16)
        vstage = mksb(actx, "vstA", [128, 2, 520], BF16)
        qin = mksb(actx, "qinA", [128, D], BF16)
        qin_fm = mksb(actx, "qinfA", [128, 8 * 128], BF16)
        qstage = mksb(actx, "qstA", [128, 8 * 128], BF16)

        for bat in range(NB):
            wdv(sy)
            wac(sy)
            for t in range(4):
                sLD.inc(sy.dma_start(
                    out=wsA["xb"][:, t * D:(t + 1) * D],
                    in_=xin[bat * TB + t * 128: bat * TB + (t + 1) * 128, :]), 16)
            wld(ve)
            wld(sc)
            emit_ln(wsA, wsA["xb"], wsA["xn_tm"])
            emit_tp(wsA, wsA["xn_tm"], wsA["xn_fm"])
            if STOP_AFTER == "A1" and bat == 0:
                break
            emit_ffn(wsA, wsA["xb"], x1b)
            if STOP_AFTER == "A2" and bat == 0:
                break
            wdv(sy)
            for t in range(4):
                sST.inc(sy.dma_start(
                    out=x1d[bat * TB + t * 128: bat * TB + (t + 1) * 128, :],
                    in_=x1b[:, t * D:(t + 1) * D]), 16)
            emit_ln(wsA, x1b, wsA["xn_tm"])
            wdv(sy)
            for t in range(4):
                sST.inc(sy.dma_start(out=xnqd[t],
                                     in_=wsA["xn_tm"][:, t * D:(t + 1) * D]), 16)
            emit_tp(wsA, wsA["xn_tm"], wsA["xn_fm"])
            wdv(te)
            for mt in range(8):
                par = mt % 2
                wac(te)
                for j in range(8):
                    te.matmul(wsA["psH"][par][:],
                              wkRT[:, j * D + mt * 128: j * D + (mt + 1) * 128],
                              wsA["xn_fm"][:, j * 512:(j + 1) * 512],
                              start=(j == 0), stop=False, skip_group_check=True)
                te.matmul(wsA["psH"][par][:], bkh[0:1, mt * 128:(mt + 1) * 128],
                          onesb[0:1, :], start=False, stop=False, skip_group_check=True)
                te.matmul(wsA["psH"][par][:], bkl[0:1, mt * 128:(mt + 1) * 128],
                          onesb[0:1, :], start=False, stop=True, skip_group_check=True)
                sPE.inc(te.drain(), 1)
                wpe(sc)
                wst(sc)
                sc.activation(kstage[:, par, :], wsA["psH"][par][:], AF.Identity)
                sAC.inc(sc.drain(), 1)
                wac(sy)
                sST.inc(sy.dma_start(
                    out=kd[mt * 128:(mt + 1) * 128, bat * TB:(bat + 1) * TB],
                    in_=kstage[:, par, :]), 16)
            if STOP_AFTER == "A3" and bat == 0:
                break
            wac(te)
            for tt in range(4):
                for mh in range(2):
                    par = (tt * 2 + mh) % 2
                    wdv(te)
                    for j in range(8):
                        te.matmul(wsA["psH"][par][:],
                                  wsA["xn_fm"][:, j * 512 + tt * 128: j * 512 + (tt + 1) * 128],
                                  wvTb[:, j * D + mh * 512: j * D + (mh + 1) * 512],
                                  start=(j == 0), stop=False, skip_group_check=True)
                    te.matmul(wsA["psH"][par][:], onesb[0:1, 0:128],
                              bvh[0:1, mh * 512:(mh + 1) * 512],
                              start=False, stop=False, skip_group_check=True)
                    te.matmul(wsA["psH"][par][:], onesb[0:1, 0:128],
                              bvl[0:1, mh * 512:(mh + 1) * 512],
                              start=False, stop=True, skip_group_check=True)
                    sPE.inc(te.drain(), 1)
                    wst(gp)
                    gp.memset(vstage[:, par, :], 1.0)
                    sGP.inc(gp.drain(), 1)
                    wgp(ve)
                    wpe(ve)
                    for g in range(8):
                        ve.tensor_copy(vstage[:, par, g * 65:g * 65 + 64],
                                       wsA["psH"][par][:, g * 64:(g + 1) * 64])
                    sDV.inc(ve.drain(), 1)
                    wdv(sy)
                    sST.inc(sy.dma_start(
                        out=vd[bat * 4 + tt, :, mh * 520:(mh + 1) * 520],
                        in_=vstage[:, par, :]), 16)
            if STOP_AFTER == "A4" and bat == 0:
                break
            wst(sy)
            sLD.inc(sy.dma_start(
                out=qin[:],
                in_=xnqd[ds(ccv, 1), :, :].rearrange("o p n -> (o p) n")), 16)
            wld(te)
            wdv(te)
            for j in range(8):
                te.transpose(wsA["psT"][:, j * 128:(j + 1) * 128],
                             qin[:, j * 128:(j + 1) * 128], identb[:])
            sPE.inc(te.drain(), 1)
            wpe(ve)
            ve.tensor_copy(qin_fm[:], wsA["psT"][:])
            sDV.inc(ve.drain(), 1)
            wdv(te)
            wst(sc)
            for mt in range(8):
                par = mt % 2
                wac(te)
                for j in range(8):
                    te.matmul(wsA["psH"][par][:, 0:128],
                              wqRT[:, j * D + mt * 128: j * D + (mt + 1) * 128],
                              qin_fm[:, j * 128:(j + 1) * 128],
                              start=(j == 0), stop=False, skip_group_check=True)
                te.matmul(wsA["psH"][par][:, 0:128], bqh[0:1, mt * 128:(mt + 1) * 128],
                          onesb[0:1, 0:128], start=False, stop=False, skip_group_check=True)
                te.matmul(wsA["psH"][par][:, 0:128], bql[0:1, mt * 128:(mt + 1) * 128],
                          onesb[0:1, 0:128], start=False, stop=True, skip_group_check=True)
                sPE.inc(te.drain(), 1)
                wpe(sc)
                sc.activation(qstage[:, mt * 128:(mt + 1) * 128],
                              wsA["psH"][par][:, 0:128], AF.Identity)
                sAC.inc(sc.drain(), 1)
            wac(sy)
            sST.inc(sy.dma_start(out=qd[bat], in_=qstage[:]), 16)
            wst(sy)
            if STOP_AFTER == "A5" and bat == 0:
                break
        wst(sy)
        barrier()
        if KDBG and STOP_AFTER == "A2":
            for t in range(4):
                sST.inc(sy.dma_start(out=dbg_x1[t * 128:(t + 1) * 128, :],
                                     in_=x1b[:, t * D:(t + 1) * D]), 16)
            sST.inc(sy.dma_start(out=dbg_k, in_=wsA["h_fm"][:, 0:512]), 16)
            sST.inc(sy.dma_start(out=dbg_q, in_=wsA["h_fm"][:, 512:1536]), 16)
            wst(sy)
        if KDBG and STOP_AFTER == "A1":
            for t in range(4):
                sST.inc(sy.dma_start(out=dbg_x1[t * 128:(t + 1) * 128, :],
                                     in_=wsA["xb"][:, t * D:(t + 1) * D]), 16)
            ve.tensor_copy(x1b[:, 0:D], wsA["xn_tm"][:, 0:D])
            ve.tensor_copy(x1b[:, D:2 * D], wsA["xn_fm"][:, 0:D])
            sDV.inc(ve.drain(), 1)
            wdv(sy)
            sST.inc(sy.dma_start(out=dbg_pp[:, 0:1024], in_=x1b[:, 0:D]), 16)
            sST.inc(sy.dma_start(out=dbg_pp[:, 1024:2048], in_=x1b[:, D:2 * D]), 16)
            sST.inc(sy.dma_start(out=dbg_pp[:, 2048:2052], in_=wsA["st_s"][:]), 16)
            sST.inc(sy.dma_start(out=dbg_pp[:, 2052:2056], in_=wsA["st_q"][:]), 16)
            sST.inc(sy.dma_start(out=dbg_pp[:, 2056:2060], in_=wsA["rstd4"][:]), 16)
            sST.inc(sy.dma_start(out=dbg_pp[:, 2060:2064], in_=wsA["nmr4"][:]), 16)
            wst(sy)
        if KDBG and STOP_AFTER not in ("A1", "A2"):
            for t in range(4):
                sLD.inc(sy.dma_start(out=wsA["xb"][:, t * D:(t + 1) * D],
                                     in_=x1d[t * 128:(t + 1) * 128, :]), 16)
            wld(sy)
            for t in range(4):
                sST.inc(sy.dma_start(out=dbg_x1[t * 128:(t + 1) * 128, :],
                                     in_=wsA["xb"][:, t * D:(t + 1) * D]), 16)
            sLD.inc(sy.dma_start(out=kstage[:, 0, :], in_=kd[0:128, 0:512]), 16)
            sLD.inc(sy.dma_start(out=qstage[:], in_=qd[0]), 16)
            sLD.inc(sy.dma_start(out=vstage[:, 0, 0:520], in_=vd[0][:, 0:520]), 16)
            sLD.inc(sy.dma_start(out=vstage[:, 1, 0:520], in_=vd[0][:, 520:1040]), 16)
            wld(sy)
            sST.inc(sy.dma_start(out=dbg_k, in_=kstage[:, 0, :]), 16)
            sST.inc(sy.dma_start(out=dbg_q, in_=qstage[:]), 16)
            sST.inc(sy.dma_start(out=dbg_v[:, 0:520], in_=vstage[:, 0, 0:520]), 16)
            sST.inc(sy.dma_start(out=dbg_v[:, 520:1040], in_=vstage[:, 1, 0:520]), 16)
            wst(sy)
            barrier()
        actx.__exit__(None, None, None)
        if STOP_AFTER in ("A", "A1", "A2", "A3", "A4", "A5"):
            return

        bctx = ExitStack()
        bctx.__enter__()
        o_fm = mksb(bctx, "o_fm", [128, 8 * 2048], BF16)
        battn = ExitStack()
        battn.__enter__()
        ksl = mksb(battn, "ksl", [128, 2, 8 * 512], BF16)
        vsl = mksb(battn, "vsl", [128, 2, 4 * 1040], BF16)
        qsl = mksb(battn, "qsl", [128, 2, 1024], BF16)
        qdup = mksb(battn, "qdup", [128, 2, 512], BF16)
        expS = mksb(battn, "expS", [128, 2, 1024], BF16)
        recq = mksb(battn, "recq", [1, 512], F32)
        rech = mksb(battn, "rech", [1, 512], BF16)
        bcst = mksb(battn, "bcst", [64, 512], F32)
        psS = mkps(battn, "psS", [128, 2048], F32)
        dbgs = mksb(battn, "dbgsB", [128, 1024], F32) if KDBG else None
        psO = [mkps(battn, "psOa", [128, 512], F32), mkps(battn, "psOb", [128, 512], F32)]
        psB = mkps(battn, "psB", [64, 512], F32)

        for tb_ in (ksl, vsl, qsl, qdup, expS):
            gp.memset(tb_[:], 0.0)
        gp.memset(bcst[:], 0.0)
        sGP.inc(gp.drain(), 1)
        barrier()

        gp_head = []

        def load_head(hd, par):
            sGP.inc(sy.dma_start(out=qsl[:, par, :], in_=qd[hd]), 16)
            for mt in range(8):
                sGP.inc(sy.dma_start(
                    out=ksl[:, par, mt * 512:(mt + 1) * 512],
                    in_=kd[mt * 128:(mt + 1) * 128, hd * TB:(hd + 1) * TB]), 16)
            for tt in range(4):
                sGP.inc(sy.dma_start(
                    out=vsl[:, par, tt * 1040:(tt + 1) * 1040],
                    in_=vd[hd * 4 + tt]), 16)
            gp_head.append(sGP.v)

        rec_pe_head_end = [0, 0]
        rec_dv_norm = [0, 0]
        load_head(0, 0)
        for hd in range(NH):
            par = hd % 2
            if hd + 1 < NH:
                sy.wait_ge(sPE.h, rec_pe_head_end[(hd + 1) % 2])
                load_head(hd + 1, (hd + 1) % 2)
            te.wait_ge(sGP.h, gp_head[hd])
            ve.wait_ge(sGP.h, gp_head[hd])
            for G in range(4):
                gpar = G % 2
                wpe(ve)
                for gl in range(4):
                    g = G * 4 + gl
                    srcq = qsl[(g % 2) * 64:(g % 2) * 64 + 64, par,
                               (g // 2) * 128:(g // 2 + 1) * 128]
                    ve.tensor_copy(qdup[0:64, gpar, gl * 128:(gl + 1) * 128], srcq)
                    ve.tensor_copy(qdup[64:128, gpar, gl * 128:(gl + 1) * 128], srcq)
                sDV.inc(ve.drain(), 1)
                wdv(te)
                rec_ac_exp = {}
                rec_pe_pv = {}
                for pj in range(32):
                    mt, ttp = pj // 4, pj % 4
                    p2 = pj % 2
                    if pj >= 2:
                        te.wait_ge(sAC.h, rec_ac_exp[pj - 2])
                    if pj == 0:
                        te.wait_ge(sDV.h, rec_dv_norm[gpar])
                    kslice = ksl[:, par, mt * 512 + ttp * 128: mt * 512 + (ttp + 1) * 128]
                    te.matmul(psS[:, p2 * 1024: p2 * 1024 + 512],
                              kslice[0:64, :], qdup[0:64, gpar, :],
                              start=True, stop=True, tile_position=(0, 0),
                              skip_group_check=True)
                    te.matmul(psS[:, p2 * 1024 + 512: p2 * 1024 + 1024],
                              kslice[64:128, :], qdup[64:128, gpar, :],
                              start=True, stop=True, tile_position=(64, 0),
                              skip_group_check=True)
                    pe_qk = sPE.inc(te.drain(), 1)
                    sc.wait_ge(sPE.h, pe_qk if pj < 1 else max(pe_qk, rec_pe_pv.get(pj - 2, 0)))
                    sc.activation(expS[:, p2, :], psS[:, p2 * 1024:(p2 + 1) * 1024],
                                  AF.Exp, bias=np.float32(-EXPC), scale=np.float32(0.125))
                    rec_ac_exp[pj] = sAC.inc(sc.drain(), 1)
                    te.wait_ge(sAC.h, rec_ac_exp[pj])
                    first = (pj == 0)
                    te.matmul(psO[gpar][0:65, :],
                              vsl[:, par, ttp * 1040 + (2 * mt) * 65:
                                        ttp * 1040 + (2 * mt) * 65 + 65],
                              expS[:, p2, 0:512],
                              start=first, stop=False, skip_group_check=True)
                    te.matmul(psO[gpar][0:65, :],
                              vsl[:, par, ttp * 1040 + (2 * mt + 1) * 65:
                                        ttp * 1040 + (2 * mt + 1) * 65 + 65],
                              expS[:, p2, 512:1024],
                              start=False, stop=(pj == 31), skip_group_check=True)
                    rec_pe_pv[pj] = sPE.inc(te.drain(), 1)
                wpe(ve)
                ve.reciprocal(recq[:], psO[gpar][64:65, :])
                sDV.inc(ve.drain(), 1)
                wdv(ve)
                ve.tensor_copy(rech[:], recq[:])
                sDV.inc(ve.drain(), 1)
                wdv(te)
                te.matmul(psB[0:64, :], onesb[0:1, 0:64], rech[0:1, :],
                          start=True, stop=True, skip_group_check=True)
                sPE.inc(te.drain(), 1)
                wpe(sc)
                sc.wait_ge(sDV.h, rec_dv_norm[1 - gpar])
                sc.activation(bcst[0:64, :], psB[0:64, :], AF.Identity)
                sAC.inc(sc.drain(), 1)
                wac(ve)
                mt_o, basep = hd // 2, (hd % 2) * 64
                ve.tensor_tensor(
                    o_fm[basep:basep + 64,
                         mt_o * 2048 + G * 512: mt_o * 2048 + (G + 1) * 512],
                    psO[gpar][0:64, :], bcst[0:64, :], ALU.mult)
                if KDBG and hd == 0 and G == 0:
                    ve.tensor_copy(dbgs[0:65, 0:512], psO[gpar][0:65, :])
                    ve.tensor_copy(dbgs[0:64, 512:1024],
                                   o_fm[0:64, 0:512])
                rec_dv_norm[gpar] = sDV.inc(ve.drain(), 1)
            rec_pe_head_end[par] = sPE.v
        barrier()
        if KDBG:
            sST.inc(sy.dma_start(out=dbg_o, in_=o_fm[:, 0:2048]), 16)
            sST.inc(sy.dma_start(out=dbg_pp[0:65, 2560:3072], in_=dbgs[0:65, 0:512]), 16)
            sST.inc(sy.dma_start(out=dbg_pp[0:64, 3072:3584], in_=dbgs[0:64, 512:1024]), 16)
            sST.inc(sy.dma_start(out=dbg_bw1, in_=o_fm[0:128, 0:512]), 16)
            wst(sy)
        battn.__exit__(None, None, None)
        if STOP_AFTER == "B":
            bctx.__exit__(None, None, None)
            return

        cctx = ExitStack()
        cctx.__enter__()
        woTb = mksb(cctx, "woTb", [128, 8 * D], BF16)
        wsC = make_ws(cctx, "C")
        x1t = mksb(cctx, "x1tC", [128, D], F32)
        outb = mksb(cctx, "outbC", [128, 4 * D], F32)
        x2b = wsC["xb"]

        for j in range(8):
            wdv(sy)
            sLD.inc(sy.dma_start(out=x2b[:, 0:D], in_=woT[j * 128:(j + 1) * 128, :]), 16)
            wld(ve)
            ve.tensor_copy(woTb[:, j * D:(j + 1) * D], x2b[:, 0:D])
            sDV.inc(ve.drain(), 1)
        wdv(sy)

        for sbi in range(4):
            wac(ve)
            for gl in range(4):
                g = sbi * 4 + gl
                G, g_l = g // 4, g % 4
                wdv(sy)
                sLD.inc(sy.dma_start(
                    out=x1t[:],
                    in_=x1d.rearrange("(a b) d -> a b d", b=16)[ds(ccv128, 128), g, :]), 16)
                for jh in range(2):
                    par = jh % 2
                    wdv(te)
                    for mt in range(8):
                        te.matmul(wsC["psH"][par][:],
                                  o_fm[:, mt * 2048 + G * 512 + g_l * 128:
                                       mt * 2048 + G * 512 + (g_l + 1) * 128],
                                  woTb[:, mt * D + jh * 512: mt * D + (jh + 1) * 512],
                                  start=(mt == 0), stop=False, skip_group_check=True)
                    te.matmul(wsC["psH"][par][:], onesb[0:1, 0:128],
                              boh[0:1, jh * 512:(jh + 1) * 512],
                              start=False, stop=False, skip_group_check=True)
                    te.matmul(wsC["psH"][par][:], onesb[0:1, 0:128],
                              bol[0:1, jh * 512:(jh + 1) * 512],
                              start=False, stop=True, skip_group_check=True)
                    sPE.inc(te.drain(), 1)
                    wpe(ve)
                    wld(ve)
                    ve.tensor_add(x2b[:, gl * D + jh * 512: gl * D + (jh + 1) * 512],
                                  x1t[:, jh * 512:(jh + 1) * 512], wsC["psH"][par][:])
                    sDV.inc(ve.drain(), 1)
            wdv(sc)
            emit_ln(wsC, x2b, wsC["xn_tm"])
            emit_tp(wsC, wsC["xn_tm"], wsC["xn_fm"])
            wst(ve)
            emit_ffn(wsC, x2b, outb)
            wdv(sy)
            for gl in range(4):
                g = sbi * 4 + gl
                sST.inc(sy.dma_start(out=out[g * 128:(g + 1) * 128, :],
                                     in_=outb[:, gl * D:(gl + 1) * D]), 16)
        wst(sy)

        cctx.__exit__(None, None, None)
        bctx.__exit__(None, None, None)

    gctx.__exit__(None, None, None)
    return nc


_NC_CACHE = {}


def kernel(L0_attn_x, L0_attn_wq, L0_attn_bq, L0_attn_wk, L0_attn_bk, L0_attn_wv,
           L0_attn_bv, L0_attn_wo, L0_attn_bo, L0_attn_rot_cos, L0_attn_rot_sin,
           L0_ffn_w1, L0_ffn_b1, L0_ffn_w2, L0_ffn_b2):
    x = np.asarray(L0_attn_x, dtype=np.float32)
    if "nc" not in _NC_CACHE:
        _NC_CACHE["nc"] = build_nc()
    nc = _NC_CACHE["nc"]

    f32 = np.float32
    common = {
        "w1T": np.ascontiguousarray(np.asarray(L0_ffn_w1, f32).T),
        "w2T": np.ascontiguousarray(np.asarray(L0_ffn_w2, f32).T),
        "wqT": np.ascontiguousarray(np.asarray(L0_attn_wq, f32).T),
        "wkT": np.ascontiguousarray(np.asarray(L0_attn_wk, f32).T),
        "wvT": np.ascontiguousarray(np.asarray(L0_attn_wv, f32).T),
        "woT": np.ascontiguousarray(np.asarray(L0_attn_wo, f32).T),
        "b1": np.ascontiguousarray(np.asarray(L0_ffn_b1, f32)),
        "b2": np.ascontiguousarray(np.asarray(L0_ffn_b2, f32)),
        "bq": np.ascontiguousarray(np.asarray(L0_attn_bq, f32)),
        "bk": np.ascontiguousarray(np.asarray(L0_attn_bk, f32)),
        "bv": np.ascontiguousarray(np.asarray(L0_attn_bv, f32)),
        "bo": np.ascontiguousarray(np.asarray(L0_attn_bo, f32)),
        "rotc": np.ascontiguousarray(np.asarray(L0_attn_rot_cos, f32)[4095]),
        "rots": np.ascontiguousarray(np.asarray(L0_attn_rot_sin, f32)[4095]),
    }
    in_maps = []
    for c in range(8):
        m = dict(common)
        m["xin"] = np.ascontiguousarray(x[c // 4])
        in_maps.append(m)

    res = run_bass_kernel_spmd(nc, in_maps, core_ids=list(range(8)))

    outp = np.empty((2, TFULL, D), np.float32)
    for c in range(8):
        bc, cc = c // 4, c % 4
        r = np.asarray(res.results[c]["out"]).reshape(16, 128, D)
        outp[bc, cc * 2048:(cc + 1) * 2048] = r.transpose(1, 0, 2).reshape(2048, D)
    return outp


# version 04
# speedup vs baseline: 1.1096x; 1.1096x over previous
import os
import numpy as np
import ml_dtypes
import concourse.bass as bass
import concourse.mybir as mybir
from concourse.bass import ds
from concourse.bass_utils import run_bass_kernel_spmd
from contextlib import ExitStack

F32 = mybir.dt.float32
BF16 = mybir.dt.bfloat16
AF = mybir.ActivationFunctionType
ALU = mybir.AluOpType
AX = mybir.AxisListType.X

D = 1024
HF = 4096
NH = 16
TFULL = 8192
NB = 16
TB = 512
EPS = 1e-5
EXPC = 12.0
CONSTS = [EPS, -EXPC, 0.125, 1.0 / 1024.0, 1.0 / 4096.0, 0.5]


STOP_AFTER = os.environ.get("KSTOP", "")
KDBG = os.environ.get("KDBG", "") == "1"



def build_nc():
    nc = bass.Bass("TRN2", target_bir_lowering=False, debug=False)

    xin = nc.dram_tensor("xin", [TFULL, D], F32, kind="ExternalInput").ap()
    w1T = nc.dram_tensor("w1T", [D, HF], F32, kind="ExternalInput").ap()
    w2T = nc.dram_tensor("w2T", [HF, D], F32, kind="ExternalInput").ap()
    wqT = nc.dram_tensor("wqT", [D, D], F32, kind="ExternalInput").ap()
    wkT = nc.dram_tensor("wkT", [D, D], F32, kind="ExternalInput").ap()
    wvT = nc.dram_tensor("wvT", [D, D], F32, kind="ExternalInput").ap()
    woT = nc.dram_tensor("woT", [D, D], F32, kind="ExternalInput").ap()
    b1i = nc.dram_tensor("b1", [HF], F32, kind="ExternalInput").ap()
    b2i = nc.dram_tensor("b2", [D], F32, kind="ExternalInput").ap()
    bqi = nc.dram_tensor("bq", [D], F32, kind="ExternalInput").ap()
    bki = nc.dram_tensor("bk", [D], F32, kind="ExternalInput").ap()
    bvi = nc.dram_tensor("bv", [D], F32, kind="ExternalInput").ap()
    boi = nc.dram_tensor("bo", [D], F32, kind="ExternalInput").ap()
    rci = nc.dram_tensor("rotc", [32], F32, kind="ExternalInput").ap()
    rsi = nc.dram_tensor("rots", [32], F32, kind="ExternalInput").ap()
    out = nc.dram_tensor("out", [2048, D], F32, kind="ExternalOutput").ap()
    if KDBG:
        dbg_pp = nc.dram_tensor("dbg_pp", [128, 4096], F32, kind="ExternalOutput").ap()
        dbg_bw1 = nc.dram_tensor("dbg_bw1", [128, 512], BF16, kind="ExternalOutput").ap()
        dbg_bw2 = nc.dram_tensor("dbg_bw2", [128, 512], BF16, kind="ExternalOutput").ap()
        dbg_x1 = nc.dram_tensor("dbg_x1", [512, D], F32, kind="ExternalOutput").ap()
        dbg_k = nc.dram_tensor("dbg_k", [128, 512], BF16, kind="ExternalOutput").ap()
        dbg_q = nc.dram_tensor("dbg_q", [128, 1024], BF16, kind="ExternalOutput").ap()
        dbg_v = nc.dram_tensor("dbg_v", [128, 1040], BF16, kind="ExternalOutput").ap()
        dbg_o = nc.dram_tensor("dbg_o", [128, 2048], BF16, kind="ExternalOutput").ap()

    bw1d = nc.dram_tensor("bw1d", [D, HF], BF16, kind="Internal").ap()
    bw2d = nc.dram_tensor("bw2d", [HF, D], BF16, kind="Internal").ap()
    kd = nc.dram_tensor("kd", [D, TFULL], BF16, kind="Internal").ap()
    vd = nc.dram_tensor("vd", [64, 128, 16 * 65], BF16, kind="Internal").ap()
    qd = nc.dram_tensor("qd", [NH, 128, 8 * 128], BF16, kind="Internal").ap()
    x1d = nc.dram_tensor("x1d", [TFULL, D], F32, kind="Internal").ap()
    xnqd = nc.dram_tensor("xnqd", [4, 128, D], BF16, kind="Internal").ap()
    rowb = nc.dram_tensor("rowb", [1024], F32, kind="Internal").ap()
    al1d = nc.dram_tensor("al1d", [HF], F32, kind="Internal").ap()
    al2d = nc.dram_tensor("al2d", [D], F32, kind="Internal").ap()
    recb = nc.dram_tensor("recb", [512], F32, kind="Internal").ap()

    gctx = ExitStack()
    gctx.__enter__()

    def mksb(ctx, name, shape, dt):
        return ctx.enter_context(nc.sbuf_tensor(name, shape, dt))

    def mkps(ctx, name, shape, dt):
        return ctx.enter_context(nc.psum_tensor(name, shape, dt))

    class Ctr:
        def __init__(self, name):
            self.h = gctx.enter_context(nc.semaphore(name))
            self.v = 0

        def inc(self, inst, n=1):
            inst.then_inc(self.h, n)
            self.v += n
            return self.v

    sLD = Ctr("sLD")
    sSC = Ctr("sSC")
    sGP = Ctr("sGP")
    sPE = Ctr("sPE")
    sAC = Ctr("sAC")
    sDV = Ctr("sDV")
    sST = Ctr("sST")
    sBR = Ctr("sBR")

    identb = mksb(gctx, "identb", [128, 128], BF16)
    onesb = mksb(gctx, "onesb", [1, 512], BF16)
    ones64b = mksb(gctx, "ones64b", [128, 64], BF16)
    b1h = mksb(gctx, "b1h", [1, HF], BF16)
    b1l = mksb(gctx, "b1l", [1, HF], BF16)
    b2h = mksb(gctx, "b2h", [1, D], BF16)
    b2l = mksb(gctx, "b2l", [1, D], BF16)
    bvh = mksb(gctx, "bvh", [1, D], BF16)
    bvl = mksb(gctx, "bvl", [1, D], BF16)
    boh = mksb(gctx, "boh", [1, D], BF16)
    bol = mksb(gctx, "bol", [1, D], BF16)
    bqh = mksb(gctx, "bqh", [1, D], BF16)
    bql = mksb(gctx, "bql", [1, D], BF16)
    bkh = mksb(gctx, "bkh", [1, D], BF16)
    bkl = mksb(gctx, "bkl", [1, D], BF16)

    blk = gctx.enter_context(nc.Block())

    @blk.sync
    def _(sync):
        te, ve, sc, gp, sy = nc.tensor, nc.vector, nc.scalar, nc.gpsimd, nc.sync
        pid = sy.partition_id()
        ccv = sy.compute_val(pid % 4)
        ccv128 = sy.compute_val(pid % 4 * 128)

        def wpe(e):
            e.wait_ge(sPE.h, sPE.v)

        def wac(e):
            e.wait_ge(sAC.h, sAC.v)

        def wdv(e):
            e.wait_ge(sDV.h, sDV.v)

        def wld(e):
            e.wait_ge(sLD.h, sLD.v)

        def wsc(e):
            e.wait_ge(sSC.h, sSC.v)

        def wst(e):
            e.wait_ge(sST.h, sST.v)

        def wgp(e):
            e.wait_ge(sGP.h, sGP.v)

        def barrier():
            for e in (te, ve, sc, gp, sy):
                sBR.inc(e.drain(), 1)
            tgt = sBR.v
            for e in (te, ve, sc, gp, sy):
                e.wait_ge(sBR.h, tgt)

        actx = ExitStack()
        actx.__enter__()
        wqRT = mksb(actx, "wqRT", [128, 8 * D], BF16)
        wkRT = mksb(actx, "wkRT", [128, 8 * D], BF16)
        wvTb = mksb(actx, "wvTb", [128, 8 * D], BF16)

        pctx = ExitStack()
        pctx.__enter__()
        crow = mksb(pctx, "crow", [1, 128], F32)
        srow = mksb(pctx, "srow", [1, 128], F32)
        ssrow = mksb(pctx, "ssrow", [1, 128], F32)
        signp = mksb(pctx, "signp", [1, 128], F32)
        cbt = mksb(pctx, "cbt", [128, 128], F32)
        ssbt = mksb(pctx, "ssbt", [128, 128], F32)
        brow_f = mksb(pctx, "brow_f", [1, 6 * D + HF], F32)
        btmp = mksb(pctx, "btmp", [1, 2 * D], F32)
        wld_t = mksb(pctx, "wld_t", [128, D], F32)
        wtmp = mksb(pctx, "wtmp", [128, 256], F32)
        wsl = mksb(pctx, "wsl", [128, 8 * 512], F32)
        wab = mksb(pctx, "wab", [128, 512], BF16)
        alc = mksb(pctx, "alc", [128, 4], F32)
        abca = mksb(pctx, "abca", [128, 512], F32)
        wsg = mksb(pctx, "wsg", [128, 8 * 512], BF16)
        psP2 = mkps(pctx, "psP2", [128, 512], F32)

        gp.memset(onesb[:], 1.0)
        gp.memset(ones64b[:], 1.0)
        gp.memset(identb[:], 0.0)
        gp.affine_select(out=identb[:], in_=identb[:], compare_op=ALU.not_equal,
                         fill=1.0, base=0, pattern=[[-1, 128]], channel_multiplier=1)
        gp.memset(signp[:], 1.0)
        sGP.inc(gp.drain(), 1)
        wgp(ve)
        spv = signp[:].rearrange("p (i e) -> p i e", e=2)
        ve.tensor_single_scalar(spv[:, :, 0], spv[:, :, 0], -1.0, ALU.mult)
        sDV.inc(ve.drain(), 1)
        for e in (te, ve, sc, sy):
            wgp(e)

        OB1, OB2, OBV, OBO, OBQ, OBK = 0, HF, HF + D, HF + 2 * D, HF + 3 * D, HF + 4 * D
        for off, t in ((OB1, b1i), (OB2, b2i), (OBV, bvi), (OBO, boi),
                       (OBQ, bqi), (OBK, bki)):
            sLD.inc(sy.dma_start(out=brow_f[0:1, off:off + t.shape[0]],
                                 in_=t[None, :]), 16)
        rr = btmp
        sLD.inc(sy.dma_start(out=rr[0:1, 0:32], in_=rci[None, :]), 16)
        sLD.inc(sy.dma_start(out=rr[0:1, 32:64], in_=rsi[None, :]), 16)
        wld(ve)
        for b in range(2):
            for e in range(2):
                cv = crow[0:1, b * 64:(b + 1) * 64].rearrange("p (i e) -> p i e", e=2)
                sv = srow[0:1, b * 64:(b + 1) * 64].rearrange("p (i e) -> p i e", e=2)
                ve.tensor_copy(cv[:, :, e], rr[0:1, 0:32])
                ve.tensor_copy(sv[:, :, e], rr[0:1, 32:64])
        ve.tensor_mul(ssrow[:], srow[:], signp[:])
        sDV.inc(ve.drain(), 1)
        wdv(sy)
        sST.inc(sy.dma_start(out=rowb[None, 0:128], in_=crow[:]), 16)
        sST.inc(sy.dma_start(out=rowb[None, 128:256], in_=ssrow[:]), 16)
        wst(sy)
        sLD.inc(sy.dma_start(out=cbt[:], in_=bass.AP(rowb.tensor, 0, [[0, 128], [1, 128]]),
                             allow_slow_non_contiguous=True), 16)
        sLD.inc(sy.dma_start(out=ssbt[:], in_=bass.AP(rowb.tensor, 128, [[0, 128], [1, 128]]),
                             allow_slow_non_contiguous=True), 16)

        wld(ve)
        for off in (OBQ, OBK):
            for ch in range(8):
                s0 = brow_f[0:1, off + ch * 128: off + (ch + 1) * 128]
                t0 = btmp[0:1, ch * 128:(ch + 1) * 128]
                sv = s0.rearrange("p (i e) -> p i e", e=2)
                tv = t0.rearrange("p (i e) -> p i e", e=2)
                ssv = ssrow[0:1, :].rearrange("p (i e) -> p i e", e=2)
                ve.tensor_tensor(tv[:, :, 0], sv[:, :, 1], ssv[:, :, 0], ALU.mult)
                ve.tensor_tensor(tv[:, :, 1], sv[:, :, 0], ssv[:, :, 1], ALU.mult)
                ve.tensor_tensor(s0, s0, crow[0:1, :], ALU.mult)
                ve.tensor_tensor(s0, s0, t0, ALU.add)
        SPLITS = ((b1h, b1l, OB1, HF), (b2h, b2l, OB2, D),
                  (bvh, bvl, OBV, D), (boh, bol, OBO, D),
                  (bqh, bql, OBQ, D), (bkh, bkl, OBK, D))
        for hi, lo, off, ln in SPLITS:
            ve.tensor_copy(hi[:], brow_f[0:1, off:off + ln])
        sDV.inc(ve.drain(), 1)
        wdv(ve)
        for hi, lo, off, ln in SPLITS:
            ve.tensor_tensor(brow_f[0:1, off:off + ln],
                             brow_f[0:1, off:off + ln], hi[:], ALU.subtract)
        sDV.inc(ve.drain(), 1)
        wdv(ve)
        for hi, lo, off, ln in SPLITS:
            ve.tensor_copy(lo[:], brow_f[0:1, off:off + ln])
        sDV.inc(ve.drain(), 1)

        for wsrc, wdst, roped in ((wqT, wqRT, True), (wkT, wkRT, True),
                                  (wvT, wvTb, False)):
            for j in range(8):
                wdv(sy)
                sLD.inc(sy.dma_start(out=wld_t[:], in_=wsrc[j * 128:(j + 1) * 128, :]), 16)
                wld(ve)
                if roped:
                    for ch in range(8):
                        s0 = wld_t[:, ch * 128:(ch + 1) * 128]
                        sv = s0.rearrange("p (i e) -> p i e", e=2)
                        t1 = wtmp[:, 0:128]
                        t2 = wtmp[:, 128:256]
                        t2v = t2.rearrange("p (i e) -> p i e", e=2)
                        ssv = ssbt[:].rearrange("p (i e) -> p i e", e=2)
                        ve.tensor_tensor(t2v[:, :, 0], sv[:, :, 1], ssv[:, :, 0], ALU.mult)
                        ve.tensor_tensor(t2v[:, :, 1], sv[:, :, 0], ssv[:, :, 1], ALU.mult)
                        ve.tensor_tensor(t1, s0, cbt[:], ALU.mult)
                        ve.tensor_tensor(wdst[:, j * D + ch * 128: j * D + (ch + 1) * 128],
                                         t1, t2, ALU.add)
                else:
                    ve.tensor_copy(wdst[:, j * D:(j + 1) * D], wld_t[:])
                sDV.inc(ve.drain(), 1)

        if STOP_AFTER == "prep3":
            if KDBG:
                sST.inc(sy.dma_start(out=dbg_pp[0:1, 0:3072], in_=brow_f[0:1, HF:HF + 3 * D]), 16)
                sST.inc(sy.dma_start(out=dbg_pp[1:2, 0:128], in_=crow[:]), 16)
                sST.inc(sy.dma_start(out=dbg_pp[1:2, 128:256], in_=ssrow[:]), 16)
                sST.inc(sy.dma_start(out=dbg_pp[2:34, 0:128], in_=cbt[0:32, :]), 16)
                wdv(ve)

                ve.tensor_copy(brow_f[0:1, 0:1024], bass.AP(wqRT, 0, [[8192, 1], [1, 1024]]))
                sDV.inc(ve.drain(), 1)
                wdv(sy)
                sST.inc(sy.dma_start(out=dbg_pp[40:41, 0:1024], in_=brow_f[0:1, 0:1024]), 16)
                sy.wait_ge(sST.h, sST.v)
            pctx.__exit__(None, None, None)
            actx.__exit__(None, None, None)
            return
        for slab in range(8):
            wst(sy)
            wdv(sy)
            for j in range(8):
                sLD.inc(sy.dma_start(
                    out=wsl[:, j * 512:(j + 1) * 512],
                    in_=w1T[j * 128:(j + 1) * 128, slab * 512:(slab + 1) * 512]), 16)
            wld(sc)
            wdv(te)
            for hs in range(4):
                for j in range(8):
                    wpe(sc)
                    sc.activation(wab[:, 0:128], wsl[:, j * 512 + hs * 128: j * 512 + (hs + 1) * 128], AF.Abs)
                    sAC.inc(sc.drain(), 1)
                    wac(te)
                    te.matmul(psP2[:, hs * 64:(hs + 1) * 64], wab[:, 0:128], ones64b[:],
                              start=(hs == 0 and j == 0), stop=(hs == 3 and j == 7),
                              skip_group_check=True)
                    sPE.inc(te.drain(), 1)
            wpe(sc)
            sc.activation(alc[:], psP2[:, 0:256].rearrange("p (h s) -> p h s", h=4)[:, :, 0],
                          AF.Identity, scale=np.float32(1.0 / 1024.0))
            sAC.inc(sc.drain(), 1)
            wac(sy)
            sST.inc(sy.dma_start(
                out=bass.AP(al1d.tensor, slab * 512, [[1, 128], [128, 4]]),
                in_=alc[:], allow_slow_non_contiguous=True), 16)
            wst(sy)
            sLD.inc(sy.dma_start(
                out=abca[:], in_=bass.AP(al1d.tensor, slab * 512, [[0, 128], [1, 512]]),
                allow_slow_non_contiguous=True), 16)
            wst(sc)
            for j in range(8):
                sc.sign(wsg[:, j * 512:(j + 1) * 512], wsl[:, j * 512:(j + 1) * 512])
            sAC.inc(sc.drain(), 1)
            wac(ve)
            wld(ve)
            for j in range(8):
                ve.tensor_tensor(wsg[:, j * 512:(j + 1) * 512],
                                 wsg[:, j * 512:(j + 1) * 512], abca[:], ALU.mult)
            sDV.inc(ve.drain(), 1)
            wdv(sy)
            for j in range(8):
                sST.inc(sy.dma_start(
                    out=bw1d[j * 128:(j + 1) * 128, slab * 512:(slab + 1) * 512],
                    in_=wsg[:, j * 512:(j + 1) * 512]), 16)

        for slab in range(2):
            for half in range(4):
                wst(sy)
                wdv(sy)
                for j in range(8):
                    hc = half * 8 + j
                    sLD.inc(sy.dma_start(
                        out=wsl[:, j * 512:(j + 1) * 512],
                        in_=w2T[hc * 128:(hc + 1) * 128, slab * 512:(slab + 1) * 512]), 16)
                wld(sc)
                wdv(te)
                for ds_ in range(4):
                    for j in range(8):
                        wpe(sc)
                        sc.activation(wab[:, 0:128],
                                      wsl[:, j * 512 + ds_ * 128: j * 512 + (ds_ + 1) * 128],
                                      AF.Abs)
                        sAC.inc(sc.drain(), 1)
                        wac(te)
                        te.matmul(psP2[:, ds_ * 64:(ds_ + 1) * 64], wab[:, 0:128], ones64b[:],
                                  start=(half == 0 and ds_ == 0 and j == 0),
                                  stop=(half == 3 and ds_ == 3 and j == 7),
                                  skip_group_check=True)
                        sPE.inc(te.drain(), 1)
                wst(sc)
                for j in range(8):
                    sc.sign(wsg[:, j * 512:(j + 1) * 512], wsl[:, j * 512:(j + 1) * 512])
                sAC.inc(sc.drain(), 1)
                wac(sy)
                for j in range(8):
                    hc = half * 8 + j
                    sST.inc(sy.dma_start(
                        out=bw2d[hc * 128:(hc + 1) * 128, slab * 512:(slab + 1) * 512],
                        in_=wsg[:, j * 512:(j + 1) * 512]), 16)
            wpe(sc)
            sc.activation(alc[:], psP2[:, 0:256].rearrange("p (h s) -> p h s", h=4)[:, :, 0],
                          AF.Identity, scale=np.float32(1.0 / 4096.0))
            sAC.inc(sc.drain(), 1)
            wac(sy)
            sST.inc(sy.dma_start(
                out=bass.AP(al2d.tensor, slab * 512, [[1, 128], [128, 4]]),
                in_=alc[:], allow_slow_non_contiguous=True), 16)
            wst(sy)
            sLD.inc(sy.dma_start(
                out=abca[:], in_=bass.AP(al2d.tensor, slab * 512, [[0, 128], [1, 512]]),
                allow_slow_non_contiguous=True), 16)
            for half in range(4):
                wst(sy)
                wdv(sy)
                for j in range(8):
                    hc = half * 8 + j
                    sLD.inc(sy.dma_start(
                        out=wsg[:, j * 512:(j + 1) * 512],
                        in_=bw2d[hc * 128:(hc + 1) * 128, slab * 512:(slab + 1) * 512]), 16)
                wld(ve)
                for j in range(8):
                    ve.tensor_tensor(wsg[:, j * 512:(j + 1) * 512],
                                     wsg[:, j * 512:(j + 1) * 512], abca[:], ALU.mult)
                sDV.inc(ve.drain(), 1)
                wdv(sy)
                for j in range(8):
                    hc = half * 8 + j
                    sST.inc(sy.dma_start(
                        out=bw2d[hc * 128:(hc + 1) * 128, slab * 512:(slab + 1) * 512],
                        in_=wsg[:, j * 512:(j + 1) * 512]), 16)
        wst(sy)
        barrier()
        pctx.__exit__(None, None, None)
        if STOP_AFTER == "prep":
            if KDBG:
                wst(sy)
                sLD.inc(sy.dma_start(out=wab[:, 0:512], in_=bw1d[0:128, 0:512]), 16)
                wld(ve)
                ve.tensor_copy(wsl[:, 0:512], wab[:, 0:512])
                sDV.inc(ve.drain(), 1)
                wdv(sy)
                sST.inc(sy.dma_start(out=dbg_pp[0:128, 1024:1536], in_=wsl[:, 0:512]), 16)
                wst(sy)
                sLD.inc(sy.dma_start(out=wab[:, 0:512], in_=bw2d[384:512, 512:1024]), 16)
                wld(ve)
                ve.tensor_copy(wsl[:, 0:512], wab[:, 0:512])
                sDV.inc(ve.drain(), 1)
                wdv(sy)
                sST.inc(sy.dma_start(out=dbg_pp[0:128, 1536:2048], in_=wsl[:, 0:512]), 16)
                sy.wait_ge(sST.h, sST.v)
            pctx.__exit__(None, None, None)
            actx.__exit__(None, None, None)
            return

        def make_ws(ctx, tag):
            w = {}
            w["xb"] = mksb(ctx, f"xb{tag}", [128, 4 * D], F32)
            w["sq"] = mksb(ctx, f"sq{tag}", [128, D], F32)
            w["st_s"] = mksb(ctx, f"sts{tag}", [128, 4], F32)
            w["st_q"] = mksb(ctx, f"stq{tag}", [128, 4], F32)
            w["mean4"] = mksb(ctx, f"mean{tag}", [128, 4], F32)
            w["var4"] = mksb(ctx, f"var{tag}", [128, 4], F32)
            w["rstd4"] = mksb(ctx, f"rstd{tag}", [128, 4], F32)
            w["nmr4"] = mksb(ctx, f"nmr{tag}", [128, 4], F32)
            w["nt1"] = mksb(ctx, f"nt1{tag}", [128, 4], F32)
            w["nt2"] = mksb(ctx, f"nt2{tag}", [128, 4], F32)
            w["xn_tm"] = mksb(ctx, f"xntm{tag}", [128, 4 * D], BF16)
            w["xn_fm"] = mksb(ctx, f"xnfm{tag}", [128, 8 * 512], BF16)
            w["h_fm"] = mksb(ctx, f"hfm{tag}", [128, 32 * 512], BF16)
            w["w1s"] = mksb(ctx, f"w1s{tag}", [128, 2, 8 * 512], BF16)
            w["w2s"] = mksb(ctx, f"w2s{tag}", [128, 2, 512], BF16)
            w["psT"] = mkps(ctx, f"psT{tag}", [128, 1024], BF16)
            w["psH"] = [mkps(ctx, f"psH{tag}a", [128, 512], F32),
                        mkps(ctx, f"psH{tag}b", [128, 512], F32)]
            w["psY"] = [mkps(ctx, f"psY{tag}{i}", [128, 512], F32) for i in range(4)]
            return w

        def emit_ln(w, src, dst):
            for t in range(4):
                xt = src[:, t * D:(t + 1) * D]
                ve.reduce_sum(w["st_s"][:, t:t + 1], xt, axis=AX)
                wdv(sc)
                sc.square(w["sq"][:], xt)
                sAC.inc(sc.drain(), 1)
                wac(ve)
                ve.reduce_sum(w["st_q"][:, t:t + 1], w["sq"][:], axis=AX)
                sDV.inc(ve.drain(), 1)
            ve.tensor_scalar_mul(w["mean4"][:], w["st_s"][:], 1.0 / D)
            sDV.inc(ve.drain(), 1)
            wdv(sc)
            sc.square(w["nt1"][:], w["mean4"][:])
            sAC.inc(sc.drain(), 1)
            wac(ve)
            ve.scalar_tensor_tensor(w["var4"][:], w["st_q"][:], 1.0 / D, w["nt1"][:],
                                    ALU.mult, ALU.subtract)
            sDV.inc(ve.drain(), 1)
            wdv(sc)
            sc.activation(w["rstd4"][:], w["var4"][:], AF.Sqrt, bias=np.float32(EPS))
            sAC.inc(sc.drain(), 1)
            wac(ve)
            ve.reciprocal(w["nt2"][:], w["rstd4"][:])
            sDV.inc(ve.drain(), 1)
            wdv(ve)
            ve.scalar_tensor_tensor(w["nmr4"][:], w["mean4"][:], -1.0, w["nt2"][:],
                                    ALU.mult, ALU.mult)
            sDV.inc(ve.drain(), 1)
            wdv(ve)
            wpe(ve)
            for t in range(4):
                ve.tensor_scalar(dst[:, t * D:(t + 1) * D], src[:, t * D:(t + 1) * D],
                                 w["nt2"][:, t:t + 1], w["nmr4"][:, t:t + 1],
                                 ALU.mult, ALU.add)
            sDV.inc(ve.drain(), 1)

        def emit_tp(w, src, dst, ntile=4):
            for t in range(ntile):
                wdv(te)
                for j in range(8):
                    te.transpose(w["psT"][:, j * 128:(j + 1) * 128],
                                 src[:, t * D + j * 128: t * D + (j + 1) * 128],
                                 identb[:])
                sPE.inc(te.drain(), 1)
                wpe(ve)
                dv = dst[:].rearrange("p (j n) -> p j n", j=8)
                sv = w["psT"][:].rearrange("p (j n) -> p j n", j=8)
                ve.tensor_copy(dv[:, :, t * 128:(t + 1) * 128], sv[:, :, :])
                sDV.inc(ve.drain(), 1)

        def emit_ffn(w, xres, yout):
            rec_pe_w1 = [0, 0]
            rec_ac_h = [0, 0]
            rec_sc_w1 = {}

            def load_w1(hg_):
                par_ = hg_ % 2
                sc.wait_ge(sPE.h, rec_pe_w1[par_])
                for j in range(8):
                    sSC.inc(sc.dma_start(
                        out=w["w1s"][:, par_, j * 512:(j + 1) * 512],
                        in_=bw1d[j * 128:(j + 1) * 128, hg_ * 512:(hg_ + 1) * 512]), 16)
                rec_sc_w1[hg_] = sSC.v

            wdv(te)
            load_w1(0)
            for hg in range(8):
                par = hg % 2
                if hg + 1 < 8:
                    load_w1(hg + 1)
                te.wait_ge(sSC.h, rec_sc_w1[hg])
                for ht in range(4):
                    hpar = ht % 2
                    hidx = hg * 4 + ht
                    te.wait_ge(sAC.h, rec_ac_h[hpar])
                    for j in range(8):
                        te.matmul(w["psH"][hpar][:],
                                  w["w1s"][:, par, j * 512 + ht * 128: j * 512 + (ht + 1) * 128],
                                  w["xn_fm"][:, j * 512:(j + 1) * 512],
                                  start=(j == 0), stop=False, skip_group_check=True)
                    te.matmul(w["psH"][hpar][:], b1h[0:1, hidx * 128:(hidx + 1) * 128],
                              onesb[0:1, :], start=False, stop=False, skip_group_check=True)
                    te.matmul(w["psH"][hpar][:], b1l[0:1, hidx * 128:(hidx + 1) * 128],
                              onesb[0:1, :], start=False, stop=True, skip_group_check=True)
                    sPE.inc(te.drain(), 1)
                    wpe(sc)
                    sc.activation(w["h_fm"][:, hidx * 512:(hidx + 1) * 512],
                                  w["psH"][hpar][:], AF.Gelu)
                    rec_ac_h[hpar] = sAC.inc(sc.drain(), 1)
                rec_pe_w1[par] = sPE.v
            wac(te)
            rec_pe_w2 = [0, 0]
            rec_sc_w2 = {}

            def load_w2(idx_):
                dh_, hc_ = idx_ // 32, idx_ % 32
                par_ = idx_ % 2
                sc.wait_ge(sPE.h, rec_pe_w2[par_])
                sSC.inc(sc.dma_start(
                    out=w["w2s"][:, par_, :],
                    in_=bw2d[hc_ * 128:(hc_ + 1) * 128, dh_ * 512:(dh_ + 1) * 512]), 16)
                rec_sc_w2[idx_] = sSC.v

            load_w2(0)
            for dh in range(2):
                for hc in range(32):
                    idx = dh * 32 + hc
                    par = idx % 2
                    if idx + 1 < 64:
                        load_w2(idx + 1)
                    te.wait_ge(sSC.h, rec_sc_w2[idx])
                    for tt in range(4):
                        te.matmul(w["psY"][tt][:],
                                  w["h_fm"][:, hc * 512 + tt * 128: hc * 512 + (tt + 1) * 128],
                                  w["w2s"][:, par, :],
                                  start=(hc == 0), stop=False, skip_group_check=True)
                    rec_pe_w2[par] = sPE.inc(te.drain(), 1)
                for tt in range(4):
                    te.matmul(w["psY"][tt][:], onesb[0:1, 0:128],
                              b2h[0:1, dh * 512:(dh + 1) * 512],
                              start=False, stop=False, skip_group_check=True)
                    te.matmul(w["psY"][tt][:], onesb[0:1, 0:128],
                              b2l[0:1, dh * 512:(dh + 1) * 512],
                              start=False, stop=True, skip_group_check=True)
                sPE.inc(te.drain(), 1)
                wpe(ve)
                for tt in range(4):
                    ve.scalar_tensor_tensor(
                        yout[:, tt * D + dh * 512: tt * D + (dh + 1) * 512],
                        w["psY"][tt][:], 0.5,
                        xres[:, tt * D + dh * 512: tt * D + (dh + 1) * 512],
                        ALU.mult, ALU.add)
                sDV.inc(ve.drain(), 1)
                wdv(te)

        wsA = make_ws(actx, "A")
        x1b = mksb(actx, "x1bA", [128, 4 * D], F32)
        kstage = mksb(actx, "kstA", [128, 2, 512], BF16)
        vstage = mksb(actx, "vstA", [128, 2, 520], BF16)
        qin = mksb(actx, "qinA", [128, D], BF16)
        qin_fm = mksb(actx, "qinfA", [128, 8 * 128], BF16)
        qstage = mksb(actx, "qstA", [128, 8 * 128], BF16)

        for bat in range(NB):
            wdv(sy)
            wac(sy)
            for t in range(4):
                sLD.inc(sy.dma_start(
                    out=wsA["xb"][:, t * D:(t + 1) * D],
                    in_=xin[bat * TB + t * 128: bat * TB + (t + 1) * 128, :]), 16)
            wld(ve)
            wld(sc)
            emit_ln(wsA, wsA["xb"], wsA["xn_tm"])
            emit_tp(wsA, wsA["xn_tm"], wsA["xn_fm"])
            if STOP_AFTER == "A1" and bat == 0:
                break
            emit_ffn(wsA, wsA["xb"], x1b)
            if STOP_AFTER == "A2" and bat == 0:
                break
            wdv(sy)
            for t in range(4):
                sST.inc(sy.dma_start(
                    out=x1d[bat * TB + t * 128: bat * TB + (t + 1) * 128, :],
                    in_=x1b[:, t * D:(t + 1) * D]), 16)
            emit_ln(wsA, x1b, wsA["xn_tm"])
            wdv(sy)
            for t in range(4):
                sST.inc(sy.dma_start(out=xnqd[t],
                                     in_=wsA["xn_tm"][:, t * D:(t + 1) * D]), 16)
            emit_tp(wsA, wsA["xn_tm"], wsA["xn_fm"])
            wdv(te)
            for mt in range(8):
                par = mt % 2
                wac(te)
                for j in range(8):
                    te.matmul(wsA["psH"][par][:],
                              wkRT[:, j * D + mt * 128: j * D + (mt + 1) * 128],
                              wsA["xn_fm"][:, j * 512:(j + 1) * 512],
                              start=(j == 0), stop=False, skip_group_check=True)
                te.matmul(wsA["psH"][par][:], bkh[0:1, mt * 128:(mt + 1) * 128],
                          onesb[0:1, :], start=False, stop=False, skip_group_check=True)
                te.matmul(wsA["psH"][par][:], bkl[0:1, mt * 128:(mt + 1) * 128],
                          onesb[0:1, :], start=False, stop=True, skip_group_check=True)
                sPE.inc(te.drain(), 1)
                wpe(sc)
                wst(sc)
                sc.activation(kstage[:, par, :], wsA["psH"][par][:], AF.Identity)
                sAC.inc(sc.drain(), 1)
                wac(sy)
                sST.inc(sy.dma_start(
                    out=kd[mt * 128:(mt + 1) * 128, bat * TB:(bat + 1) * TB],
                    in_=kstage[:, par, :]), 16)
            if STOP_AFTER == "A3" and bat == 0:
                break
            wac(te)
            for tt in range(4):
                for mh in range(2):
                    par = (tt * 2 + mh) % 2
                    wdv(te)
                    for j in range(8):
                        te.matmul(wsA["psH"][par][:],
                                  wsA["xn_fm"][:, j * 512 + tt * 128: j * 512 + (tt + 1) * 128],
                                  wvTb[:, j * D + mh * 512: j * D + (mh + 1) * 512],
                                  start=(j == 0), stop=False, skip_group_check=True)
                    te.matmul(wsA["psH"][par][:], onesb[0:1, 0:128],
                              bvh[0:1, mh * 512:(mh + 1) * 512],
                              start=False, stop=False, skip_group_check=True)
                    te.matmul(wsA["psH"][par][:], onesb[0:1, 0:128],
                              bvl[0:1, mh * 512:(mh + 1) * 512],
                              start=False, stop=True, skip_group_check=True)
                    sPE.inc(te.drain(), 1)
                    wst(gp)
                    gp.memset(vstage[:, par, :], 1.0)
                    sGP.inc(gp.drain(), 1)
                    wgp(ve)
                    wpe(ve)
                    for g in range(8):
                        ve.tensor_copy(vstage[:, par, g * 65:g * 65 + 64],
                                       wsA["psH"][par][:, g * 64:(g + 1) * 64])
                    sDV.inc(ve.drain(), 1)
                    wdv(sy)
                    sST.inc(sy.dma_start(
                        out=vd[bat * 4 + tt, :, mh * 520:(mh + 1) * 520],
                        in_=vstage[:, par, :]), 16)
            if STOP_AFTER == "A4" and bat == 0:
                break
            wst(sy)
            sLD.inc(sy.dma_start(
                out=qin[:],
                in_=xnqd[ds(ccv, 1), :, :].rearrange("o p n -> (o p) n")), 16)
            wld(te)
            wdv(te)
            for j in range(8):
                te.transpose(wsA["psT"][:, j * 128:(j + 1) * 128],
                             qin[:, j * 128:(j + 1) * 128], identb[:])
            sPE.inc(te.drain(), 1)
            wpe(ve)
            ve.tensor_copy(qin_fm[:], wsA["psT"][:])
            sDV.inc(ve.drain(), 1)
            wdv(te)
            wst(sc)
            for mt in range(8):
                par = mt % 2
                wac(te)
                for j in range(8):
                    te.matmul(wsA["psH"][par][:, 0:128],
                              wqRT[:, j * D + mt * 128: j * D + (mt + 1) * 128],
                              qin_fm[:, j * 128:(j + 1) * 128],
                              start=(j == 0), stop=False, skip_group_check=True)
                te.matmul(wsA["psH"][par][:, 0:128], bqh[0:1, mt * 128:(mt + 1) * 128],
                          onesb[0:1, 0:128], start=False, stop=False, skip_group_check=True)
                te.matmul(wsA["psH"][par][:, 0:128], bql[0:1, mt * 128:(mt + 1) * 128],
                          onesb[0:1, 0:128], start=False, stop=True, skip_group_check=True)
                sPE.inc(te.drain(), 1)
                wpe(sc)
                sc.activation(qstage[:, mt * 128:(mt + 1) * 128],
                              wsA["psH"][par][:, 0:128], AF.Identity)
                sAC.inc(sc.drain(), 1)
            wac(sy)
            sST.inc(sy.dma_start(out=qd[bat], in_=qstage[:]), 16)
            wst(sy)
            if STOP_AFTER == "A5" and bat == 0:
                break
        wst(sy)
        barrier()
        if KDBG and STOP_AFTER == "A2":
            for t in range(4):
                sST.inc(sy.dma_start(out=dbg_x1[t * 128:(t + 1) * 128, :],
                                     in_=x1b[:, t * D:(t + 1) * D]), 16)
            sST.inc(sy.dma_start(out=dbg_k, in_=wsA["h_fm"][:, 0:512]), 16)
            sST.inc(sy.dma_start(out=dbg_q, in_=wsA["h_fm"][:, 512:1536]), 16)
            wst(sy)
        if KDBG and STOP_AFTER == "A1":
            for t in range(4):
                sST.inc(sy.dma_start(out=dbg_x1[t * 128:(t + 1) * 128, :],
                                     in_=wsA["xb"][:, t * D:(t + 1) * D]), 16)
            ve.tensor_copy(x1b[:, 0:D], wsA["xn_tm"][:, 0:D])
            ve.tensor_copy(x1b[:, D:2 * D], wsA["xn_fm"][:, 0:D])
            sDV.inc(ve.drain(), 1)
            wdv(sy)
            sST.inc(sy.dma_start(out=dbg_pp[:, 0:1024], in_=x1b[:, 0:D]), 16)
            sST.inc(sy.dma_start(out=dbg_pp[:, 1024:2048], in_=x1b[:, D:2 * D]), 16)
            sST.inc(sy.dma_start(out=dbg_pp[:, 2048:2052], in_=wsA["st_s"][:]), 16)
            sST.inc(sy.dma_start(out=dbg_pp[:, 2052:2056], in_=wsA["st_q"][:]), 16)
            sST.inc(sy.dma_start(out=dbg_pp[:, 2056:2060], in_=wsA["rstd4"][:]), 16)
            sST.inc(sy.dma_start(out=dbg_pp[:, 2060:2064], in_=wsA["nmr4"][:]), 16)
            wst(sy)
        if KDBG and STOP_AFTER not in ("A1", "A2"):
            for t in range(4):
                sLD.inc(sy.dma_start(out=wsA["xb"][:, t * D:(t + 1) * D],
                                     in_=x1d[t * 128:(t + 1) * 128, :]), 16)
            wld(sy)
            for t in range(4):
                sST.inc(sy.dma_start(out=dbg_x1[t * 128:(t + 1) * 128, :],
                                     in_=wsA["xb"][:, t * D:(t + 1) * D]), 16)
            sLD.inc(sy.dma_start(out=kstage[:, 0, :], in_=kd[0:128, 0:512]), 16)
            sLD.inc(sy.dma_start(out=qstage[:], in_=qd[0]), 16)
            sLD.inc(sy.dma_start(out=vstage[:, 0, 0:520], in_=vd[0][:, 0:520]), 16)
            sLD.inc(sy.dma_start(out=vstage[:, 1, 0:520], in_=vd[0][:, 520:1040]), 16)
            wld(sy)
            sST.inc(sy.dma_start(out=dbg_k, in_=kstage[:, 0, :]), 16)
            sST.inc(sy.dma_start(out=dbg_q, in_=qstage[:]), 16)
            sST.inc(sy.dma_start(out=dbg_v[:, 0:520], in_=vstage[:, 0, 0:520]), 16)
            sST.inc(sy.dma_start(out=dbg_v[:, 520:1040], in_=vstage[:, 1, 0:520]), 16)
            wst(sy)
            barrier()
        actx.__exit__(None, None, None)
        if STOP_AFTER in ("A", "A1", "A2", "A3", "A4", "A5"):
            return

        bctx = ExitStack()
        bctx.__enter__()
        o_fm = mksb(bctx, "o_fm", [128, 8 * 2048], BF16)
        battn = ExitStack()
        battn.__enter__()
        ksl = mksb(battn, "ksl", [128, 2, 8 * 512], BF16)
        vsl = mksb(battn, "vsl", [128, 2, 4 * 1040], BF16)
        qsl = mksb(battn, "qsl", [128, 2, 1024], BF16)
        qdup = mksb(battn, "qdup", [128, 2, 512], BF16)
        expS = mksb(battn, "expS", [128, 2, 1024], BF16)
        recq = mksb(battn, "recq", [1, 512], F32)
        rech = mksb(battn, "rech", [1, 512], BF16)
        bcst = mksb(battn, "bcst", [64, 512], F32)
        psS = mkps(battn, "psS", [128, 2048], F32)
        dbgs = mksb(battn, "dbgsB", [128, 1024], F32) if KDBG else None
        psO = [mkps(battn, "psOa", [128, 512], F32), mkps(battn, "psOb", [128, 512], F32)]
        psB = mkps(battn, "psB", [64, 512], F32)

        for tb_ in (ksl, vsl, qsl, qdup, expS):
            gp.memset(tb_[:], 0.0)
        gp.memset(bcst[:], 0.0)
        sGP.inc(gp.drain(), 1)
        barrier()

        gp_head = []

        def load_head(hd, par):
            sGP.inc(sy.dma_start(out=qsl[:, par, :], in_=qd[hd]), 16)
            for mt in range(8):
                sGP.inc(sy.dma_start(
                    out=ksl[:, par, mt * 512:(mt + 1) * 512],
                    in_=kd[mt * 128:(mt + 1) * 128, hd * TB:(hd + 1) * TB]), 16)
            for tt in range(4):
                sGP.inc(sy.dma_start(
                    out=vsl[:, par, tt * 1040:(tt + 1) * 1040],
                    in_=vd[hd * 4 + tt]), 16)
            gp_head.append(sGP.v)

        rec_pe_head_end = [0, 0]
        rec_dv_norm = [0, 0]
        load_head(0, 0)
        for hd in range(NH):
            par = hd % 2
            if hd + 1 < NH:
                sy.wait_ge(sPE.h, rec_pe_head_end[(hd + 1) % 2])
                load_head(hd + 1, (hd + 1) % 2)
            te.wait_ge(sGP.h, gp_head[hd])
            ve.wait_ge(sGP.h, gp_head[hd])
            for G in range(4):
                gpar = G % 2
                wpe(ve)
                for gl in range(4):
                    g = G * 4 + gl
                    srcq = qsl[(g % 2) * 64:(g % 2) * 64 + 64, par,
                               (g // 2) * 128:(g // 2 + 1) * 128]
                    ve.tensor_copy(qdup[0:64, gpar, gl * 128:(gl + 1) * 128], srcq)
                    ve.tensor_copy(qdup[64:128, gpar, gl * 128:(gl + 1) * 128], srcq)
                sDV.inc(ve.drain(), 1)
                wdv(te)
                rec_ac_exp = {}
                rec_pe_pv = {}
                for pj in range(32):
                    mt, ttp = pj // 4, pj % 4
                    p2 = pj % 2
                    if pj >= 2:
                        te.wait_ge(sAC.h, rec_ac_exp[pj - 2])
                    if pj == 0:
                        te.wait_ge(sDV.h, rec_dv_norm[gpar])
                    kslice = ksl[:, par, mt * 512 + ttp * 128: mt * 512 + (ttp + 1) * 128]
                    te.matmul(psS[:, p2 * 1024: p2 * 1024 + 512],
                              kslice[0:64, :], qdup[0:64, gpar, :],
                              start=True, stop=True, tile_position=(0, 0),
                              skip_group_check=True)
                    te.matmul(psS[:, p2 * 1024 + 512: p2 * 1024 + 1024],
                              kslice[64:128, :], qdup[64:128, gpar, :],
                              start=True, stop=True, tile_position=(64, 0),
                              skip_group_check=True)
                    pe_qk = sPE.inc(te.drain(), 1)
                    sc.wait_ge(sPE.h, pe_qk)
                    sc.activation(expS[:, p2, :], psS[:, p2 * 1024:(p2 + 1) * 1024],
                                  AF.Exp, bias=np.float32(-EXPC), scale=np.float32(0.125))
                    rec_ac_exp[pj] = sAC.inc(sc.drain(), 1)
                    te.wait_ge(sAC.h, rec_ac_exp[pj])
                    first = (pj == 0)
                    te.matmul(psO[gpar][0:65, :],
                              vsl[:, par, ttp * 1040 + (2 * mt) * 65:
                                        ttp * 1040 + (2 * mt) * 65 + 65],
                              expS[:, p2, 0:512],
                              start=first, stop=False, skip_group_check=True)
                    te.matmul(psO[gpar][0:65, :],
                              vsl[:, par, ttp * 1040 + (2 * mt + 1) * 65:
                                        ttp * 1040 + (2 * mt + 1) * 65 + 65],
                              expS[:, p2, 512:1024],
                              start=False, stop=(pj == 31), skip_group_check=True)
                    if pj == 31:
                        sPE.inc(te.drain(), 1)
                wpe(ve)
                ve.reciprocal(recq[:], psO[gpar][64:65, :])
                sDV.inc(ve.drain(), 1)
                wdv(ve)
                ve.tensor_copy(rech[:], recq[:])
                sDV.inc(ve.drain(), 1)
                wdv(te)
                te.matmul(psB[0:64, :], onesb[0:1, 0:64], rech[0:1, :],
                          start=True, stop=True, skip_group_check=True)
                sPE.inc(te.drain(), 1)
                wpe(sc)
                sc.wait_ge(sDV.h, rec_dv_norm[1 - gpar])
                sc.activation(bcst[0:64, :], psB[0:64, :], AF.Identity)
                sAC.inc(sc.drain(), 1)
                wac(ve)
                mt_o, basep = hd // 2, (hd % 2) * 64
                ve.tensor_tensor(
                    o_fm[basep:basep + 64,
                         mt_o * 2048 + G * 512: mt_o * 2048 + (G + 1) * 512],
                    psO[gpar][0:64, :], bcst[0:64, :], ALU.mult)
                if KDBG and hd == 0 and G == 0:
                    ve.tensor_copy(dbgs[0:65, 0:512], psO[gpar][0:65, :])
                    ve.tensor_copy(dbgs[0:64, 512:1024],
                                   o_fm[0:64, 0:512])
                rec_dv_norm[gpar] = sDV.inc(ve.drain(), 1)
            rec_pe_head_end[par] = sPE.v
        barrier()
        if KDBG:
            sST.inc(sy.dma_start(out=dbg_o, in_=o_fm[:, 0:2048]), 16)
            sST.inc(sy.dma_start(out=dbg_pp[0:65, 2560:3072], in_=dbgs[0:65, 0:512]), 16)
            sST.inc(sy.dma_start(out=dbg_pp[0:64, 3072:3584], in_=dbgs[0:64, 512:1024]), 16)
            sST.inc(sy.dma_start(out=dbg_bw1, in_=o_fm[0:128, 0:512]), 16)
            wst(sy)
        battn.__exit__(None, None, None)
        if STOP_AFTER == "B":
            bctx.__exit__(None, None, None)
            return

        cctx = ExitStack()
        cctx.__enter__()
        woTb = mksb(cctx, "woTb", [128, 8 * D], BF16)
        wsC = make_ws(cctx, "C")
        x1t = mksb(cctx, "x1tC", [128, D], F32)
        outb = mksb(cctx, "outbC", [128, 4 * D], F32)
        x2b = wsC["xb"]

        for j in range(8):
            wdv(sy)
            sLD.inc(sy.dma_start(out=x2b[:, 0:D], in_=woT[j * 128:(j + 1) * 128, :]), 16)
            wld(ve)
            ve.tensor_copy(woTb[:, j * D:(j + 1) * D], x2b[:, 0:D])
            sDV.inc(ve.drain(), 1)
        wdv(sy)

        for sbi in range(4):
            wac(ve)
            for gl in range(4):
                g = sbi * 4 + gl
                G, g_l = g // 4, g % 4
                wdv(sy)
                sLD.inc(sy.dma_start(
                    out=x1t[:],
                    in_=x1d.rearrange("(a b) d -> a b d", b=16)[ds(ccv128, 128), g, :]), 16)
                for jh in range(2):
                    par = jh % 2
                    wdv(te)
                    for mt in range(8):
                        te.matmul(wsC["psH"][par][:],
                                  o_fm[:, mt * 2048 + G * 512 + g_l * 128:
                                       mt * 2048 + G * 512 + (g_l + 1) * 128],
                                  woTb[:, mt * D + jh * 512: mt * D + (jh + 1) * 512],
                                  start=(mt == 0), stop=False, skip_group_check=True)
                    te.matmul(wsC["psH"][par][:], onesb[0:1, 0:128],
                              boh[0:1, jh * 512:(jh + 1) * 512],
                              start=False, stop=False, skip_group_check=True)
                    te.matmul(wsC["psH"][par][:], onesb[0:1, 0:128],
                              bol[0:1, jh * 512:(jh + 1) * 512],
                              start=False, stop=True, skip_group_check=True)
                    sPE.inc(te.drain(), 1)
                    wpe(ve)
                    wld(ve)
                    ve.tensor_add(x2b[:, gl * D + jh * 512: gl * D + (jh + 1) * 512],
                                  x1t[:, jh * 512:(jh + 1) * 512], wsC["psH"][par][:])
                    sDV.inc(ve.drain(), 1)
            wdv(sc)
            emit_ln(wsC, x2b, wsC["xn_tm"])
            emit_tp(wsC, wsC["xn_tm"], wsC["xn_fm"])
            wst(ve)
            emit_ffn(wsC, x2b, outb)
            wdv(sy)
            for gl in range(4):
                g = sbi * 4 + gl
                sST.inc(sy.dma_start(out=out[g * 128:(g + 1) * 128, :],
                                     in_=outb[:, gl * D:(gl + 1) * D]), 16)
        wst(sy)

        cctx.__exit__(None, None, None)
        bctx.__exit__(None, None, None)

    gctx.__exit__(None, None, None)
    return nc


_NC_CACHE = {}


def kernel(L0_attn_x, L0_attn_wq, L0_attn_bq, L0_attn_wk, L0_attn_bk, L0_attn_wv,
           L0_attn_bv, L0_attn_wo, L0_attn_bo, L0_attn_rot_cos, L0_attn_rot_sin,
           L0_ffn_w1, L0_ffn_b1, L0_ffn_w2, L0_ffn_b2):
    x = np.asarray(L0_attn_x, dtype=np.float32)
    if "nc" not in _NC_CACHE:
        _NC_CACHE["nc"] = build_nc()
    nc = _NC_CACHE["nc"]

    f32 = np.float32
    common = {
        "w1T": np.ascontiguousarray(np.asarray(L0_ffn_w1, f32).T),
        "w2T": np.ascontiguousarray(np.asarray(L0_ffn_w2, f32).T),
        "wqT": np.ascontiguousarray(np.asarray(L0_attn_wq, f32).T),
        "wkT": np.ascontiguousarray(np.asarray(L0_attn_wk, f32).T),
        "wvT": np.ascontiguousarray(np.asarray(L0_attn_wv, f32).T),
        "woT": np.ascontiguousarray(np.asarray(L0_attn_wo, f32).T),
        "b1": np.ascontiguousarray(np.asarray(L0_ffn_b1, f32)),
        "b2": np.ascontiguousarray(np.asarray(L0_ffn_b2, f32)),
        "bq": np.ascontiguousarray(np.asarray(L0_attn_bq, f32)),
        "bk": np.ascontiguousarray(np.asarray(L0_attn_bk, f32)),
        "bv": np.ascontiguousarray(np.asarray(L0_attn_bv, f32)),
        "bo": np.ascontiguousarray(np.asarray(L0_attn_bo, f32)),
        "rotc": np.ascontiguousarray(np.asarray(L0_attn_rot_cos, f32)[4095]),
        "rots": np.ascontiguousarray(np.asarray(L0_attn_rot_sin, f32)[4095]),
    }
    in_maps = []
    for c in range(8):
        m = dict(common)
        m["xin"] = np.ascontiguousarray(x[c // 4])
        in_maps.append(m)

    res = run_bass_kernel_spmd(nc, in_maps, core_ids=list(range(8)))

    outp = np.empty((2, TFULL, D), np.float32)
    for c in range(8):
        bc, cc = c // 4, c % 4
        r = np.asarray(res.results[c]["out"]).reshape(16, 128, D)
        outp[bc, cc * 2048:(cc + 1) * 2048] = r.transpose(1, 0, 2).reshape(2048, D)
    return outp


# version 05
# speedup vs baseline: 1.1226x; 1.0117x over previous
import os
import numpy as np
import ml_dtypes
import concourse.bass as bass
import concourse.mybir as mybir
from concourse.bass import ds
from concourse.bass_utils import run_bass_kernel_spmd
from contextlib import ExitStack

F32 = mybir.dt.float32
BF16 = mybir.dt.bfloat16
AF = mybir.ActivationFunctionType
ALU = mybir.AluOpType
AX = mybir.AxisListType.X

D = 1024
HF = 4096
NH = 16
TFULL = 8192
NB = 16
TB = 512
EPS = 1e-5
EXPC = 12.0
CONSTS = [EPS, -EXPC, 0.125, 1.0 / 1024.0, 1.0 / 4096.0, 0.5]


STOP_AFTER = os.environ.get("KSTOP", "")
KDBG = os.environ.get("KDBG", "") == "1"



def build_nc():
    nc = bass.Bass("TRN2", target_bir_lowering=False, debug=False)

    xin = nc.dram_tensor("xin", [TFULL, D], F32, kind="ExternalInput").ap()
    w1T = nc.dram_tensor("w1T", [D, HF], F32, kind="ExternalInput").ap()
    w2T = nc.dram_tensor("w2T", [HF, D], F32, kind="ExternalInput").ap()
    wqT = nc.dram_tensor("wqT", [D, D], F32, kind="ExternalInput").ap()
    wkT = nc.dram_tensor("wkT", [D, D], F32, kind="ExternalInput").ap()
    wvT = nc.dram_tensor("wvT", [D, D], F32, kind="ExternalInput").ap()
    woT = nc.dram_tensor("woT", [D, D], F32, kind="ExternalInput").ap()
    b1i = nc.dram_tensor("b1", [HF], F32, kind="ExternalInput").ap()
    b2i = nc.dram_tensor("b2", [D], F32, kind="ExternalInput").ap()
    bqi = nc.dram_tensor("bq", [D], F32, kind="ExternalInput").ap()
    bki = nc.dram_tensor("bk", [D], F32, kind="ExternalInput").ap()
    bvi = nc.dram_tensor("bv", [D], F32, kind="ExternalInput").ap()
    boi = nc.dram_tensor("bo", [D], F32, kind="ExternalInput").ap()
    rci = nc.dram_tensor("rotc", [32], F32, kind="ExternalInput").ap()
    rsi = nc.dram_tensor("rots", [32], F32, kind="ExternalInput").ap()
    out = nc.dram_tensor("out", [2048, D], F32, kind="ExternalOutput").ap()
    if KDBG:
        dbg_pp = nc.dram_tensor("dbg_pp", [128, 4096], F32, kind="ExternalOutput").ap()
        dbg_bw1 = nc.dram_tensor("dbg_bw1", [128, 512], BF16, kind="ExternalOutput").ap()
        dbg_bw2 = nc.dram_tensor("dbg_bw2", [128, 512], BF16, kind="ExternalOutput").ap()
        dbg_x1 = nc.dram_tensor("dbg_x1", [512, D], F32, kind="ExternalOutput").ap()
        dbg_k = nc.dram_tensor("dbg_k", [128, 512], BF16, kind="ExternalOutput").ap()
        dbg_q = nc.dram_tensor("dbg_q", [128, 1024], BF16, kind="ExternalOutput").ap()
        dbg_v = nc.dram_tensor("dbg_v", [128, 1040], BF16, kind="ExternalOutput").ap()
        dbg_o = nc.dram_tensor("dbg_o", [128, 2048], BF16, kind="ExternalOutput").ap()

    bw1d = nc.dram_tensor("bw1d", [D, HF], BF16, kind="Internal").ap()
    bw2d = nc.dram_tensor("bw2d", [HF, D], BF16, kind="Internal").ap()
    kd = nc.dram_tensor("kd", [D, TFULL], BF16, kind="Internal").ap()
    vd = nc.dram_tensor("vd", [64, 128, 16 * 65], BF16, kind="Internal").ap()
    qd = nc.dram_tensor("qd", [NH, 128, 8 * 128], BF16, kind="Internal").ap()
    x1d = nc.dram_tensor("x1d", [TFULL, D], F32, kind="Internal").ap()
    xnqd = nc.dram_tensor("xnqd", [4, 128, D], BF16, kind="Internal").ap()
    rowb = nc.dram_tensor("rowb", [1024], F32, kind="Internal").ap()
    al1d = nc.dram_tensor("al1d", [HF], F32, kind="Internal").ap()
    al2d = nc.dram_tensor("al2d", [D], F32, kind="Internal").ap()
    recb = nc.dram_tensor("recb", [512], F32, kind="Internal").ap()

    gctx = ExitStack()
    gctx.__enter__()

    def mksb(ctx, name, shape, dt):
        return ctx.enter_context(nc.sbuf_tensor(name, shape, dt))

    def mkps(ctx, name, shape, dt):
        return ctx.enter_context(nc.psum_tensor(name, shape, dt))

    class Ctr:
        def __init__(self, name):
            self.h = gctx.enter_context(nc.semaphore(name))
            self.v = 0

        def inc(self, inst, n=1):
            inst.then_inc(self.h, n)
            self.v += n
            return self.v

    sLD = Ctr("sLD")
    sSC = Ctr("sSC")
    sGP = Ctr("sGP")
    sPE = Ctr("sPE")
    sAC = Ctr("sAC")
    sDV = Ctr("sDV")
    sST = Ctr("sST")
    sBR = Ctr("sBR")

    identb = mksb(gctx, "identb", [128, 128], BF16)
    onesb = mksb(gctx, "onesb", [1, 512], BF16)
    ones64b = mksb(gctx, "ones64b", [128, 64], BF16)
    b1h = mksb(gctx, "b1h", [1, HF], BF16)
    b1l = mksb(gctx, "b1l", [1, HF], BF16)
    b2h = mksb(gctx, "b2h", [1, D], BF16)
    b2l = mksb(gctx, "b2l", [1, D], BF16)
    bvh = mksb(gctx, "bvh", [1, D], BF16)
    bvl = mksb(gctx, "bvl", [1, D], BF16)
    boh = mksb(gctx, "boh", [1, D], BF16)
    bol = mksb(gctx, "bol", [1, D], BF16)
    bqh = mksb(gctx, "bqh", [1, D], BF16)
    bql = mksb(gctx, "bql", [1, D], BF16)
    bkh = mksb(gctx, "bkh", [1, D], BF16)
    bkl = mksb(gctx, "bkl", [1, D], BF16)

    blk = gctx.enter_context(nc.Block())

    @blk.sync
    def _(sync):
        te, ve, sc, gp, sy = nc.tensor, nc.vector, nc.scalar, nc.gpsimd, nc.sync
        pid = sy.partition_id()
        ccv = sy.compute_val(pid % 4)
        ccv128 = sy.compute_val(pid % 4 * 128)

        def wpe(e):
            e.wait_ge(sPE.h, sPE.v)

        def wac(e):
            e.wait_ge(sAC.h, sAC.v)

        def wdv(e):
            e.wait_ge(sDV.h, sDV.v)

        def wld(e):
            e.wait_ge(sLD.h, sLD.v)

        def wsc(e):
            e.wait_ge(sSC.h, sSC.v)

        def wst(e):
            e.wait_ge(sST.h, sST.v)

        def wgp(e):
            e.wait_ge(sGP.h, sGP.v)

        def barrier():
            for e in (te, ve, sc, gp, sy):
                sBR.inc(e.drain(), 1)
            tgt = sBR.v
            for e in (te, ve, sc, gp, sy):
                e.wait_ge(sBR.h, tgt)

        actx = ExitStack()
        actx.__enter__()
        wqRT = mksb(actx, "wqRT", [128, 8 * D], BF16)
        wkRT = mksb(actx, "wkRT", [128, 8 * D], BF16)
        wvTb = mksb(actx, "wvTb", [128, 8 * D], BF16)

        pctx = ExitStack()
        pctx.__enter__()
        crow = mksb(pctx, "crow", [1, 128], F32)
        srow = mksb(pctx, "srow", [1, 128], F32)
        ssrow = mksb(pctx, "ssrow", [1, 128], F32)
        signp = mksb(pctx, "signp", [1, 128], F32)
        cbt = mksb(pctx, "cbt", [128, 128], F32)
        ssbt = mksb(pctx, "ssbt", [128, 128], F32)
        brow_f = mksb(pctx, "brow_f", [1, 6 * D + HF], F32)
        btmp = mksb(pctx, "btmp", [1, 2 * D], F32)
        wld_t = mksb(pctx, "wld_t", [128, D], F32)
        wtmp = mksb(pctx, "wtmp", [128, 256], F32)
        wsl = mksb(pctx, "wsl", [128, 8 * 512], F32)
        wab = mksb(pctx, "wab", [128, 512], BF16)
        alc = mksb(pctx, "alc", [128, 4], F32)
        abca = mksb(pctx, "abca", [128, 512], F32)
        wsg = mksb(pctx, "wsg", [128, 8 * 512], BF16)
        psP2 = mkps(pctx, "psP2", [128, 512], F32)

        gp.memset(onesb[:], 1.0)
        gp.memset(ones64b[:], 1.0)
        gp.memset(identb[:], 0.0)
        gp.affine_select(out=identb[:], in_=identb[:], compare_op=ALU.not_equal,
                         fill=1.0, base=0, pattern=[[-1, 128]], channel_multiplier=1)
        gp.memset(signp[:], 1.0)
        sGP.inc(gp.drain(), 1)
        wgp(ve)
        spv = signp[:].rearrange("p (i e) -> p i e", e=2)
        ve.tensor_single_scalar(spv[:, :, 0], spv[:, :, 0], -1.0, ALU.mult)
        sDV.inc(ve.drain(), 1)
        for e in (te, ve, sc, sy):
            wgp(e)

        OB1, OB2, OBV, OBO, OBQ, OBK = 0, HF, HF + D, HF + 2 * D, HF + 3 * D, HF + 4 * D
        for off, t in ((OB1, b1i), (OB2, b2i), (OBV, bvi), (OBO, boi),
                       (OBQ, bqi), (OBK, bki)):
            sLD.inc(sy.dma_start(out=brow_f[0:1, off:off + t.shape[0]],
                                 in_=t[None, :]), 16)
        rr = btmp
        sLD.inc(sy.dma_start(out=rr[0:1, 0:32], in_=rci[None, :]), 16)
        sLD.inc(sy.dma_start(out=rr[0:1, 32:64], in_=rsi[None, :]), 16)
        wld(ve)
        for b in range(2):
            for e in range(2):
                cv = crow[0:1, b * 64:(b + 1) * 64].rearrange("p (i e) -> p i e", e=2)
                sv = srow[0:1, b * 64:(b + 1) * 64].rearrange("p (i e) -> p i e", e=2)
                ve.tensor_copy(cv[:, :, e], rr[0:1, 0:32])
                ve.tensor_copy(sv[:, :, e], rr[0:1, 32:64])
        ve.tensor_mul(ssrow[:], srow[:], signp[:])
        sDV.inc(ve.drain(), 1)
        wdv(sy)
        sST.inc(sy.dma_start(out=rowb[None, 0:128], in_=crow[:]), 16)
        sST.inc(sy.dma_start(out=rowb[None, 128:256], in_=ssrow[:]), 16)
        wst(sy)
        sLD.inc(sy.dma_start(out=cbt[:], in_=bass.AP(rowb.tensor, 0, [[0, 128], [1, 128]]),
                             allow_slow_non_contiguous=True), 16)
        sLD.inc(sy.dma_start(out=ssbt[:], in_=bass.AP(rowb.tensor, 128, [[0, 128], [1, 128]]),
                             allow_slow_non_contiguous=True), 16)

        wld(ve)
        for off in (OBQ, OBK):
            for ch in range(8):
                s0 = brow_f[0:1, off + ch * 128: off + (ch + 1) * 128]
                t0 = btmp[0:1, ch * 128:(ch + 1) * 128]
                sv = s0.rearrange("p (i e) -> p i e", e=2)
                tv = t0.rearrange("p (i e) -> p i e", e=2)
                ssv = ssrow[0:1, :].rearrange("p (i e) -> p i e", e=2)
                ve.tensor_tensor(tv[:, :, 0], sv[:, :, 1], ssv[:, :, 0], ALU.mult)
                ve.tensor_tensor(tv[:, :, 1], sv[:, :, 0], ssv[:, :, 1], ALU.mult)
                ve.tensor_tensor(s0, s0, crow[0:1, :], ALU.mult)
                ve.tensor_tensor(s0, s0, t0, ALU.add)
        SPLITS = ((b1h, b1l, OB1, HF), (b2h, b2l, OB2, D),
                  (bvh, bvl, OBV, D), (boh, bol, OBO, D),
                  (bqh, bql, OBQ, D), (bkh, bkl, OBK, D))
        for hi, lo, off, ln in SPLITS:
            ve.tensor_copy(hi[:], brow_f[0:1, off:off + ln])
        sDV.inc(ve.drain(), 1)
        wdv(ve)
        for hi, lo, off, ln in SPLITS:
            ve.tensor_tensor(brow_f[0:1, off:off + ln],
                             brow_f[0:1, off:off + ln], hi[:], ALU.subtract)
        sDV.inc(ve.drain(), 1)
        wdv(ve)
        for hi, lo, off, ln in SPLITS:
            ve.tensor_copy(lo[:], brow_f[0:1, off:off + ln])
        sDV.inc(ve.drain(), 1)

        for wsrc, wdst, roped in ((wqT, wqRT, True), (wkT, wkRT, True),
                                  (wvT, wvTb, False)):
            for j in range(8):
                wdv(sy)
                sLD.inc(sy.dma_start(out=wld_t[:], in_=wsrc[j * 128:(j + 1) * 128, :]), 16)
                wld(ve)
                if roped:
                    for ch in range(8):
                        s0 = wld_t[:, ch * 128:(ch + 1) * 128]
                        sv = s0.rearrange("p (i e) -> p i e", e=2)
                        t1 = wtmp[:, 0:128]
                        t2 = wtmp[:, 128:256]
                        t2v = t2.rearrange("p (i e) -> p i e", e=2)
                        ssv = ssbt[:].rearrange("p (i e) -> p i e", e=2)
                        ve.tensor_tensor(t2v[:, :, 0], sv[:, :, 1], ssv[:, :, 0], ALU.mult)
                        ve.tensor_tensor(t2v[:, :, 1], sv[:, :, 0], ssv[:, :, 1], ALU.mult)
                        ve.tensor_tensor(t1, s0, cbt[:], ALU.mult)
                        ve.tensor_tensor(wdst[:, j * D + ch * 128: j * D + (ch + 1) * 128],
                                         t1, t2, ALU.add)
                else:
                    ve.tensor_copy(wdst[:, j * D:(j + 1) * D], wld_t[:])
                sDV.inc(ve.drain(), 1)

        if STOP_AFTER == "prep3":
            if KDBG:
                sST.inc(sy.dma_start(out=dbg_pp[0:1, 0:3072], in_=brow_f[0:1, HF:HF + 3 * D]), 16)
                sST.inc(sy.dma_start(out=dbg_pp[1:2, 0:128], in_=crow[:]), 16)
                sST.inc(sy.dma_start(out=dbg_pp[1:2, 128:256], in_=ssrow[:]), 16)
                sST.inc(sy.dma_start(out=dbg_pp[2:34, 0:128], in_=cbt[0:32, :]), 16)
                wdv(ve)

                ve.tensor_copy(brow_f[0:1, 0:1024], bass.AP(wqRT, 0, [[8192, 1], [1, 1024]]))
                sDV.inc(ve.drain(), 1)
                wdv(sy)
                sST.inc(sy.dma_start(out=dbg_pp[40:41, 0:1024], in_=brow_f[0:1, 0:1024]), 16)
                sy.wait_ge(sST.h, sST.v)
            pctx.__exit__(None, None, None)
            actx.__exit__(None, None, None)
            return
        for slab in range(8):
            wst(sy)
            wdv(sy)
            for j in range(8):
                sLD.inc(sy.dma_start(
                    out=wsl[:, j * 512:(j + 1) * 512],
                    in_=w1T[j * 128:(j + 1) * 128, slab * 512:(slab + 1) * 512]), 16)
            wld(sc)
            wdv(te)
            rec_pe_ab = [sPE.v, sPE.v]
            rec_ac_ab = {}
            for u in range(32):
                hs, j = u // 8, u % 8
                par = u % 2
                sc.wait_ge(sPE.h, rec_pe_ab[par])
                sc.activation(wab[:, par * 128:(par + 1) * 128],
                              wsl[:, j * 512 + hs * 128: j * 512 + (hs + 1) * 128], AF.Abs)
                rec_ac_ab[u] = sAC.inc(sc.drain(), 1)
                te.wait_ge(sAC.h, rec_ac_ab[u])
                te.matmul(psP2[:, hs * 64:(hs + 1) * 64], wab[:, par * 128:(par + 1) * 128],
                          ones64b[:], start=(u == 0), stop=(u == 31),
                          skip_group_check=True)
                rec_pe_ab[par] = sPE.inc(te.drain(), 1)
            wpe(sc)
            sc.activation(alc[:], psP2[:, 0:256].rearrange("p (h s) -> p h s", h=4)[:, :, 0],
                          AF.Identity, scale=np.float32(1.0 / 1024.0))
            sAC.inc(sc.drain(), 1)
            wac(sy)
            sST.inc(sy.dma_start(
                out=bass.AP(al1d.tensor, slab * 512, [[1, 128], [128, 4]]),
                in_=alc[:], allow_slow_non_contiguous=True), 16)
            wst(sy)
            sLD.inc(sy.dma_start(
                out=abca[:], in_=bass.AP(al1d.tensor, slab * 512, [[0, 128], [1, 512]]),
                allow_slow_non_contiguous=True), 16)
            wst(sc)
            for j in range(8):
                sc.sign(wsg[:, j * 512:(j + 1) * 512], wsl[:, j * 512:(j + 1) * 512])
            sAC.inc(sc.drain(), 1)
            wac(ve)
            wld(ve)
            for j in range(8):
                ve.tensor_tensor(wsg[:, j * 512:(j + 1) * 512],
                                 wsg[:, j * 512:(j + 1) * 512], abca[:], ALU.mult)
            sDV.inc(ve.drain(), 1)
            wdv(sy)
            for j in range(8):
                sST.inc(sy.dma_start(
                    out=bw1d[j * 128:(j + 1) * 128, slab * 512:(slab + 1) * 512],
                    in_=wsg[:, j * 512:(j + 1) * 512]), 16)

        for slab in range(2):
            for half in range(4):
                wst(sy)
                wdv(sy)
                for j in range(8):
                    hc = half * 8 + j
                    sLD.inc(sy.dma_start(
                        out=wsl[:, j * 512:(j + 1) * 512],
                        in_=w2T[hc * 128:(hc + 1) * 128, slab * 512:(slab + 1) * 512]), 16)
                wld(sc)
                wdv(te)
                rec_pe_ab = [sPE.v, sPE.v]
                rec_ac_ab = {}
                for u in range(32):
                    ds_, j = u // 8, u % 8
                    par = u % 2
                    sc.wait_ge(sPE.h, rec_pe_ab[par])
                    sc.activation(wab[:, par * 128:(par + 1) * 128],
                                  wsl[:, j * 512 + ds_ * 128: j * 512 + (ds_ + 1) * 128],
                                  AF.Abs)
                    rec_ac_ab[u] = sAC.inc(sc.drain(), 1)
                    te.wait_ge(sAC.h, rec_ac_ab[u])
                    te.matmul(psP2[:, ds_ * 64:(ds_ + 1) * 64],
                              wab[:, par * 128:(par + 1) * 128], ones64b[:],
                              start=(half == 0 and u == 0),
                              stop=(half == 3 and u == 31),
                              skip_group_check=True)
                    rec_pe_ab[par] = sPE.inc(te.drain(), 1)
                wst(sc)
                for j in range(8):
                    sc.sign(wsg[:, j * 512:(j + 1) * 512], wsl[:, j * 512:(j + 1) * 512])
                sAC.inc(sc.drain(), 1)
                wac(sy)
                for j in range(8):
                    hc = half * 8 + j
                    sST.inc(sy.dma_start(
                        out=bw2d[hc * 128:(hc + 1) * 128, slab * 512:(slab + 1) * 512],
                        in_=wsg[:, j * 512:(j + 1) * 512]), 16)
            wpe(sc)
            sc.activation(alc[:], psP2[:, 0:256].rearrange("p (h s) -> p h s", h=4)[:, :, 0],
                          AF.Identity, scale=np.float32(1.0 / 4096.0))
            sAC.inc(sc.drain(), 1)
            wac(sy)
            sST.inc(sy.dma_start(
                out=bass.AP(al2d.tensor, slab * 512, [[1, 128], [128, 4]]),
                in_=alc[:], allow_slow_non_contiguous=True), 16)
            wst(sy)
            sLD.inc(sy.dma_start(
                out=abca[:], in_=bass.AP(al2d.tensor, slab * 512, [[0, 128], [1, 512]]),
                allow_slow_non_contiguous=True), 16)
            for half in range(4):
                wst(sy)
                wdv(sy)
                for j in range(8):
                    hc = half * 8 + j
                    sLD.inc(sy.dma_start(
                        out=wsg[:, j * 512:(j + 1) * 512],
                        in_=bw2d[hc * 128:(hc + 1) * 128, slab * 512:(slab + 1) * 512]), 16)
                wld(ve)
                for j in range(8):
                    ve.tensor_tensor(wsg[:, j * 512:(j + 1) * 512],
                                     wsg[:, j * 512:(j + 1) * 512], abca[:], ALU.mult)
                sDV.inc(ve.drain(), 1)
                wdv(sy)
                for j in range(8):
                    hc = half * 8 + j
                    sST.inc(sy.dma_start(
                        out=bw2d[hc * 128:(hc + 1) * 128, slab * 512:(slab + 1) * 512],
                        in_=wsg[:, j * 512:(j + 1) * 512]), 16)
        wst(sy)
        barrier()
        pctx.__exit__(None, None, None)
        if STOP_AFTER == "prep":
            if KDBG:
                wst(sy)
                sLD.inc(sy.dma_start(out=wab[:, 0:512], in_=bw1d[0:128, 0:512]), 16)
                wld(ve)
                ve.tensor_copy(wsl[:, 0:512], wab[:, 0:512])
                sDV.inc(ve.drain(), 1)
                wdv(sy)
                sST.inc(sy.dma_start(out=dbg_pp[0:128, 1024:1536], in_=wsl[:, 0:512]), 16)
                wst(sy)
                sLD.inc(sy.dma_start(out=wab[:, 0:512], in_=bw2d[384:512, 512:1024]), 16)
                wld(ve)
                ve.tensor_copy(wsl[:, 0:512], wab[:, 0:512])
                sDV.inc(ve.drain(), 1)
                wdv(sy)
                sST.inc(sy.dma_start(out=dbg_pp[0:128, 1536:2048], in_=wsl[:, 0:512]), 16)
                sy.wait_ge(sST.h, sST.v)
            pctx.__exit__(None, None, None)
            actx.__exit__(None, None, None)
            return

        def make_ws(ctx, tag):
            w = {}
            w["xb"] = mksb(ctx, f"xb{tag}", [128, 4 * D], F32)
            w["sq"] = mksb(ctx, f"sq{tag}", [128, D], F32)
            w["st_s"] = mksb(ctx, f"sts{tag}", [128, 4], F32)
            w["st_q"] = mksb(ctx, f"stq{tag}", [128, 4], F32)
            w["mean4"] = mksb(ctx, f"mean{tag}", [128, 4], F32)
            w["var4"] = mksb(ctx, f"var{tag}", [128, 4], F32)
            w["rstd4"] = mksb(ctx, f"rstd{tag}", [128, 4], F32)
            w["nmr4"] = mksb(ctx, f"nmr{tag}", [128, 4], F32)
            w["nt1"] = mksb(ctx, f"nt1{tag}", [128, 4], F32)
            w["nt2"] = mksb(ctx, f"nt2{tag}", [128, 4], F32)
            w["xn_tm"] = mksb(ctx, f"xntm{tag}", [128, 4 * D], BF16)
            w["xn_fm"] = mksb(ctx, f"xnfm{tag}", [128, 8 * 512], BF16)
            w["h_fm"] = mksb(ctx, f"hfm{tag}", [128, 32 * 512], BF16)
            w["w1s"] = mksb(ctx, f"w1s{tag}", [128, 2, 8 * 512], BF16)
            w["w2s"] = mksb(ctx, f"w2s{tag}", [128, 2, 512], BF16)
            w["psT"] = mkps(ctx, f"psT{tag}", [128, 1024], BF16)
            w["psH"] = [mkps(ctx, f"psH{tag}a", [128, 512], F32),
                        mkps(ctx, f"psH{tag}b", [128, 512], F32)]
            w["psY"] = [mkps(ctx, f"psY{tag}{i}", [128, 512], F32) for i in range(4)]
            return w

        def emit_ln(w, src, dst):
            for t in range(4):
                xt = src[:, t * D:(t + 1) * D]
                ve.reduce_sum(w["st_s"][:, t:t + 1], xt, axis=AX)
                wdv(sc)
                sc.square(w["sq"][:], xt)
                sAC.inc(sc.drain(), 1)
                wac(ve)
                ve.reduce_sum(w["st_q"][:, t:t + 1], w["sq"][:], axis=AX)
                sDV.inc(ve.drain(), 1)
            ve.tensor_scalar_mul(w["mean4"][:], w["st_s"][:], 1.0 / D)
            sDV.inc(ve.drain(), 1)
            wdv(sc)
            sc.square(w["nt1"][:], w["mean4"][:])
            sAC.inc(sc.drain(), 1)
            wac(ve)
            ve.scalar_tensor_tensor(w["var4"][:], w["st_q"][:], 1.0 / D, w["nt1"][:],
                                    ALU.mult, ALU.subtract)
            sDV.inc(ve.drain(), 1)
            wdv(sc)
            sc.activation(w["rstd4"][:], w["var4"][:], AF.Sqrt, bias=np.float32(EPS))
            sAC.inc(sc.drain(), 1)
            wac(ve)
            ve.reciprocal(w["nt2"][:], w["rstd4"][:])
            sDV.inc(ve.drain(), 1)
            wdv(ve)
            ve.scalar_tensor_tensor(w["nmr4"][:], w["mean4"][:], -1.0, w["nt2"][:],
                                    ALU.mult, ALU.mult)
            sDV.inc(ve.drain(), 1)
            wdv(ve)
            wpe(ve)
            for t in range(4):
                ve.tensor_scalar(dst[:, t * D:(t + 1) * D], src[:, t * D:(t + 1) * D],
                                 w["nt2"][:, t:t + 1], w["nmr4"][:, t:t + 1],
                                 ALU.mult, ALU.add)
            sDV.inc(ve.drain(), 1)

        def emit_tp(w, src, dst, ntile=4):
            for t in range(ntile):
                wdv(te)
                for j in range(8):
                    te.transpose(w["psT"][:, j * 128:(j + 1) * 128],
                                 src[:, t * D + j * 128: t * D + (j + 1) * 128],
                                 identb[:])
                sPE.inc(te.drain(), 1)
                wpe(ve)
                dv = dst[:].rearrange("p (j n) -> p j n", j=8)
                sv = w["psT"][:].rearrange("p (j n) -> p j n", j=8)
                ve.tensor_copy(dv[:, :, t * 128:(t + 1) * 128], sv[:, :, :])
                sDV.inc(ve.drain(), 1)

        def emit_ffn(w, xres, yout):
            rec_pe_w1 = [0, 0]
            rec_ac_h = [0, 0]
            rec_sc_w1 = {}

            def load_w1(hg_):
                par_ = hg_ % 2
                sc.wait_ge(sPE.h, rec_pe_w1[par_])
                for j in range(8):
                    sSC.inc(sc.dma_start(
                        out=w["w1s"][:, par_, j * 512:(j + 1) * 512],
                        in_=bw1d[j * 128:(j + 1) * 128, hg_ * 512:(hg_ + 1) * 512]), 16)
                rec_sc_w1[hg_] = sSC.v

            wdv(te)
            load_w1(0)
            for hg in range(8):
                par = hg % 2
                if hg + 1 < 8:
                    load_w1(hg + 1)
                te.wait_ge(sSC.h, rec_sc_w1[hg])
                for ht in range(4):
                    hpar = ht % 2
                    hidx = hg * 4 + ht
                    te.wait_ge(sAC.h, rec_ac_h[hpar])
                    for j in range(8):
                        te.matmul(w["psH"][hpar][:],
                                  w["w1s"][:, par, j * 512 + ht * 128: j * 512 + (ht + 1) * 128],
                                  w["xn_fm"][:, j * 512:(j + 1) * 512],
                                  start=(j == 0), stop=False, skip_group_check=True)
                    te.matmul(w["psH"][hpar][:], b1h[0:1, hidx * 128:(hidx + 1) * 128],
                              onesb[0:1, :], start=False, stop=False, skip_group_check=True)
                    te.matmul(w["psH"][hpar][:], b1l[0:1, hidx * 128:(hidx + 1) * 128],
                              onesb[0:1, :], start=False, stop=True, skip_group_check=True)
                    sPE.inc(te.drain(), 1)
                    wpe(sc)
                    sc.activation(w["h_fm"][:, hidx * 512:(hidx + 1) * 512],
                                  w["psH"][hpar][:], AF.Gelu)
                    rec_ac_h[hpar] = sAC.inc(sc.drain(), 1)
                rec_pe_w1[par] = sPE.v
            wac(te)
            rec_pe_w2 = [0, 0]
            rec_sc_w2 = {}

            def load_w2(idx_):
                dh_, hc_ = idx_ // 32, idx_ % 32
                par_ = idx_ % 2
                sc.wait_ge(sPE.h, rec_pe_w2[par_])
                sSC.inc(sc.dma_start(
                    out=w["w2s"][:, par_, :],
                    in_=bw2d[hc_ * 128:(hc_ + 1) * 128, dh_ * 512:(dh_ + 1) * 512]), 16)
                rec_sc_w2[idx_] = sSC.v

            load_w2(0)
            for dh in range(2):
                for hc in range(32):
                    idx = dh * 32 + hc
                    par = idx % 2
                    if idx + 1 < 64:
                        load_w2(idx + 1)
                    te.wait_ge(sSC.h, rec_sc_w2[idx])
                    for tt in range(4):
                        te.matmul(w["psY"][tt][:],
                                  w["h_fm"][:, hc * 512 + tt * 128: hc * 512 + (tt + 1) * 128],
                                  w["w2s"][:, par, :],
                                  start=(hc == 0), stop=False, skip_group_check=True)
                    rec_pe_w2[par] = sPE.inc(te.drain(), 1)
                for tt in range(4):
                    te.matmul(w["psY"][tt][:], onesb[0:1, 0:128],
                              b2h[0:1, dh * 512:(dh + 1) * 512],
                              start=False, stop=False, skip_group_check=True)
                    te.matmul(w["psY"][tt][:], onesb[0:1, 0:128],
                              b2l[0:1, dh * 512:(dh + 1) * 512],
                              start=False, stop=True, skip_group_check=True)
                sPE.inc(te.drain(), 1)
                wpe(ve)
                for tt in range(4):
                    ve.scalar_tensor_tensor(
                        yout[:, tt * D + dh * 512: tt * D + (dh + 1) * 512],
                        w["psY"][tt][:], 0.5,
                        xres[:, tt * D + dh * 512: tt * D + (dh + 1) * 512],
                        ALU.mult, ALU.add)
                sDV.inc(ve.drain(), 1)
                wdv(te)

        wsA = make_ws(actx, "A")
        x1b = mksb(actx, "x1bA", [128, 4 * D], F32)
        kstage = mksb(actx, "kstA", [128, 2, 512], BF16)
        vstage = mksb(actx, "vstA", [128, 2, 520], BF16)
        qin = mksb(actx, "qinA", [128, D], BF16)
        qin_fm = mksb(actx, "qinfA", [128, 8 * 128], BF16)
        qstage = mksb(actx, "qstA", [128, 8 * 128], BF16)

        for bat in range(NB):
            wdv(sy)
            wac(sy)
            for t in range(4):
                sLD.inc(sy.dma_start(
                    out=wsA["xb"][:, t * D:(t + 1) * D],
                    in_=xin[bat * TB + t * 128: bat * TB + (t + 1) * 128, :]), 16)
            wld(ve)
            wld(sc)
            emit_ln(wsA, wsA["xb"], wsA["xn_tm"])
            emit_tp(wsA, wsA["xn_tm"], wsA["xn_fm"])
            if STOP_AFTER == "A1" and bat == 0:
                break
            emit_ffn(wsA, wsA["xb"], x1b)
            if STOP_AFTER == "A2" and bat == 0:
                break
            wdv(sy)
            for t in range(4):
                sST.inc(sy.dma_start(
                    out=x1d[bat * TB + t * 128: bat * TB + (t + 1) * 128, :],
                    in_=x1b[:, t * D:(t + 1) * D]), 16)
            emit_ln(wsA, x1b, wsA["xn_tm"])
            wdv(sy)
            for t in range(4):
                sST.inc(sy.dma_start(out=xnqd[t],
                                     in_=wsA["xn_tm"][:, t * D:(t + 1) * D]), 16)
            emit_tp(wsA, wsA["xn_tm"], wsA["xn_fm"])
            wdv(te)
            for mt in range(8):
                par = mt % 2
                wac(te)
                for j in range(8):
                    te.matmul(wsA["psH"][par][:],
                              wkRT[:, j * D + mt * 128: j * D + (mt + 1) * 128],
                              wsA["xn_fm"][:, j * 512:(j + 1) * 512],
                              start=(j == 0), stop=False, skip_group_check=True)
                te.matmul(wsA["psH"][par][:], bkh[0:1, mt * 128:(mt + 1) * 128],
                          onesb[0:1, :], start=False, stop=False, skip_group_check=True)
                te.matmul(wsA["psH"][par][:], bkl[0:1, mt * 128:(mt + 1) * 128],
                          onesb[0:1, :], start=False, stop=True, skip_group_check=True)
                sPE.inc(te.drain(), 1)
                wpe(sc)
                wst(sc)
                sc.activation(kstage[:, par, :], wsA["psH"][par][:], AF.Identity)
                sAC.inc(sc.drain(), 1)
                wac(sy)
                sST.inc(sy.dma_start(
                    out=kd[mt * 128:(mt + 1) * 128, bat * TB:(bat + 1) * TB],
                    in_=kstage[:, par, :]), 16)
            if STOP_AFTER == "A3" and bat == 0:
                break
            wac(te)
            for tt in range(4):
                for mh in range(2):
                    par = (tt * 2 + mh) % 2
                    wdv(te)
                    for j in range(8):
                        te.matmul(wsA["psH"][par][:],
                                  wsA["xn_fm"][:, j * 512 + tt * 128: j * 512 + (tt + 1) * 128],
                                  wvTb[:, j * D + mh * 512: j * D + (mh + 1) * 512],
                                  start=(j == 0), stop=False, skip_group_check=True)
                    te.matmul(wsA["psH"][par][:], onesb[0:1, 0:128],
                              bvh[0:1, mh * 512:(mh + 1) * 512],
                              start=False, stop=False, skip_group_check=True)
                    te.matmul(wsA["psH"][par][:], onesb[0:1, 0:128],
                              bvl[0:1, mh * 512:(mh + 1) * 512],
                              start=False, stop=True, skip_group_check=True)
                    sPE.inc(te.drain(), 1)
                    wst(gp)
                    gp.memset(vstage[:, par, :], 1.0)
                    sGP.inc(gp.drain(), 1)
                    wgp(ve)
                    wpe(ve)
                    for g in range(8):
                        ve.tensor_copy(vstage[:, par, g * 65:g * 65 + 64],
                                       wsA["psH"][par][:, g * 64:(g + 1) * 64])
                    sDV.inc(ve.drain(), 1)
                    wdv(sy)
                    sST.inc(sy.dma_start(
                        out=vd[bat * 4 + tt, :, mh * 520:(mh + 1) * 520],
                        in_=vstage[:, par, :]), 16)
            if STOP_AFTER == "A4" and bat == 0:
                break
            wst(sy)
            sLD.inc(sy.dma_start(
                out=qin[:],
                in_=xnqd[ds(ccv, 1), :, :].rearrange("o p n -> (o p) n")), 16)
            wld(te)
            wdv(te)
            for j in range(8):
                te.transpose(wsA["psT"][:, j * 128:(j + 1) * 128],
                             qin[:, j * 128:(j + 1) * 128], identb[:])
            sPE.inc(te.drain(), 1)
            wpe(ve)
            ve.tensor_copy(qin_fm[:], wsA["psT"][:])
            sDV.inc(ve.drain(), 1)
            wdv(te)
            wst(sc)
            for mt in range(8):
                par = mt % 2
                wac(te)
                for j in range(8):
                    te.matmul(wsA["psH"][par][:, 0:128],
                              wqRT[:, j * D + mt * 128: j * D + (mt + 1) * 128],
                              qin_fm[:, j * 128:(j + 1) * 128],
                              start=(j == 0), stop=False, skip_group_check=True)
                te.matmul(wsA["psH"][par][:, 0:128], bqh[0:1, mt * 128:(mt + 1) * 128],
                          onesb[0:1, 0:128], start=False, stop=False, skip_group_check=True)
                te.matmul(wsA["psH"][par][:, 0:128], bql[0:1, mt * 128:(mt + 1) * 128],
                          onesb[0:1, 0:128], start=False, stop=True, skip_group_check=True)
                sPE.inc(te.drain(), 1)
                wpe(sc)
                sc.activation(qstage[:, mt * 128:(mt + 1) * 128],
                              wsA["psH"][par][:, 0:128], AF.Identity)
                sAC.inc(sc.drain(), 1)
            wac(sy)
            sST.inc(sy.dma_start(out=qd[bat], in_=qstage[:]), 16)
            wst(sy)
            if STOP_AFTER == "A5" and bat == 0:
                break
        wst(sy)
        barrier()
        if KDBG and STOP_AFTER == "A2":
            for t in range(4):
                sST.inc(sy.dma_start(out=dbg_x1[t * 128:(t + 1) * 128, :],
                                     in_=x1b[:, t * D:(t + 1) * D]), 16)
            sST.inc(sy.dma_start(out=dbg_k, in_=wsA["h_fm"][:, 0:512]), 16)
            sST.inc(sy.dma_start(out=dbg_q, in_=wsA["h_fm"][:, 512:1536]), 16)
            wst(sy)
        if KDBG and STOP_AFTER == "A1":
            for t in range(4):
                sST.inc(sy.dma_start(out=dbg_x1[t * 128:(t + 1) * 128, :],
                                     in_=wsA["xb"][:, t * D:(t + 1) * D]), 16)
            ve.tensor_copy(x1b[:, 0:D], wsA["xn_tm"][:, 0:D])
            ve.tensor_copy(x1b[:, D:2 * D], wsA["xn_fm"][:, 0:D])
            sDV.inc(ve.drain(), 1)
            wdv(sy)
            sST.inc(sy.dma_start(out=dbg_pp[:, 0:1024], in_=x1b[:, 0:D]), 16)
            sST.inc(sy.dma_start(out=dbg_pp[:, 1024:2048], in_=x1b[:, D:2 * D]), 16)
            sST.inc(sy.dma_start(out=dbg_pp[:, 2048:2052], in_=wsA["st_s"][:]), 16)
            sST.inc(sy.dma_start(out=dbg_pp[:, 2052:2056], in_=wsA["st_q"][:]), 16)
            sST.inc(sy.dma_start(out=dbg_pp[:, 2056:2060], in_=wsA["rstd4"][:]), 16)
            sST.inc(sy.dma_start(out=dbg_pp[:, 2060:2064], in_=wsA["nmr4"][:]), 16)
            wst(sy)
        if KDBG and STOP_AFTER not in ("A1", "A2"):
            for t in range(4):
                sLD.inc(sy.dma_start(out=wsA["xb"][:, t * D:(t + 1) * D],
                                     in_=x1d[t * 128:(t + 1) * 128, :]), 16)
            wld(sy)
            for t in range(4):
                sST.inc(sy.dma_start(out=dbg_x1[t * 128:(t + 1) * 128, :],
                                     in_=wsA["xb"][:, t * D:(t + 1) * D]), 16)
            sLD.inc(sy.dma_start(out=kstage[:, 0, :], in_=kd[0:128, 0:512]), 16)
            sLD.inc(sy.dma_start(out=qstage[:], in_=qd[0]), 16)
            sLD.inc(sy.dma_start(out=vstage[:, 0, 0:520], in_=vd[0][:, 0:520]), 16)
            sLD.inc(sy.dma_start(out=vstage[:, 1, 0:520], in_=vd[0][:, 520:1040]), 16)
            wld(sy)
            sST.inc(sy.dma_start(out=dbg_k, in_=kstage[:, 0, :]), 16)
            sST.inc(sy.dma_start(out=dbg_q, in_=qstage[:]), 16)
            sST.inc(sy.dma_start(out=dbg_v[:, 0:520], in_=vstage[:, 0, 0:520]), 16)
            sST.inc(sy.dma_start(out=dbg_v[:, 520:1040], in_=vstage[:, 1, 0:520]), 16)
            wst(sy)
            barrier()
        actx.__exit__(None, None, None)
        if STOP_AFTER in ("A", "A1", "A2", "A3", "A4", "A5"):
            return

        bctx = ExitStack()
        bctx.__enter__()
        o_fm = mksb(bctx, "o_fm", [128, 8 * 2048], BF16)
        battn = ExitStack()
        battn.__enter__()
        ksl = mksb(battn, "ksl", [128, 2, 8 * 512], BF16)
        vsl = mksb(battn, "vsl", [128, 2, 4 * 1040], BF16)
        qsl = mksb(battn, "qsl", [128, 2, 1024], BF16)
        qdup = mksb(battn, "qdup", [128, 2, 512], BF16)
        expS = mksb(battn, "expS", [128, 2, 1024], BF16)
        recq = mksb(battn, "recq", [1, 512], F32)
        rech = mksb(battn, "rech", [1, 512], BF16)
        bcst = mksb(battn, "bcst", [64, 512], F32)
        psS = mkps(battn, "psS", [128, 2048], F32)
        dbgs = mksb(battn, "dbgsB", [128, 1024], F32) if KDBG else None
        psO = [mkps(battn, "psOa", [128, 512], F32), mkps(battn, "psOb", [128, 512], F32)]
        psB = mkps(battn, "psB", [64, 512], F32)

        for tb_ in (ksl, vsl, qsl, qdup, expS):
            gp.memset(tb_[:], 0.0)
        gp.memset(bcst[:], 0.0)
        sGP.inc(gp.drain(), 1)
        barrier()

        gp_head = []

        def load_head(hd, par):
            sGP.inc(sy.dma_start(out=qsl[:, par, :], in_=qd[hd]), 16)
            for mt in range(8):
                sGP.inc(sy.dma_start(
                    out=ksl[:, par, mt * 512:(mt + 1) * 512],
                    in_=kd[mt * 128:(mt + 1) * 128, hd * TB:(hd + 1) * TB]), 16)
            for tt in range(4):
                sGP.inc(sy.dma_start(
                    out=vsl[:, par, tt * 1040:(tt + 1) * 1040],
                    in_=vd[hd * 4 + tt]), 16)
            gp_head.append(sGP.v)

        rec_pe_head_end = [0, 0]
        rec_dv_norm = [0, 0]
        load_head(0, 0)
        for hd in range(NH):
            par = hd % 2
            if hd + 1 < NH:
                sy.wait_ge(sPE.h, rec_pe_head_end[(hd + 1) % 2])
                load_head(hd + 1, (hd + 1) % 2)
            te.wait_ge(sGP.h, gp_head[hd])
            ve.wait_ge(sGP.h, gp_head[hd])
            for G in range(4):
                gpar = G % 2
                wpe(ve)
                for gl in range(4):
                    g = G * 4 + gl
                    srcq = qsl[(g % 2) * 64:(g % 2) * 64 + 64, par,
                               (g // 2) * 128:(g // 2 + 1) * 128]
                    ve.tensor_copy(qdup[0:64, gpar, gl * 128:(gl + 1) * 128], srcq)
                    ve.tensor_copy(qdup[64:128, gpar, gl * 128:(gl + 1) * 128], srcq)
                sDV.inc(ve.drain(), 1)
                wdv(te)
                rec_ac_exp = {}
                rec_pe_pv = {}
                for pj in range(32):
                    mt, ttp = pj // 4, pj % 4
                    p2 = pj % 2
                    if pj >= 2:
                        te.wait_ge(sAC.h, rec_ac_exp[pj - 2])
                    if pj == 0:
                        te.wait_ge(sDV.h, rec_dv_norm[gpar])
                    kslice = ksl[:, par, mt * 512 + ttp * 128: mt * 512 + (ttp + 1) * 128]
                    te.matmul(psS[:, p2 * 1024: p2 * 1024 + 512],
                              kslice[0:64, :], qdup[0:64, gpar, :],
                              start=True, stop=True, tile_position=(0, 0),
                              skip_group_check=True)
                    te.matmul(psS[:, p2 * 1024 + 512: p2 * 1024 + 1024],
                              kslice[64:128, :], qdup[64:128, gpar, :],
                              start=True, stop=True, tile_position=(64, 0),
                              skip_group_check=True)
                    pe_qk = sPE.inc(te.drain(), 1)
                    sc.wait_ge(sPE.h, pe_qk)
                    sc.activation(expS[:, p2, :], psS[:, p2 * 1024:(p2 + 1) * 1024],
                                  AF.Exp, bias=np.float32(-EXPC), scale=np.float32(0.125))
                    rec_ac_exp[pj] = sAC.inc(sc.drain(), 1)
                    te.wait_ge(sAC.h, rec_ac_exp[pj])
                    first = (pj == 0)
                    te.matmul(psO[gpar][0:65, :],
                              vsl[:, par, ttp * 1040 + (2 * mt) * 65:
                                        ttp * 1040 + (2 * mt) * 65 + 65],
                              expS[:, p2, 0:512],
                              start=first, stop=False, skip_group_check=True)
                    te.matmul(psO[gpar][0:65, :],
                              vsl[:, par, ttp * 1040 + (2 * mt + 1) * 65:
                                        ttp * 1040 + (2 * mt + 1) * 65 + 65],
                              expS[:, p2, 512:1024],
                              start=False, stop=(pj == 31), skip_group_check=True)
                    if pj == 31:
                        sPE.inc(te.drain(), 1)
                wpe(ve)
                ve.reciprocal(recq[:], psO[gpar][64:65, :])
                sDV.inc(ve.drain(), 1)
                wdv(ve)
                ve.tensor_copy(rech[:], recq[:])
                sDV.inc(ve.drain(), 1)
                wdv(te)
                te.matmul(psB[0:64, :], onesb[0:1, 0:64], rech[0:1, :],
                          start=True, stop=True, skip_group_check=True)
                sPE.inc(te.drain(), 1)
                wpe(sc)
                sc.wait_ge(sDV.h, rec_dv_norm[1 - gpar])
                sc.activation(bcst[0:64, :], psB[0:64, :], AF.Identity)
                sAC.inc(sc.drain(), 1)
                wac(ve)
                mt_o, basep = hd // 2, (hd % 2) * 64
                ve.tensor_tensor(
                    o_fm[basep:basep + 64,
                         mt_o * 2048 + G * 512: mt_o * 2048 + (G + 1) * 512],
                    psO[gpar][0:64, :], bcst[0:64, :], ALU.mult)
                if KDBG and hd == 0 and G == 0:
                    ve.tensor_copy(dbgs[0:65, 0:512], psO[gpar][0:65, :])
                    ve.tensor_copy(dbgs[0:64, 512:1024],
                                   o_fm[0:64, 0:512])
                rec_dv_norm[gpar] = sDV.inc(ve.drain(), 1)
            rec_pe_head_end[par] = sPE.v
        barrier()
        if KDBG:
            sST.inc(sy.dma_start(out=dbg_o, in_=o_fm[:, 0:2048]), 16)
            sST.inc(sy.dma_start(out=dbg_pp[0:65, 2560:3072], in_=dbgs[0:65, 0:512]), 16)
            sST.inc(sy.dma_start(out=dbg_pp[0:64, 3072:3584], in_=dbgs[0:64, 512:1024]), 16)
            sST.inc(sy.dma_start(out=dbg_bw1, in_=o_fm[0:128, 0:512]), 16)
            wst(sy)
        battn.__exit__(None, None, None)
        if STOP_AFTER == "B":
            bctx.__exit__(None, None, None)
            return

        cctx = ExitStack()
        cctx.__enter__()
        woTb = mksb(cctx, "woTb", [128, 8 * D], BF16)
        wsC = make_ws(cctx, "C")
        x1t = mksb(cctx, "x1tC", [128, D], F32)
        outb = mksb(cctx, "outbC", [128, 4 * D], F32)
        x2b = wsC["xb"]

        for j in range(8):
            wdv(sy)
            sLD.inc(sy.dma_start(out=x2b[:, 0:D], in_=woT[j * 128:(j + 1) * 128, :]), 16)
            wld(ve)
            ve.tensor_copy(woTb[:, j * D:(j + 1) * D], x2b[:, 0:D])
            sDV.inc(ve.drain(), 1)
        wdv(sy)

        for sbi in range(4):
            wac(ve)
            for gl in range(4):
                g = sbi * 4 + gl
                G, g_l = g // 4, g % 4
                wdv(sy)
                sLD.inc(sy.dma_start(
                    out=x1t[:],
                    in_=x1d.rearrange("(a b) d -> a b d", b=16)[ds(ccv128, 128), g, :]), 16)
                for jh in range(2):
                    par = jh % 2
                    wdv(te)
                    for mt in range(8):
                        te.matmul(wsC["psH"][par][:],
                                  o_fm[:, mt * 2048 + G * 512 + g_l * 128:
                                       mt * 2048 + G * 512 + (g_l + 1) * 128],
                                  woTb[:, mt * D + jh * 512: mt * D + (jh + 1) * 512],
                                  start=(mt == 0), stop=False, skip_group_check=True)
                    te.matmul(wsC["psH"][par][:], onesb[0:1, 0:128],
                              boh[0:1, jh * 512:(jh + 1) * 512],
                              start=False, stop=False, skip_group_check=True)
                    te.matmul(wsC["psH"][par][:], onesb[0:1, 0:128],
                              bol[0:1, jh * 512:(jh + 1) * 512],
                              start=False, stop=True, skip_group_check=True)
                    sPE.inc(te.drain(), 1)
                    wpe(ve)
                    wld(ve)
                    ve.tensor_add(x2b[:, gl * D + jh * 512: gl * D + (jh + 1) * 512],
                                  x1t[:, jh * 512:(jh + 1) * 512], wsC["psH"][par][:])
                    sDV.inc(ve.drain(), 1)
            wdv(sc)
            emit_ln(wsC, x2b, wsC["xn_tm"])
            emit_tp(wsC, wsC["xn_tm"], wsC["xn_fm"])
            wst(ve)
            emit_ffn(wsC, x2b, outb)
            wdv(sy)
            for gl in range(4):
                g = sbi * 4 + gl
                sST.inc(sy.dma_start(out=out[g * 128:(g + 1) * 128, :],
                                     in_=outb[:, gl * D:(gl + 1) * D]), 16)
        wst(sy)

        cctx.__exit__(None, None, None)
        bctx.__exit__(None, None, None)

    gctx.__exit__(None, None, None)
    return nc


_NC_CACHE = {}


def kernel(L0_attn_x, L0_attn_wq, L0_attn_bq, L0_attn_wk, L0_attn_bk, L0_attn_wv,
           L0_attn_bv, L0_attn_wo, L0_attn_bo, L0_attn_rot_cos, L0_attn_rot_sin,
           L0_ffn_w1, L0_ffn_b1, L0_ffn_w2, L0_ffn_b2):
    x = np.asarray(L0_attn_x, dtype=np.float32)
    if "nc" not in _NC_CACHE:
        _NC_CACHE["nc"] = build_nc()
    nc = _NC_CACHE["nc"]

    f32 = np.float32
    common = {
        "w1T": np.ascontiguousarray(np.asarray(L0_ffn_w1, f32).T),
        "w2T": np.ascontiguousarray(np.asarray(L0_ffn_w2, f32).T),
        "wqT": np.ascontiguousarray(np.asarray(L0_attn_wq, f32).T),
        "wkT": np.ascontiguousarray(np.asarray(L0_attn_wk, f32).T),
        "wvT": np.ascontiguousarray(np.asarray(L0_attn_wv, f32).T),
        "woT": np.ascontiguousarray(np.asarray(L0_attn_wo, f32).T),
        "b1": np.ascontiguousarray(np.asarray(L0_ffn_b1, f32)),
        "b2": np.ascontiguousarray(np.asarray(L0_ffn_b2, f32)),
        "bq": np.ascontiguousarray(np.asarray(L0_attn_bq, f32)),
        "bk": np.ascontiguousarray(np.asarray(L0_attn_bk, f32)),
        "bv": np.ascontiguousarray(np.asarray(L0_attn_bv, f32)),
        "bo": np.ascontiguousarray(np.asarray(L0_attn_bo, f32)),
        "rotc": np.ascontiguousarray(np.asarray(L0_attn_rot_cos, f32)[4095]),
        "rots": np.ascontiguousarray(np.asarray(L0_attn_rot_sin, f32)[4095]),
    }
    in_maps = []
    for c in range(8):
        m = dict(common)
        m["xin"] = np.ascontiguousarray(x[c // 4])
        in_maps.append(m)

    res = run_bass_kernel_spmd(nc, in_maps, core_ids=list(range(8)))

    outp = np.empty((2, TFULL, D), np.float32)
    for c in range(8):
        bc, cc = c // 4, c % 4
        r = np.asarray(res.results[c]["out"]).reshape(16, 128, D)
        outp[bc, cc * 2048:(cc + 1) * 2048] = r.transpose(1, 0, 2).reshape(2048, D)
    return outp


# version 06
# speedup vs baseline: 1.1573x; 1.0310x over previous
import os
import numpy as np
import ml_dtypes
import concourse.bass as bass
import concourse.mybir as mybir
from concourse.bass import ds
from concourse.bass_utils import run_bass_kernel_spmd
from contextlib import ExitStack

F32 = mybir.dt.float32
BF16 = mybir.dt.bfloat16
AF = mybir.ActivationFunctionType
ALU = mybir.AluOpType
AX = mybir.AxisListType.X

D = 1024
HF = 4096
NH = 16
TFULL = 8192
NB = 16
TB = 512
EPS = 1e-5
EXPC = 12.0
CONSTS = [EPS, -EXPC, 0.125, 1.0 / 1024.0, 1.0 / 4096.0, 0.5]


STOP_AFTER = os.environ.get("KSTOP", "")
KDBG = os.environ.get("KDBG", "") == "1"



def build_nc():
    nc = bass.Bass("TRN2", target_bir_lowering=False, debug=False)

    xin = nc.dram_tensor("xin", [TFULL, D], F32, kind="ExternalInput").ap()
    w1T = nc.dram_tensor("w1T", [D, HF], F32, kind="ExternalInput").ap()
    w2T = nc.dram_tensor("w2T", [HF, D], F32, kind="ExternalInput").ap()
    wqT = nc.dram_tensor("wqT", [D, D], F32, kind="ExternalInput").ap()
    wkT = nc.dram_tensor("wkT", [D, D], F32, kind="ExternalInput").ap()
    wvT = nc.dram_tensor("wvT", [D, D], F32, kind="ExternalInput").ap()
    woT = nc.dram_tensor("woT", [D, D], F32, kind="ExternalInput").ap()
    b1i = nc.dram_tensor("b1", [HF], F32, kind="ExternalInput").ap()
    b2i = nc.dram_tensor("b2", [D], F32, kind="ExternalInput").ap()
    bqi = nc.dram_tensor("bq", [D], F32, kind="ExternalInput").ap()
    bki = nc.dram_tensor("bk", [D], F32, kind="ExternalInput").ap()
    bvi = nc.dram_tensor("bv", [D], F32, kind="ExternalInput").ap()
    boi = nc.dram_tensor("bo", [D], F32, kind="ExternalInput").ap()
    rci = nc.dram_tensor("rotc", [32], F32, kind="ExternalInput").ap()
    rsi = nc.dram_tensor("rots", [32], F32, kind="ExternalInput").ap()
    out = nc.dram_tensor("out", [2048, D], F32, kind="ExternalOutput").ap()
    if KDBG:
        dbg_pp = nc.dram_tensor("dbg_pp", [128, 4096], F32, kind="ExternalOutput").ap()
        dbg_bw1 = nc.dram_tensor("dbg_bw1", [128, 512], BF16, kind="ExternalOutput").ap()
        dbg_bw2 = nc.dram_tensor("dbg_bw2", [128, 512], BF16, kind="ExternalOutput").ap()
        dbg_x1 = nc.dram_tensor("dbg_x1", [512, D], F32, kind="ExternalOutput").ap()
        dbg_k = nc.dram_tensor("dbg_k", [128, 512], BF16, kind="ExternalOutput").ap()
        dbg_q = nc.dram_tensor("dbg_q", [128, 1024], BF16, kind="ExternalOutput").ap()
        dbg_v = nc.dram_tensor("dbg_v", [128, 1040], BF16, kind="ExternalOutput").ap()
        dbg_o = nc.dram_tensor("dbg_o", [128, 2048], BF16, kind="ExternalOutput").ap()

    bw1d = nc.dram_tensor("bw1d", [D, HF], BF16, kind="Internal").ap()
    bw2d = nc.dram_tensor("bw2d", [HF, D], BF16, kind="Internal").ap()
    kd = nc.dram_tensor("kd", [D, TFULL], BF16, kind="Internal").ap()
    vd = nc.dram_tensor("vd", [64, 128, 16 * 65], BF16, kind="Internal").ap()
    qd = nc.dram_tensor("qd", [NH, 128, 8 * 128], BF16, kind="Internal").ap()
    x1d = nc.dram_tensor("x1d", [TFULL, D], F32, kind="Internal").ap()
    xnqd = nc.dram_tensor("xnqd", [4, 128, D], BF16, kind="Internal").ap()
    rowb = nc.dram_tensor("rowb", [1024], F32, kind="Internal").ap()
    al1d = nc.dram_tensor("al1d", [HF], F32, kind="Internal").ap()
    al2d = nc.dram_tensor("al2d", [D], F32, kind="Internal").ap()
    recb = nc.dram_tensor("recb", [512], F32, kind="Internal").ap()

    gctx = ExitStack()
    gctx.__enter__()

    def mksb(ctx, name, shape, dt):
        return ctx.enter_context(nc.sbuf_tensor(name, shape, dt))

    def mkps(ctx, name, shape, dt):
        return ctx.enter_context(nc.psum_tensor(name, shape, dt))

    class Ctr:
        def __init__(self, name):
            self.h = gctx.enter_context(nc.semaphore(name))
            self.v = 0

        def inc(self, inst, n=1):
            inst.then_inc(self.h, n)
            self.v += n
            return self.v

    sLD = Ctr("sLD")
    sSC = Ctr("sSC")
    sGP = Ctr("sGP")
    sPE = Ctr("sPE")
    sAC = Ctr("sAC")
    sDV = Ctr("sDV")
    sST = Ctr("sST")
    sBR = Ctr("sBR")

    identb = mksb(gctx, "identb", [128, 128], BF16)
    onesb = mksb(gctx, "onesb", [1, 512], BF16)
    ones64b = mksb(gctx, "ones64b", [128, 64], BF16)
    b1h = mksb(gctx, "b1h", [1, HF], BF16)
    b1l = mksb(gctx, "b1l", [1, HF], BF16)
    b2h = mksb(gctx, "b2h", [1, D], BF16)
    b2l = mksb(gctx, "b2l", [1, D], BF16)
    bvh = mksb(gctx, "bvh", [1, D], BF16)
    bvl = mksb(gctx, "bvl", [1, D], BF16)
    boh = mksb(gctx, "boh", [1, D], BF16)
    bol = mksb(gctx, "bol", [1, D], BF16)
    bqh = mksb(gctx, "bqh", [1, D], BF16)
    bql = mksb(gctx, "bql", [1, D], BF16)
    bkh = mksb(gctx, "bkh", [1, D], BF16)
    bkl = mksb(gctx, "bkl", [1, D], BF16)

    blk = gctx.enter_context(nc.Block())

    @blk.sync
    def _(sync):
        te, ve, sc, gp, sy = nc.tensor, nc.vector, nc.scalar, nc.gpsimd, nc.sync
        pid = sy.partition_id()
        ccv = sy.compute_val(pid % 4)
        ccv128 = sy.compute_val(pid % 4 * 128)

        def wpe(e):
            e.wait_ge(sPE.h, sPE.v)

        def wac(e):
            e.wait_ge(sAC.h, sAC.v)

        def wdv(e):
            e.wait_ge(sDV.h, sDV.v)

        def wld(e):
            e.wait_ge(sLD.h, sLD.v)

        def wsc(e):
            e.wait_ge(sSC.h, sSC.v)

        def wst(e):
            e.wait_ge(sST.h, sST.v)

        def wgp(e):
            e.wait_ge(sGP.h, sGP.v)

        def barrier():
            for e in (te, ve, sc, gp, sy):
                sBR.inc(e.drain(), 1)
            tgt = sBR.v
            for e in (te, ve, sc, gp, sy):
                e.wait_ge(sBR.h, tgt)

        actx = ExitStack()
        actx.__enter__()
        wqRT = mksb(actx, "wqRT", [128, 8 * D], BF16)
        wkRT = mksb(actx, "wkRT", [128, 8 * D], BF16)
        wvTb = mksb(actx, "wvTb", [128, 8 * D], BF16)

        pctx = ExitStack()
        pctx.__enter__()
        crow = mksb(pctx, "crow", [1, 128], F32)
        srow = mksb(pctx, "srow", [1, 128], F32)
        ssrow = mksb(pctx, "ssrow", [1, 128], F32)
        signp = mksb(pctx, "signp", [1, 128], F32)
        cbt = mksb(pctx, "cbt", [128, 128], F32)
        ssbt = mksb(pctx, "ssbt", [128, 128], F32)
        brow_f = mksb(pctx, "brow_f", [1, 6 * D + HF], F32)
        btmp = mksb(pctx, "btmp", [1, 2 * D], F32)
        wld_t = mksb(pctx, "wld_t", [128, D], F32)
        wtmp = mksb(pctx, "wtmp", [128, 256], F32)
        wsl = mksb(pctx, "wsl", [128, 8 * 512], F32)
        wab = mksb(pctx, "wab", [128, 512], BF16)
        alc = mksb(pctx, "alc", [128, 4], F32)
        abca = mksb(pctx, "abca", [128, 512], F32)
        wsg = mksb(pctx, "wsg", [128, 8 * 512], BF16)
        psP2 = mkps(pctx, "psP2", [128, 512], F32)

        gp.memset(onesb[:], 1.0)
        gp.memset(ones64b[:], 1.0)
        gp.memset(identb[:], 0.0)
        gp.affine_select(out=identb[:], in_=identb[:], compare_op=ALU.not_equal,
                         fill=1.0, base=0, pattern=[[-1, 128]], channel_multiplier=1)
        gp.memset(signp[:], 1.0)
        sGP.inc(gp.drain(), 1)
        wgp(ve)
        spv = signp[:].rearrange("p (i e) -> p i e", e=2)
        ve.tensor_single_scalar(spv[:, :, 0], spv[:, :, 0], -1.0, ALU.mult)
        sDV.inc(ve.drain(), 1)
        for e in (te, ve, sc, sy):
            wgp(e)

        OB1, OB2, OBV, OBO, OBQ, OBK = 0, HF, HF + D, HF + 2 * D, HF + 3 * D, HF + 4 * D
        for off, t in ((OB1, b1i), (OB2, b2i), (OBV, bvi), (OBO, boi),
                       (OBQ, bqi), (OBK, bki)):
            sLD.inc(sy.dma_start(out=brow_f[0:1, off:off + t.shape[0]],
                                 in_=t[None, :]), 16)
        rr = btmp
        sLD.inc(sy.dma_start(out=rr[0:1, 0:32], in_=rci[None, :]), 16)
        sLD.inc(sy.dma_start(out=rr[0:1, 32:64], in_=rsi[None, :]), 16)
        wld(ve)
        for b in range(2):
            for e in range(2):
                cv = crow[0:1, b * 64:(b + 1) * 64].rearrange("p (i e) -> p i e", e=2)
                sv = srow[0:1, b * 64:(b + 1) * 64].rearrange("p (i e) -> p i e", e=2)
                ve.tensor_copy(cv[:, :, e], rr[0:1, 0:32])
                ve.tensor_copy(sv[:, :, e], rr[0:1, 32:64])
        ve.tensor_mul(ssrow[:], srow[:], signp[:])
        sDV.inc(ve.drain(), 1)
        wdv(sy)
        sST.inc(sy.dma_start(out=rowb[None, 0:128], in_=crow[:]), 16)
        sST.inc(sy.dma_start(out=rowb[None, 128:256], in_=ssrow[:]), 16)
        wst(sy)
        sLD.inc(sy.dma_start(out=cbt[:], in_=bass.AP(rowb.tensor, 0, [[0, 128], [1, 128]]),
                             allow_slow_non_contiguous=True), 16)
        sLD.inc(sy.dma_start(out=ssbt[:], in_=bass.AP(rowb.tensor, 128, [[0, 128], [1, 128]]),
                             allow_slow_non_contiguous=True), 16)

        wld(ve)
        for off in (OBQ, OBK):
            for ch in range(8):
                s0 = brow_f[0:1, off + ch * 128: off + (ch + 1) * 128]
                t0 = btmp[0:1, ch * 128:(ch + 1) * 128]
                sv = s0.rearrange("p (i e) -> p i e", e=2)
                tv = t0.rearrange("p (i e) -> p i e", e=2)
                ssv = ssrow[0:1, :].rearrange("p (i e) -> p i e", e=2)
                ve.tensor_tensor(tv[:, :, 0], sv[:, :, 1], ssv[:, :, 0], ALU.mult)
                ve.tensor_tensor(tv[:, :, 1], sv[:, :, 0], ssv[:, :, 1], ALU.mult)
                ve.tensor_tensor(s0, s0, crow[0:1, :], ALU.mult)
                ve.tensor_tensor(s0, s0, t0, ALU.add)
        SPLITS = ((b1h, b1l, OB1, HF), (b2h, b2l, OB2, D),
                  (bvh, bvl, OBV, D), (boh, bol, OBO, D),
                  (bqh, bql, OBQ, D), (bkh, bkl, OBK, D))
        for hi, lo, off, ln in SPLITS:
            ve.tensor_copy(hi[:], brow_f[0:1, off:off + ln])
        sDV.inc(ve.drain(), 1)
        wdv(ve)
        for hi, lo, off, ln in SPLITS:
            ve.tensor_tensor(brow_f[0:1, off:off + ln],
                             brow_f[0:1, off:off + ln], hi[:], ALU.subtract)
        sDV.inc(ve.drain(), 1)
        wdv(ve)
        for hi, lo, off, ln in SPLITS:
            ve.tensor_copy(lo[:], brow_f[0:1, off:off + ln])
        sDV.inc(ve.drain(), 1)

        for wsrc, wdst, roped in ((wqT, wqRT, True), (wkT, wkRT, True),
                                  (wvT, wvTb, False)):
            for j in range(8):
                wdv(sy)
                sLD.inc(sy.dma_start(out=wld_t[:], in_=wsrc[j * 128:(j + 1) * 128, :]), 16)
                wld(ve)
                if roped:
                    for ch in range(8):
                        s0 = wld_t[:, ch * 128:(ch + 1) * 128]
                        sv = s0.rearrange("p (i e) -> p i e", e=2)
                        t1 = wtmp[:, 0:128]
                        t2 = wtmp[:, 128:256]
                        t2v = t2.rearrange("p (i e) -> p i e", e=2)
                        ssv = ssbt[:].rearrange("p (i e) -> p i e", e=2)
                        ve.tensor_tensor(t2v[:, :, 0], sv[:, :, 1], ssv[:, :, 0], ALU.mult)
                        ve.tensor_tensor(t2v[:, :, 1], sv[:, :, 0], ssv[:, :, 1], ALU.mult)
                        ve.tensor_tensor(t1, s0, cbt[:], ALU.mult)
                        ve.tensor_tensor(wdst[:, j * D + ch * 128: j * D + (ch + 1) * 128],
                                         t1, t2, ALU.add)
                else:
                    ve.tensor_copy(wdst[:, j * D:(j + 1) * D], wld_t[:])
                sDV.inc(ve.drain(), 1)

        if STOP_AFTER == "prep3":
            if KDBG:
                sST.inc(sy.dma_start(out=dbg_pp[0:1, 0:3072], in_=brow_f[0:1, HF:HF + 3 * D]), 16)
                sST.inc(sy.dma_start(out=dbg_pp[1:2, 0:128], in_=crow[:]), 16)
                sST.inc(sy.dma_start(out=dbg_pp[1:2, 128:256], in_=ssrow[:]), 16)
                sST.inc(sy.dma_start(out=dbg_pp[2:34, 0:128], in_=cbt[0:32, :]), 16)
                wdv(ve)

                ve.tensor_copy(brow_f[0:1, 0:1024], bass.AP(wqRT, 0, [[8192, 1], [1, 1024]]))
                sDV.inc(ve.drain(), 1)
                wdv(sy)
                sST.inc(sy.dma_start(out=dbg_pp[40:41, 0:1024], in_=brow_f[0:1, 0:1024]), 16)
                sy.wait_ge(sST.h, sST.v)
            pctx.__exit__(None, None, None)
            actx.__exit__(None, None, None)
            return
        for slab in range(8):
            wst(sy)
            wdv(sy)
            for j in range(8):
                sLD.inc(sy.dma_start(
                    out=wsl[:, j * 512:(j + 1) * 512],
                    in_=w1T[j * 128:(j + 1) * 128, slab * 512:(slab + 1) * 512]), 16)
            wld(sc)
            wdv(te)
            rec_pe_ab = [sPE.v, sPE.v]
            rec_ac_ab = {}
            for u in range(32):
                hs, j = u // 8, u % 8
                par = u % 2
                sc.wait_ge(sPE.h, rec_pe_ab[par])
                sc.activation(wab[:, par * 128:(par + 1) * 128],
                              wsl[:, j * 512 + hs * 128: j * 512 + (hs + 1) * 128], AF.Abs)
                rec_ac_ab[u] = sAC.inc(sc.drain(), 1)
                te.wait_ge(sAC.h, rec_ac_ab[u])
                te.matmul(psP2[:, hs * 64:(hs + 1) * 64], wab[:, par * 128:(par + 1) * 128],
                          ones64b[:], start=(u == 0), stop=(u == 31),
                          skip_group_check=True)
                rec_pe_ab[par] = sPE.inc(te.drain(), 1)
            wpe(sc)
            sc.activation(alc[:], psP2[:, 0:256].rearrange("p (h s) -> p h s", h=4)[:, :, 0],
                          AF.Identity, scale=np.float32(1.0 / 1024.0))
            sAC.inc(sc.drain(), 1)
            wac(sy)
            sST.inc(sy.dma_start(
                out=bass.AP(al1d.tensor, slab * 512, [[1, 128], [128, 4]]),
                in_=alc[:], allow_slow_non_contiguous=True), 16)
            wst(sy)
            sLD.inc(sy.dma_start(
                out=abca[:], in_=bass.AP(al1d.tensor, slab * 512, [[0, 128], [1, 512]]),
                allow_slow_non_contiguous=True), 16)
            wst(sc)
            for j in range(8):
                sc.sign(wsg[:, j * 512:(j + 1) * 512], wsl[:, j * 512:(j + 1) * 512])
            sAC.inc(sc.drain(), 1)
            wac(ve)
            wld(ve)
            for j in range(8):
                ve.tensor_tensor(wsg[:, j * 512:(j + 1) * 512],
                                 wsg[:, j * 512:(j + 1) * 512], abca[:], ALU.mult)
            sDV.inc(ve.drain(), 1)
            wdv(sy)
            for j in range(8):
                sST.inc(sy.dma_start(
                    out=bw1d[j * 128:(j + 1) * 128, slab * 512:(slab + 1) * 512],
                    in_=wsg[:, j * 512:(j + 1) * 512]), 16)

        for slab in range(2):
            for half in range(4):
                wst(sy)
                wdv(sy)
                for j in range(8):
                    hc = half * 8 + j
                    sLD.inc(sy.dma_start(
                        out=wsl[:, j * 512:(j + 1) * 512],
                        in_=w2T[hc * 128:(hc + 1) * 128, slab * 512:(slab + 1) * 512]), 16)
                wld(sc)
                wdv(te)
                rec_pe_ab = [sPE.v, sPE.v]
                rec_ac_ab = {}
                for u in range(32):
                    ds_, j = u // 8, u % 8
                    par = u % 2
                    sc.wait_ge(sPE.h, rec_pe_ab[par])
                    sc.activation(wab[:, par * 128:(par + 1) * 128],
                                  wsl[:, j * 512 + ds_ * 128: j * 512 + (ds_ + 1) * 128],
                                  AF.Abs)
                    rec_ac_ab[u] = sAC.inc(sc.drain(), 1)
                    te.wait_ge(sAC.h, rec_ac_ab[u])
                    te.matmul(psP2[:, ds_ * 64:(ds_ + 1) * 64],
                              wab[:, par * 128:(par + 1) * 128], ones64b[:],
                              start=(half == 0 and u == 0),
                              stop=(half == 3 and u == 31),
                              skip_group_check=True)
                    rec_pe_ab[par] = sPE.inc(te.drain(), 1)
                wst(sc)
                for j in range(8):
                    sc.sign(wsg[:, j * 512:(j + 1) * 512], wsl[:, j * 512:(j + 1) * 512])
                sAC.inc(sc.drain(), 1)
                wac(sy)
                for j in range(8):
                    hc = half * 8 + j
                    sST.inc(sy.dma_start(
                        out=bw2d[hc * 128:(hc + 1) * 128, slab * 512:(slab + 1) * 512],
                        in_=wsg[:, j * 512:(j + 1) * 512]), 16)
            wpe(sc)
            sc.activation(alc[:], psP2[:, 0:256].rearrange("p (h s) -> p h s", h=4)[:, :, 0],
                          AF.Identity, scale=np.float32(1.0 / 4096.0))
            sAC.inc(sc.drain(), 1)
            wac(sy)
            sST.inc(sy.dma_start(
                out=bass.AP(al2d.tensor, slab * 512, [[1, 128], [128, 4]]),
                in_=alc[:], allow_slow_non_contiguous=True), 16)
            wst(sy)
            sLD.inc(sy.dma_start(
                out=abca[:], in_=bass.AP(al2d.tensor, slab * 512, [[0, 128], [1, 512]]),
                allow_slow_non_contiguous=True), 16)
            for half in range(4):
                wst(sy)
                wdv(sy)
                for j in range(8):
                    hc = half * 8 + j
                    sLD.inc(sy.dma_start(
                        out=wsg[:, j * 512:(j + 1) * 512],
                        in_=bw2d[hc * 128:(hc + 1) * 128, slab * 512:(slab + 1) * 512]), 16)
                wld(ve)
                for j in range(8):
                    ve.tensor_tensor(wsg[:, j * 512:(j + 1) * 512],
                                     wsg[:, j * 512:(j + 1) * 512], abca[:], ALU.mult)
                sDV.inc(ve.drain(), 1)
                wdv(sy)
                for j in range(8):
                    hc = half * 8 + j
                    sST.inc(sy.dma_start(
                        out=bw2d[hc * 128:(hc + 1) * 128, slab * 512:(slab + 1) * 512],
                        in_=wsg[:, j * 512:(j + 1) * 512]), 16)
        wst(sy)
        barrier()
        pctx.__exit__(None, None, None)
        if STOP_AFTER == "prep":
            if KDBG:
                wst(sy)
                sLD.inc(sy.dma_start(out=wab[:, 0:512], in_=bw1d[0:128, 0:512]), 16)
                wld(ve)
                ve.tensor_copy(wsl[:, 0:512], wab[:, 0:512])
                sDV.inc(ve.drain(), 1)
                wdv(sy)
                sST.inc(sy.dma_start(out=dbg_pp[0:128, 1024:1536], in_=wsl[:, 0:512]), 16)
                wst(sy)
                sLD.inc(sy.dma_start(out=wab[:, 0:512], in_=bw2d[384:512, 512:1024]), 16)
                wld(ve)
                ve.tensor_copy(wsl[:, 0:512], wab[:, 0:512])
                sDV.inc(ve.drain(), 1)
                wdv(sy)
                sST.inc(sy.dma_start(out=dbg_pp[0:128, 1536:2048], in_=wsl[:, 0:512]), 16)
                sy.wait_ge(sST.h, sST.v)
            pctx.__exit__(None, None, None)
            actx.__exit__(None, None, None)
            return

        def make_ws(ctx, tag):
            w = {}
            w["xb"] = mksb(ctx, f"xb{tag}", [128, 4 * D], F32)
            w["sq"] = mksb(ctx, f"sq{tag}", [128, D], F32)
            w["st_s"] = mksb(ctx, f"sts{tag}", [128, 4], F32)
            w["st_q"] = mksb(ctx, f"stq{tag}", [128, 4], F32)
            w["mean4"] = mksb(ctx, f"mean{tag}", [128, 4], F32)
            w["var4"] = mksb(ctx, f"var{tag}", [128, 4], F32)
            w["rstd4"] = mksb(ctx, f"rstd{tag}", [128, 4], F32)
            w["nmr4"] = mksb(ctx, f"nmr{tag}", [128, 4], F32)
            w["nt1"] = mksb(ctx, f"nt1{tag}", [128, 4], F32)
            w["nt2"] = mksb(ctx, f"nt2{tag}", [128, 4], F32)
            w["xn_tm"] = mksb(ctx, f"xntm{tag}", [128, 4 * D], BF16)
            w["xn_fm"] = mksb(ctx, f"xnfm{tag}", [128, 8 * 512], BF16)
            w["h_fm"] = mksb(ctx, f"hfm{tag}", [128, 32 * 512], BF16)
            w["w1s"] = mksb(ctx, f"w1s{tag}", [128, 2, 8 * 512], BF16)
            w["w2s"] = mksb(ctx, f"w2s{tag}", [128, 2, 512], BF16)
            w["psT"] = mkps(ctx, f"psT{tag}", [128, 1024], BF16)
            w["psH"] = [mkps(ctx, f"psH{tag}a", [128, 512], F32),
                        mkps(ctx, f"psH{tag}b", [128, 512], F32)]
            w["psY"] = [mkps(ctx, f"psY{tag}{i}", [128, 512], F32) for i in range(4)]
            return w

        def emit_ln(w, src, dst):
            for t in range(4):
                xt = src[:, t * D:(t + 1) * D]
                ve.reduce_sum(w["st_s"][:, t:t + 1], xt, axis=AX)
                wdv(sc)
                sc.square(w["sq"][:], xt)
                sAC.inc(sc.drain(), 1)
                wac(ve)
                ve.reduce_sum(w["st_q"][:, t:t + 1], w["sq"][:], axis=AX)
                sDV.inc(ve.drain(), 1)
            ve.tensor_scalar_mul(w["mean4"][:], w["st_s"][:], 1.0 / D)
            sDV.inc(ve.drain(), 1)
            wdv(sc)
            sc.square(w["nt1"][:], w["mean4"][:])
            sAC.inc(sc.drain(), 1)
            wac(ve)
            ve.scalar_tensor_tensor(w["var4"][:], w["st_q"][:], 1.0 / D, w["nt1"][:],
                                    ALU.mult, ALU.subtract)
            sDV.inc(ve.drain(), 1)
            wdv(sc)
            sc.activation(w["rstd4"][:], w["var4"][:], AF.Sqrt, bias=np.float32(EPS))
            sAC.inc(sc.drain(), 1)
            wac(ve)
            ve.reciprocal(w["nt2"][:], w["rstd4"][:])
            sDV.inc(ve.drain(), 1)
            wdv(ve)
            ve.scalar_tensor_tensor(w["nmr4"][:], w["mean4"][:], -1.0, w["nt2"][:],
                                    ALU.mult, ALU.mult)
            sDV.inc(ve.drain(), 1)
            wdv(ve)
            wpe(ve)
            for t in range(4):
                ve.tensor_scalar(dst[:, t * D:(t + 1) * D], src[:, t * D:(t + 1) * D],
                                 w["nt2"][:, t:t + 1], w["nmr4"][:, t:t + 1],
                                 ALU.mult, ALU.add)
            sDV.inc(ve.drain(), 1)

        def emit_tp(w, src, dst, ntile=4):
            for t in range(ntile):
                wdv(te)
                for j in range(8):
                    te.transpose(w["psT"][:, j * 128:(j + 1) * 128],
                                 src[:, t * D + j * 128: t * D + (j + 1) * 128],
                                 identb[:])
                sPE.inc(te.drain(), 1)
                wpe(ve)
                dv = dst[:].rearrange("p (j n) -> p j n", j=8)
                sv = w["psT"][:].rearrange("p (j n) -> p j n", j=8)
                ve.tensor_copy(dv[:, :, t * 128:(t + 1) * 128], sv[:, :, :])
                sDV.inc(ve.drain(), 1)

        def emit_ffn(w, xres, yout):
            rec_pe_w1 = [0, 0]
            rec_ac_h = [0, 0]
            rec_sc_w1 = {}

            def load_w1(hg_):
                par_ = hg_ % 2
                sc.wait_ge(sPE.h, rec_pe_w1[par_])
                for j in range(8):
                    sSC.inc(sc.dma_start(
                        out=w["w1s"][:, par_, j * 512:(j + 1) * 512],
                        in_=bw1d[j * 128:(j + 1) * 128, hg_ * 512:(hg_ + 1) * 512]), 16)
                rec_sc_w1[hg_] = sSC.v

            wdv(te)
            load_w1(0)
            for hg in range(8):
                par = hg % 2
                if hg + 1 < 8:
                    load_w1(hg + 1)
                te.wait_ge(sSC.h, rec_sc_w1[hg])
                for ht in range(4):
                    hpar = ht % 2
                    hidx = hg * 4 + ht
                    te.wait_ge(sAC.h, rec_ac_h[hpar])
                    for j in range(8):
                        te.matmul(w["psH"][hpar][:],
                                  w["w1s"][:, par, j * 512 + ht * 128: j * 512 + (ht + 1) * 128],
                                  w["xn_fm"][:, j * 512:(j + 1) * 512],
                                  start=(j == 0), stop=False, skip_group_check=True)
                    te.matmul(w["psH"][hpar][:], b1h[0:1, hidx * 128:(hidx + 1) * 128],
                              onesb[0:1, :], start=False, stop=False, skip_group_check=True)
                    te.matmul(w["psH"][hpar][:], b1l[0:1, hidx * 128:(hidx + 1) * 128],
                              onesb[0:1, :], start=False, stop=True, skip_group_check=True)
                    sPE.inc(te.drain(), 1)
                    wpe(sc)
                    sc.activation(w["h_fm"][:, hidx * 512:(hidx + 1) * 512],
                                  w["psH"][hpar][:], AF.Gelu)
                    rec_ac_h[hpar] = sAC.inc(sc.drain(), 1)
                rec_pe_w1[par] = sPE.v
            wac(te)
            rec_pe_w2 = [0, 0]
            rec_sc_w2 = {}

            def load_w2(idx_):
                dh_, hc_ = idx_ // 32, idx_ % 32
                par_ = idx_ % 2
                sc.wait_ge(sPE.h, rec_pe_w2[par_])
                sSC.inc(sc.dma_start(
                    out=w["w2s"][:, par_, :],
                    in_=bw2d[hc_ * 128:(hc_ + 1) * 128, dh_ * 512:(dh_ + 1) * 512]), 16)
                rec_sc_w2[idx_] = sSC.v

            load_w2(0)
            for dh in range(2):
                for hc in range(32):
                    idx = dh * 32 + hc
                    par = idx % 2
                    if idx + 1 < 64:
                        load_w2(idx + 1)
                    te.wait_ge(sSC.h, rec_sc_w2[idx])
                    for tt in range(4):
                        te.matmul(w["psY"][tt][:],
                                  w["h_fm"][:, hc * 512 + tt * 128: hc * 512 + (tt + 1) * 128],
                                  w["w2s"][:, par, :],
                                  start=(hc == 0), stop=False, skip_group_check=True)
                    rec_pe_w2[par] = sPE.inc(te.drain(), 1)
                for tt in range(4):
                    te.matmul(w["psY"][tt][:], onesb[0:1, 0:128],
                              b2h[0:1, dh * 512:(dh + 1) * 512],
                              start=False, stop=False, skip_group_check=True)
                    te.matmul(w["psY"][tt][:], onesb[0:1, 0:128],
                              b2l[0:1, dh * 512:(dh + 1) * 512],
                              start=False, stop=True, skip_group_check=True)
                sPE.inc(te.drain(), 1)
                wpe(ve)
                for tt in range(4):
                    ve.scalar_tensor_tensor(
                        yout[:, tt * D + dh * 512: tt * D + (dh + 1) * 512],
                        w["psY"][tt][:], 0.5,
                        xres[:, tt * D + dh * 512: tt * D + (dh + 1) * 512],
                        ALU.mult, ALU.add)
                sDV.inc(ve.drain(), 1)
                wdv(te)

        wsA = make_ws(actx, "A")
        x1b = mksb(actx, "x1bA", [128, 4 * D], F32)
        kstage = mksb(actx, "kstA", [128, 2, 512], BF16)
        vstage = mksb(actx, "vstA", [128, 2, 520], BF16)
        qin = mksb(actx, "qinA", [128, D], BF16)
        qin_fm = mksb(actx, "qinfA", [128, 8 * 128], BF16)
        qstage = mksb(actx, "qstA", [128, 8 * 128], BF16)

        for bat in range(NB):
            wdv(sy)
            wac(sy)
            for t in range(4):
                sLD.inc(sy.dma_start(
                    out=wsA["xb"][:, t * D:(t + 1) * D],
                    in_=xin[bat * TB + t * 128: bat * TB + (t + 1) * 128, :]), 16)
            wld(ve)
            wld(sc)
            emit_ln(wsA, wsA["xb"], wsA["xn_tm"])
            emit_tp(wsA, wsA["xn_tm"], wsA["xn_fm"])
            if STOP_AFTER == "A1" and bat == 0:
                break
            emit_ffn(wsA, wsA["xb"], x1b)
            if STOP_AFTER == "A2" and bat == 0:
                break
            wdv(sy)
            for t in range(4):
                sST.inc(sy.dma_start(
                    out=x1d[bat * TB + t * 128: bat * TB + (t + 1) * 128, :],
                    in_=x1b[:, t * D:(t + 1) * D]), 16)
            emit_ln(wsA, x1b, wsA["xn_tm"])
            wdv(sy)
            for t in range(4):
                sST.inc(sy.dma_start(out=xnqd[t],
                                     in_=wsA["xn_tm"][:, t * D:(t + 1) * D]), 16)
            emit_tp(wsA, wsA["xn_tm"], wsA["xn_fm"])
            wdv(te)
            rec_ac_k = [sAC.v, sAC.v]
            rec_st_k = [sST.v, sST.v]
            for mt in range(8):
                par = mt % 2
                te.wait_ge(sAC.h, rec_ac_k[par])
                for j in range(8):
                    te.matmul(wsA["psH"][par][:],
                              wkRT[:, j * D + mt * 128: j * D + (mt + 1) * 128],
                              wsA["xn_fm"][:, j * 512:(j + 1) * 512],
                              start=(j == 0), stop=False, skip_group_check=True)
                te.matmul(wsA["psH"][par][:], bkh[0:1, mt * 128:(mt + 1) * 128],
                          onesb[0:1, :], start=False, stop=False, skip_group_check=True)
                te.matmul(wsA["psH"][par][:], bkl[0:1, mt * 128:(mt + 1) * 128],
                          onesb[0:1, :], start=False, stop=True, skip_group_check=True)
                sPE.inc(te.drain(), 1)
                wpe(sc)
                sc.wait_ge(sST.h, rec_st_k[par])
                sc.activation(kstage[:, par, :], wsA["psH"][par][:], AF.Identity)
                rec_ac_k[par] = sAC.inc(sc.drain(), 1)
                wac(sy)
                rec_st_k[par] = sST.inc(sy.dma_start(
                    out=kd[mt * 128:(mt + 1) * 128, bat * TB:(bat + 1) * TB],
                    in_=kstage[:, par, :]), 16)
            if STOP_AFTER == "A3" and bat == 0:
                break
            wac(te)
            rec_dv_v = [sDV.v, sDV.v]
            rec_st_v = [sST.v, sST.v]
            for tt in range(4):
                for mh in range(2):
                    par = (tt * 2 + mh) % 2
                    te.wait_ge(sDV.h, rec_dv_v[par])
                    for j in range(8):
                        te.matmul(wsA["psH"][par][:],
                                  wsA["xn_fm"][:, j * 512 + tt * 128: j * 512 + (tt + 1) * 128],
                                  wvTb[:, j * D + mh * 512: j * D + (mh + 1) * 512],
                                  start=(j == 0), stop=False, skip_group_check=True)
                    te.matmul(wsA["psH"][par][:], onesb[0:1, 0:128],
                              bvh[0:1, mh * 512:(mh + 1) * 512],
                              start=False, stop=False, skip_group_check=True)
                    te.matmul(wsA["psH"][par][:], onesb[0:1, 0:128],
                              bvl[0:1, mh * 512:(mh + 1) * 512],
                              start=False, stop=True, skip_group_check=True)
                    sPE.inc(te.drain(), 1)
                    gp.wait_ge(sST.h, rec_st_v[par])
                    gp.memset(vstage[:, par, :], 1.0)
                    sGP.inc(gp.drain(), 1)
                    wgp(ve)
                    wpe(ve)
                    for g in range(8):
                        ve.tensor_copy(vstage[:, par, g * 65:g * 65 + 64],
                                       wsA["psH"][par][:, g * 64:(g + 1) * 64])
                    rec_dv_v[par] = sDV.inc(ve.drain(), 1)
                    wdv(sy)
                    rec_st_v[par] = sST.inc(sy.dma_start(
                        out=vd[bat * 4 + tt, :, mh * 520:(mh + 1) * 520],
                        in_=vstage[:, par, :]), 16)
            if STOP_AFTER == "A4" and bat == 0:
                break
            wst(sy)
            sLD.inc(sy.dma_start(
                out=qin[:],
                in_=xnqd[ds(ccv, 1), :, :].rearrange("o p n -> (o p) n")), 16)
            wld(te)
            wdv(te)
            for j in range(8):
                te.transpose(wsA["psT"][:, j * 128:(j + 1) * 128],
                             qin[:, j * 128:(j + 1) * 128], identb[:])
            sPE.inc(te.drain(), 1)
            wpe(ve)
            ve.tensor_copy(qin_fm[:], wsA["psT"][:])
            sDV.inc(ve.drain(), 1)
            wdv(te)
            wst(sc)
            rec_ac_q = [sAC.v, sAC.v]
            for mt in range(8):
                par = mt % 2
                te.wait_ge(sAC.h, rec_ac_q[par])
                for j in range(8):
                    te.matmul(wsA["psH"][par][:, 0:128],
                              wqRT[:, j * D + mt * 128: j * D + (mt + 1) * 128],
                              qin_fm[:, j * 128:(j + 1) * 128],
                              start=(j == 0), stop=False, skip_group_check=True)
                te.matmul(wsA["psH"][par][:, 0:128], bqh[0:1, mt * 128:(mt + 1) * 128],
                          onesb[0:1, 0:128], start=False, stop=False, skip_group_check=True)
                te.matmul(wsA["psH"][par][:, 0:128], bql[0:1, mt * 128:(mt + 1) * 128],
                          onesb[0:1, 0:128], start=False, stop=True, skip_group_check=True)
                sPE.inc(te.drain(), 1)
                wpe(sc)
                sc.activation(qstage[:, mt * 128:(mt + 1) * 128],
                              wsA["psH"][par][:, 0:128], AF.Identity)
                rec_ac_q[par] = sAC.inc(sc.drain(), 1)
            wac(sy)
            sST.inc(sy.dma_start(out=qd[bat], in_=qstage[:]), 16)
            wst(sy)
            if STOP_AFTER == "A5" and bat == 0:
                break
        wst(sy)
        barrier()
        if KDBG and STOP_AFTER == "A2":
            for t in range(4):
                sST.inc(sy.dma_start(out=dbg_x1[t * 128:(t + 1) * 128, :],
                                     in_=x1b[:, t * D:(t + 1) * D]), 16)
            sST.inc(sy.dma_start(out=dbg_k, in_=wsA["h_fm"][:, 0:512]), 16)
            sST.inc(sy.dma_start(out=dbg_q, in_=wsA["h_fm"][:, 512:1536]), 16)
            wst(sy)
        if KDBG and STOP_AFTER == "A1":
            for t in range(4):
                sST.inc(sy.dma_start(out=dbg_x1[t * 128:(t + 1) * 128, :],
                                     in_=wsA["xb"][:, t * D:(t + 1) * D]), 16)
            ve.tensor_copy(x1b[:, 0:D], wsA["xn_tm"][:, 0:D])
            ve.tensor_copy(x1b[:, D:2 * D], wsA["xn_fm"][:, 0:D])
            sDV.inc(ve.drain(), 1)
            wdv(sy)
            sST.inc(sy.dma_start(out=dbg_pp[:, 0:1024], in_=x1b[:, 0:D]), 16)
            sST.inc(sy.dma_start(out=dbg_pp[:, 1024:2048], in_=x1b[:, D:2 * D]), 16)
            sST.inc(sy.dma_start(out=dbg_pp[:, 2048:2052], in_=wsA["st_s"][:]), 16)
            sST.inc(sy.dma_start(out=dbg_pp[:, 2052:2056], in_=wsA["st_q"][:]), 16)
            sST.inc(sy.dma_start(out=dbg_pp[:, 2056:2060], in_=wsA["rstd4"][:]), 16)
            sST.inc(sy.dma_start(out=dbg_pp[:, 2060:2064], in_=wsA["nmr4"][:]), 16)
            wst(sy)
        if KDBG and STOP_AFTER not in ("A1", "A2"):
            for t in range(4):
                sLD.inc(sy.dma_start(out=wsA["xb"][:, t * D:(t + 1) * D],
                                     in_=x1d[t * 128:(t + 1) * 128, :]), 16)
            wld(sy)
            for t in range(4):
                sST.inc(sy.dma_start(out=dbg_x1[t * 128:(t + 1) * 128, :],
                                     in_=wsA["xb"][:, t * D:(t + 1) * D]), 16)
            sLD.inc(sy.dma_start(out=kstage[:, 0, :], in_=kd[0:128, 0:512]), 16)
            sLD.inc(sy.dma_start(out=qstage[:], in_=qd[0]), 16)
            sLD.inc(sy.dma_start(out=vstage[:, 0, 0:520], in_=vd[0][:, 0:520]), 16)
            sLD.inc(sy.dma_start(out=vstage[:, 1, 0:520], in_=vd[0][:, 520:1040]), 16)
            wld(sy)
            sST.inc(sy.dma_start(out=dbg_k, in_=kstage[:, 0, :]), 16)
            sST.inc(sy.dma_start(out=dbg_q, in_=qstage[:]), 16)
            sST.inc(sy.dma_start(out=dbg_v[:, 0:520], in_=vstage[:, 0, 0:520]), 16)
            sST.inc(sy.dma_start(out=dbg_v[:, 520:1040], in_=vstage[:, 1, 0:520]), 16)
            wst(sy)
            barrier()
        actx.__exit__(None, None, None)
        if STOP_AFTER in ("A", "A1", "A2", "A3", "A4", "A5"):
            return

        bctx = ExitStack()
        bctx.__enter__()
        o_fm = mksb(bctx, "o_fm", [128, 8 * 2048], BF16)
        battn = ExitStack()
        battn.__enter__()
        ksl = mksb(battn, "ksl", [128, 2, 8 * 512], BF16)
        vsl = mksb(battn, "vsl", [128, 2, 4 * 1040], BF16)
        qsl = mksb(battn, "qsl", [128, 2, 1024], BF16)
        qdup = mksb(battn, "qdup", [128, 2, 512], BF16)
        expS = mksb(battn, "expS", [128, 2, 1024], BF16)
        recq = mksb(battn, "recq", [1, 512], F32)
        rech = mksb(battn, "rech", [1, 512], BF16)
        bcst = mksb(battn, "bcst", [64, 512], F32)
        psS = mkps(battn, "psS", [128, 2048], F32)
        dbgs = mksb(battn, "dbgsB", [128, 1024], F32) if KDBG else None
        psO = [mkps(battn, "psOa", [128, 512], F32), mkps(battn, "psOb", [128, 512], F32)]
        psB = mkps(battn, "psB", [64, 512], F32)

        for tb_ in (ksl, vsl, qsl, qdup, expS):
            gp.memset(tb_[:], 0.0)
        gp.memset(bcst[:], 0.0)
        sGP.inc(gp.drain(), 1)
        barrier()

        gp_head = []

        def load_head(hd, par):
            sGP.inc(sy.dma_start(out=qsl[:, par, :], in_=qd[hd]), 16)
            for mt in range(8):
                sGP.inc(sy.dma_start(
                    out=ksl[:, par, mt * 512:(mt + 1) * 512],
                    in_=kd[mt * 128:(mt + 1) * 128, hd * TB:(hd + 1) * TB]), 16)
            for tt in range(4):
                sGP.inc(sy.dma_start(
                    out=vsl[:, par, tt * 1040:(tt + 1) * 1040],
                    in_=vd[hd * 4 + tt]), 16)
            gp_head.append(sGP.v)

        rec_pe_head_end = [0, 0]
        rec_dv_norm = [0, 0]
        load_head(0, 0)
        for hd in range(NH):
            par = hd % 2
            if hd + 1 < NH:
                sy.wait_ge(sPE.h, rec_pe_head_end[(hd + 1) % 2])
                load_head(hd + 1, (hd + 1) % 2)
            te.wait_ge(sGP.h, gp_head[hd])
            ve.wait_ge(sGP.h, gp_head[hd])
            for G in range(4):
                gpar = G % 2
                wpe(ve)
                for gl in range(4):
                    g = G * 4 + gl
                    srcq = qsl[(g % 2) * 64:(g % 2) * 64 + 64, par,
                               (g // 2) * 128:(g // 2 + 1) * 128]
                    ve.tensor_copy(qdup[0:64, gpar, gl * 128:(gl + 1) * 128], srcq)
                    ve.tensor_copy(qdup[64:128, gpar, gl * 128:(gl + 1) * 128], srcq)
                sDV.inc(ve.drain(), 1)
                wdv(te)
                rec_ac_exp = {}
                rec_pe_pv = {}
                for pj in range(32):
                    mt, ttp = pj // 4, pj % 4
                    p2 = pj % 2
                    if pj >= 2:
                        te.wait_ge(sAC.h, rec_ac_exp[pj - 2])
                    if pj == 0:
                        te.wait_ge(sDV.h, rec_dv_norm[gpar])
                    kslice = ksl[:, par, mt * 512 + ttp * 128: mt * 512 + (ttp + 1) * 128]
                    te.matmul(psS[:, p2 * 1024: p2 * 1024 + 512],
                              kslice[0:64, :], qdup[0:64, gpar, :],
                              start=True, stop=True, tile_position=(0, 0),
                              skip_group_check=True)
                    te.matmul(psS[:, p2 * 1024 + 512: p2 * 1024 + 1024],
                              kslice[64:128, :], qdup[64:128, gpar, :],
                              start=True, stop=True, tile_position=(64, 0),
                              skip_group_check=True)
                    pe_qk = sPE.inc(te.drain(), 1)
                    sc.wait_ge(sPE.h, pe_qk)
                    sc.activation(expS[:, p2, :], psS[:, p2 * 1024:(p2 + 1) * 1024],
                                  AF.Exp, bias=np.float32(-EXPC), scale=np.float32(0.125))
                    rec_ac_exp[pj] = sAC.inc(sc.drain(), 1)
                    te.wait_ge(sAC.h, rec_ac_exp[pj])
                    first = (pj == 0)
                    te.matmul(psO[gpar][0:65, :],
                              vsl[:, par, ttp * 1040 + (2 * mt) * 65:
                                        ttp * 1040 + (2 * mt) * 65 + 65],
                              expS[:, p2, 0:512],
                              start=first, stop=False, skip_group_check=True)
                    te.matmul(psO[gpar][0:65, :],
                              vsl[:, par, ttp * 1040 + (2 * mt + 1) * 65:
                                        ttp * 1040 + (2 * mt + 1) * 65 + 65],
                              expS[:, p2, 512:1024],
                              start=False, stop=(pj == 31), skip_group_check=True)
                    if pj == 31:
                        sPE.inc(te.drain(), 1)
                wpe(ve)
                ve.reciprocal(recq[:], psO[gpar][64:65, :])
                sDV.inc(ve.drain(), 1)
                wdv(ve)
                ve.tensor_copy(rech[:], recq[:])
                sDV.inc(ve.drain(), 1)
                wdv(te)
                te.matmul(psB[0:64, :], onesb[0:1, 0:64], rech[0:1, :],
                          start=True, stop=True, skip_group_check=True)
                sPE.inc(te.drain(), 1)
                wpe(sc)
                sc.wait_ge(sDV.h, rec_dv_norm[1 - gpar])
                sc.activation(bcst[0:64, :], psB[0:64, :], AF.Identity)
                sAC.inc(sc.drain(), 1)
                wac(ve)
                mt_o, basep = hd // 2, (hd % 2) * 64
                ve.tensor_tensor(
                    o_fm[basep:basep + 64,
                         mt_o * 2048 + G * 512: mt_o * 2048 + (G + 1) * 512],
                    psO[gpar][0:64, :], bcst[0:64, :], ALU.mult)
                if KDBG and hd == 0 and G == 0:
                    ve.tensor_copy(dbgs[0:65, 0:512], psO[gpar][0:65, :])
                    ve.tensor_copy(dbgs[0:64, 512:1024],
                                   o_fm[0:64, 0:512])
                rec_dv_norm[gpar] = sDV.inc(ve.drain(), 1)
            rec_pe_head_end[par] = sPE.v
        barrier()
        if KDBG:
            sST.inc(sy.dma_start(out=dbg_o, in_=o_fm[:, 0:2048]), 16)
            sST.inc(sy.dma_start(out=dbg_pp[0:65, 2560:3072], in_=dbgs[0:65, 0:512]), 16)
            sST.inc(sy.dma_start(out=dbg_pp[0:64, 3072:3584], in_=dbgs[0:64, 512:1024]), 16)
            sST.inc(sy.dma_start(out=dbg_bw1, in_=o_fm[0:128, 0:512]), 16)
            wst(sy)
        battn.__exit__(None, None, None)
        if STOP_AFTER == "B":
            bctx.__exit__(None, None, None)
            return

        cctx = ExitStack()
        cctx.__enter__()
        woTb = mksb(cctx, "woTb", [128, 8 * D], BF16)
        wsC = make_ws(cctx, "C")
        x1t = mksb(cctx, "x1tC", [128, D], F32)
        outb = mksb(cctx, "outbC", [128, 4 * D], F32)
        x2b = wsC["xb"]

        for j in range(8):
            wdv(sy)
            sLD.inc(sy.dma_start(out=x2b[:, 0:D], in_=woT[j * 128:(j + 1) * 128, :]), 16)
            wld(ve)
            ve.tensor_copy(woTb[:, j * D:(j + 1) * D], x2b[:, 0:D])
            sDV.inc(ve.drain(), 1)
        wdv(sy)

        for sbi in range(4):
            wac(ve)
            for gl in range(4):
                g = sbi * 4 + gl
                G, g_l = g // 4, g % 4
                wdv(sy)
                sLD.inc(sy.dma_start(
                    out=x1t[:],
                    in_=x1d.rearrange("(a b) d -> a b d", b=16)[ds(ccv128, 128), g, :]), 16)
                for jh in range(2):
                    par = jh % 2
                    wdv(te)
                    for mt in range(8):
                        te.matmul(wsC["psH"][par][:],
                                  o_fm[:, mt * 2048 + G * 512 + g_l * 128:
                                       mt * 2048 + G * 512 + (g_l + 1) * 128],
                                  woTb[:, mt * D + jh * 512: mt * D + (jh + 1) * 512],
                                  start=(mt == 0), stop=False, skip_group_check=True)
                    te.matmul(wsC["psH"][par][:], onesb[0:1, 0:128],
                              boh[0:1, jh * 512:(jh + 1) * 512],
                              start=False, stop=False, skip_group_check=True)
                    te.matmul(wsC["psH"][par][:], onesb[0:1, 0:128],
                              bol[0:1, jh * 512:(jh + 1) * 512],
                              start=False, stop=True, skip_group_check=True)
                    sPE.inc(te.drain(), 1)
                    wpe(ve)
                    wld(ve)
                    ve.tensor_add(x2b[:, gl * D + jh * 512: gl * D + (jh + 1) * 512],
                                  x1t[:, jh * 512:(jh + 1) * 512], wsC["psH"][par][:])
                    sDV.inc(ve.drain(), 1)
            wdv(sc)
            emit_ln(wsC, x2b, wsC["xn_tm"])
            emit_tp(wsC, wsC["xn_tm"], wsC["xn_fm"])
            wst(ve)
            emit_ffn(wsC, x2b, outb)
            wdv(sy)
            for gl in range(4):
                g = sbi * 4 + gl
                sST.inc(sy.dma_start(out=out[g * 128:(g + 1) * 128, :],
                                     in_=outb[:, gl * D:(gl + 1) * D]), 16)
        wst(sy)

        cctx.__exit__(None, None, None)
        bctx.__exit__(None, None, None)

    gctx.__exit__(None, None, None)
    return nc


_NC_CACHE = {}


def kernel(L0_attn_x, L0_attn_wq, L0_attn_bq, L0_attn_wk, L0_attn_bk, L0_attn_wv,
           L0_attn_bv, L0_attn_wo, L0_attn_bo, L0_attn_rot_cos, L0_attn_rot_sin,
           L0_ffn_w1, L0_ffn_b1, L0_ffn_w2, L0_ffn_b2):
    x = np.asarray(L0_attn_x, dtype=np.float32)
    if "nc" not in _NC_CACHE:
        _NC_CACHE["nc"] = build_nc()
    nc = _NC_CACHE["nc"]

    f32 = np.float32
    common = {
        "w1T": np.ascontiguousarray(np.asarray(L0_ffn_w1, f32).T),
        "w2T": np.ascontiguousarray(np.asarray(L0_ffn_w2, f32).T),
        "wqT": np.ascontiguousarray(np.asarray(L0_attn_wq, f32).T),
        "wkT": np.ascontiguousarray(np.asarray(L0_attn_wk, f32).T),
        "wvT": np.ascontiguousarray(np.asarray(L0_attn_wv, f32).T),
        "woT": np.ascontiguousarray(np.asarray(L0_attn_wo, f32).T),
        "b1": np.ascontiguousarray(np.asarray(L0_ffn_b1, f32)),
        "b2": np.ascontiguousarray(np.asarray(L0_ffn_b2, f32)),
        "bq": np.ascontiguousarray(np.asarray(L0_attn_bq, f32)),
        "bk": np.ascontiguousarray(np.asarray(L0_attn_bk, f32)),
        "bv": np.ascontiguousarray(np.asarray(L0_attn_bv, f32)),
        "bo": np.ascontiguousarray(np.asarray(L0_attn_bo, f32)),
        "rotc": np.ascontiguousarray(np.asarray(L0_attn_rot_cos, f32)[4095]),
        "rots": np.ascontiguousarray(np.asarray(L0_attn_rot_sin, f32)[4095]),
    }
    in_maps = []
    for c in range(8):
        m = dict(common)
        m["xin"] = np.ascontiguousarray(x[c // 4])
        in_maps.append(m)

    res = run_bass_kernel_spmd(nc, in_maps, core_ids=list(range(8)))

    outp = np.empty((2, TFULL, D), np.float32)
    for c in range(8):
        bc, cc = c // 4, c % 4
        r = np.asarray(res.results[c]["out"]).reshape(16, 128, D)
        outp[bc, cc * 2048:(cc + 1) * 2048] = r.transpose(1, 0, 2).reshape(2048, D)
    return outp


# version 08
# speedup vs baseline: 1.4561x; 1.2582x over previous
import os
import numpy as np
import ml_dtypes
import concourse.bass as bass
import concourse.mybir as mybir
from concourse.bass import ds
from concourse.bass_utils import run_bass_kernel_spmd
from contextlib import ExitStack

F32 = mybir.dt.float32
BF16 = mybir.dt.bfloat16
AF = mybir.ActivationFunctionType
ALU = mybir.AluOpType
AX = mybir.AxisListType.X

D = 1024
HF = 4096
NH = 16
TFULL = 8192
NB = 16
TB = 512
EPS = 1e-5
EXPC = 12.0
CONSTS = [EPS, -EXPC, 0.125, 1.0 / 1024.0, 1.0 / 4096.0, 0.5]


STOP_AFTER = os.environ.get("KSTOP", "")
KDBG = os.environ.get("KDBG", "") == "1"



def build_nc():
    nc = bass.Bass("TRN2", target_bir_lowering=False, debug=False)

    xin = nc.dram_tensor("xin", [TFULL, D], F32, kind="ExternalInput").ap()
    w1T = nc.dram_tensor("w1T", [D, HF], F32, kind="ExternalInput").ap()
    w2T = nc.dram_tensor("w2T", [HF, D], F32, kind="ExternalInput").ap()
    wqT = nc.dram_tensor("wqT", [D, D], F32, kind="ExternalInput").ap()
    wkT = nc.dram_tensor("wkT", [D, D], F32, kind="ExternalInput").ap()
    wvT = nc.dram_tensor("wvT", [D, D], F32, kind="ExternalInput").ap()
    woT = nc.dram_tensor("woT", [D, D], F32, kind="ExternalInput").ap()
    b1i = nc.dram_tensor("b1", [HF], F32, kind="ExternalInput").ap()
    b2i = nc.dram_tensor("b2", [D], F32, kind="ExternalInput").ap()
    bqi = nc.dram_tensor("bq", [D], F32, kind="ExternalInput").ap()
    bki = nc.dram_tensor("bk", [D], F32, kind="ExternalInput").ap()
    bvi = nc.dram_tensor("bv", [D], F32, kind="ExternalInput").ap()
    boi = nc.dram_tensor("bo", [D], F32, kind="ExternalInput").ap()
    rci = nc.dram_tensor("rotc", [32], F32, kind="ExternalInput").ap()
    rsi = nc.dram_tensor("rots", [32], F32, kind="ExternalInput").ap()
    out = nc.dram_tensor("out", [2048, D], F32, kind="ExternalOutput").ap()
    if KDBG:
        dbg_pp = nc.dram_tensor("dbg_pp", [128, 4096], F32, kind="ExternalOutput").ap()
        dbg_bw1 = nc.dram_tensor("dbg_bw1", [128, 512], BF16, kind="ExternalOutput").ap()
        dbg_bw2 = nc.dram_tensor("dbg_bw2", [128, 512], BF16, kind="ExternalOutput").ap()
        dbg_x1 = nc.dram_tensor("dbg_x1", [512, D], F32, kind="ExternalOutput").ap()
        dbg_k = nc.dram_tensor("dbg_k", [128, 512], BF16, kind="ExternalOutput").ap()
        dbg_q = nc.dram_tensor("dbg_q", [128, 1024], BF16, kind="ExternalOutput").ap()
        dbg_v = nc.dram_tensor("dbg_v", [128, 1040], BF16, kind="ExternalOutput").ap()
        dbg_o = nc.dram_tensor("dbg_o", [128, 2048], BF16, kind="ExternalOutput").ap()

    bw1d = nc.dram_tensor("bw1d", [D, HF], BF16, kind="Internal").ap()
    bw2d = nc.dram_tensor("bw2d", [HF, D], BF16, kind="Internal").ap()
    kd = nc.dram_tensor("kd", [D, TFULL], BF16, kind="Internal").ap()
    vd = nc.dram_tensor("vd", [64, 128, 16 * 65], BF16, kind="Internal").ap()
    qd = nc.dram_tensor("qd", [NH, 128, 8 * 128], BF16, kind="Internal").ap()
    x1d = nc.dram_tensor("x1d", [TFULL, D], F32, kind="Internal").ap()
    xnqd = nc.dram_tensor("xnqd", [4, 128, D], BF16, kind="Internal").ap()
    rowb = nc.dram_tensor("rowb", [1024], F32, kind="Internal").ap()
    al1d = nc.dram_tensor("al1d", [HF], F32, kind="Internal").ap()
    al2d = nc.dram_tensor("al2d", [D], F32, kind="Internal").ap()
    recb = nc.dram_tensor("recb", [512], F32, kind="Internal").ap()

    gctx = ExitStack()
    gctx.__enter__()

    def mksb(ctx, name, shape, dt):
        return ctx.enter_context(nc.sbuf_tensor(name, shape, dt))

    def mkps(ctx, name, shape, dt):
        return ctx.enter_context(nc.psum_tensor(name, shape, dt))

    class Ctr:
        def __init__(self, name):
            self.h = gctx.enter_context(nc.semaphore(name))
            self.v = 0

        def inc(self, inst, n=1):
            inst.then_inc(self.h, n)
            self.v += n
            return self.v

    sLD = Ctr("sLD")
    sSC = Ctr("sSC")
    sGP = Ctr("sGP")
    sPE = Ctr("sPE")
    sAC = Ctr("sAC")
    sDV = Ctr("sDV")
    sST = Ctr("sST")
    sBR = Ctr("sBR")

    identb = mksb(gctx, "identb", [128, 128], BF16)
    onesb = mksb(gctx, "onesb", [1, 512], BF16)
    ones64b = mksb(gctx, "ones64b", [128, 64], BF16)
    b1h = mksb(gctx, "b1h", [1, HF], BF16)
    b1l = mksb(gctx, "b1l", [1, HF], BF16)
    b2h = mksb(gctx, "b2h", [1, D], BF16)
    b2l = mksb(gctx, "b2l", [1, D], BF16)
    bvh = mksb(gctx, "bvh", [1, D], BF16)
    bvl = mksb(gctx, "bvl", [1, D], BF16)
    boh = mksb(gctx, "boh", [1, D], BF16)
    bol = mksb(gctx, "bol", [1, D], BF16)
    bqh = mksb(gctx, "bqh", [1, D], BF16)
    bql = mksb(gctx, "bql", [1, D], BF16)
    bkh = mksb(gctx, "bkh", [1, D], BF16)
    bkl = mksb(gctx, "bkl", [1, D], BF16)

    blk = gctx.enter_context(nc.Block())

    @blk.sync
    def _(sync):
        te, ve, sc, gp, sy = nc.tensor, nc.vector, nc.scalar, nc.gpsimd, nc.sync
        pid = sy.partition_id()
        ccv = sy.compute_val(pid % 4)
        ccv128 = sy.compute_val(pid % 4 * 128)

        def wpe(e):
            e.wait_ge(sPE.h, sPE.v)

        def wac(e):
            e.wait_ge(sAC.h, sAC.v)

        def wdv(e):
            e.wait_ge(sDV.h, sDV.v)

        def wld(e):
            e.wait_ge(sLD.h, sLD.v)

        def wsc(e):
            e.wait_ge(sSC.h, sSC.v)

        def wst(e):
            e.wait_ge(sST.h, sST.v)

        def wgp(e):
            e.wait_ge(sGP.h, sGP.v)

        def barrier():
            for e in (te, ve, sc, gp, sy):
                sBR.inc(e.drain(), 1)
            tgt = sBR.v
            for e in (te, ve, sc, gp, sy):
                e.wait_ge(sBR.h, tgt)

        actx = ExitStack()
        actx.__enter__()
        wqRT = mksb(actx, "wqRT", [128, 8 * D], BF16)
        wkRT = mksb(actx, "wkRT", [128, 8 * D], BF16)
        wvTb = mksb(actx, "wvTb", [128, 8 * D], BF16)

        pctx = ExitStack()
        pctx.__enter__()
        crow = mksb(pctx, "crow", [1, 128], F32)
        srow = mksb(pctx, "srow", [1, 128], F32)
        ssrow = mksb(pctx, "ssrow", [1, 128], F32)
        signp = mksb(pctx, "signp", [1, 128], F32)
        cbt = mksb(pctx, "cbt", [128, 128], F32)
        ssbt = mksb(pctx, "ssbt", [128, 128], F32)
        brow_f = mksb(pctx, "brow_f", [1, 6 * D + HF], F32)
        btmp = mksb(pctx, "btmp", [1, 2 * D], F32)
        wld_t = mksb(pctx, "wld_t", [128, D], F32)
        wtmp = mksb(pctx, "wtmp", [128, 256], F32)
        wsl = mksb(pctx, "wsl", [128, 8 * 512], F32)
        wab = mksb(pctx, "wab", [128, 512], BF16)
        alc = mksb(pctx, "alc", [128, 4], F32)
        abca = mksb(pctx, "abca", [128, 512], F32)
        wsg = mksb(pctx, "wsg", [128, 8 * 512], BF16)
        psP2 = mkps(pctx, "psP2", [128, 512], F32)

        gp.memset(onesb[:], 1.0)
        gp.memset(ones64b[:], 1.0)
        gp.memset(identb[:], 0.0)
        gp.affine_select(out=identb[:], in_=identb[:], compare_op=ALU.not_equal,
                         fill=1.0, base=0, pattern=[[-1, 128]], channel_multiplier=1)
        gp.memset(signp[:], 1.0)
        sGP.inc(gp.drain(), 1)
        wgp(ve)
        spv = signp[:].rearrange("p (i e) -> p i e", e=2)
        ve.tensor_single_scalar(spv[:, :, 0], spv[:, :, 0], -1.0, ALU.mult)
        sDV.inc(ve.drain(), 1)
        for e in (te, ve, sc, sy):
            wgp(e)

        OB1, OB2, OBV, OBO, OBQ, OBK = 0, HF, HF + D, HF + 2 * D, HF + 3 * D, HF + 4 * D
        for off, t in ((OB1, b1i), (OB2, b2i), (OBV, bvi), (OBO, boi),
                       (OBQ, bqi), (OBK, bki)):
            sLD.inc(sy.dma_start(out=brow_f[0:1, off:off + t.shape[0]],
                                 in_=t[None, :]), 16)
        rr = btmp
        sLD.inc(sy.dma_start(out=rr[0:1, 0:32], in_=rci[None, :]), 16)
        sLD.inc(sy.dma_start(out=rr[0:1, 32:64], in_=rsi[None, :]), 16)
        wld(ve)
        for b in range(2):
            for e in range(2):
                cv = crow[0:1, b * 64:(b + 1) * 64].rearrange("p (i e) -> p i e", e=2)
                sv = srow[0:1, b * 64:(b + 1) * 64].rearrange("p (i e) -> p i e", e=2)
                ve.tensor_copy(cv[:, :, e], rr[0:1, 0:32])
                ve.tensor_copy(sv[:, :, e], rr[0:1, 32:64])
        ve.tensor_mul(ssrow[:], srow[:], signp[:])
        sDV.inc(ve.drain(), 1)
        wdv(sy)
        sST.inc(sy.dma_start(out=rowb[None, 0:128], in_=crow[:]), 16)
        sST.inc(sy.dma_start(out=rowb[None, 128:256], in_=ssrow[:]), 16)
        wst(sy)
        sLD.inc(sy.dma_start(out=cbt[:], in_=bass.AP(rowb.tensor, 0, [[0, 128], [1, 128]]),
                             allow_slow_non_contiguous=True), 16)
        sLD.inc(sy.dma_start(out=ssbt[:], in_=bass.AP(rowb.tensor, 128, [[0, 128], [1, 128]]),
                             allow_slow_non_contiguous=True), 16)

        wld(ve)
        for off in (OBQ, OBK):
            for ch in range(8):
                s0 = brow_f[0:1, off + ch * 128: off + (ch + 1) * 128]
                t0 = btmp[0:1, ch * 128:(ch + 1) * 128]
                sv = s0.rearrange("p (i e) -> p i e", e=2)
                tv = t0.rearrange("p (i e) -> p i e", e=2)
                ssv = ssrow[0:1, :].rearrange("p (i e) -> p i e", e=2)
                ve.tensor_tensor(tv[:, :, 0], sv[:, :, 1], ssv[:, :, 0], ALU.mult)
                ve.tensor_tensor(tv[:, :, 1], sv[:, :, 0], ssv[:, :, 1], ALU.mult)
                ve.tensor_tensor(s0, s0, crow[0:1, :], ALU.mult)
                ve.tensor_tensor(s0, s0, t0, ALU.add)
        SPLITS = ((b1h, b1l, OB1, HF), (b2h, b2l, OB2, D),
                  (bvh, bvl, OBV, D), (boh, bol, OBO, D),
                  (bqh, bql, OBQ, D), (bkh, bkl, OBK, D))
        for hi, lo, off, ln in SPLITS:
            ve.tensor_copy(hi[:], brow_f[0:1, off:off + ln])
        sDV.inc(ve.drain(), 1)
        wdv(ve)
        for hi, lo, off, ln in SPLITS:
            ve.tensor_tensor(brow_f[0:1, off:off + ln],
                             brow_f[0:1, off:off + ln], hi[:], ALU.subtract)
        sDV.inc(ve.drain(), 1)
        wdv(ve)
        for hi, lo, off, ln in SPLITS:
            ve.tensor_copy(lo[:], brow_f[0:1, off:off + ln])
        sDV.inc(ve.drain(), 1)

        for wsrc, wdst, roped in ((wqT, wqRT, True), (wkT, wkRT, True),
                                  (wvT, wvTb, False)):
            for j in range(8):
                wdv(sy)
                sLD.inc(sy.dma_start(out=wld_t[:], in_=wsrc[j * 128:(j + 1) * 128, :]), 16)
                wld(ve)
                if roped:
                    for ch in range(8):
                        s0 = wld_t[:, ch * 128:(ch + 1) * 128]
                        sv = s0.rearrange("p (i e) -> p i e", e=2)
                        t1 = wtmp[:, 0:128]
                        t2 = wtmp[:, 128:256]
                        t2v = t2.rearrange("p (i e) -> p i e", e=2)
                        ssv = ssbt[:].rearrange("p (i e) -> p i e", e=2)
                        ve.tensor_tensor(t2v[:, :, 0], sv[:, :, 1], ssv[:, :, 0], ALU.mult)
                        ve.tensor_tensor(t2v[:, :, 1], sv[:, :, 0], ssv[:, :, 1], ALU.mult)
                        ve.tensor_tensor(t1, s0, cbt[:], ALU.mult)
                        ve.tensor_tensor(wdst[:, j * D + ch * 128: j * D + (ch + 1) * 128],
                                         t1, t2, ALU.add)
                else:
                    ve.tensor_copy(wdst[:, j * D:(j + 1) * D], wld_t[:])
                sDV.inc(ve.drain(), 1)

        if STOP_AFTER == "prep3":
            if KDBG:
                sST.inc(sy.dma_start(out=dbg_pp[0:1, 0:3072], in_=brow_f[0:1, HF:HF + 3 * D]), 16)
                sST.inc(sy.dma_start(out=dbg_pp[1:2, 0:128], in_=crow[:]), 16)
                sST.inc(sy.dma_start(out=dbg_pp[1:2, 128:256], in_=ssrow[:]), 16)
                sST.inc(sy.dma_start(out=dbg_pp[2:34, 0:128], in_=cbt[0:32, :]), 16)
                wdv(ve)

                ve.tensor_copy(brow_f[0:1, 0:1024], bass.AP(wqRT, 0, [[8192, 1], [1, 1024]]))
                sDV.inc(ve.drain(), 1)
                wdv(sy)
                sST.inc(sy.dma_start(out=dbg_pp[40:41, 0:1024], in_=brow_f[0:1, 0:1024]), 16)
                sy.wait_ge(sST.h, sST.v)
            pctx.__exit__(None, None, None)
            actx.__exit__(None, None, None)
            return
        for slab in range(8):
            wst(sy)
            wdv(sy)
            for j in range(8):
                sLD.inc(sy.dma_start(
                    out=wsl[:, j * 512:(j + 1) * 512],
                    in_=w1T[j * 128:(j + 1) * 128, slab * 512:(slab + 1) * 512]), 16)
            wld(sc)
            wdv(te)
            rec_pe_ab = [sPE.v, sPE.v]
            rec_ac_ab = {}
            for u in range(32):
                hs, j = u // 8, u % 8
                par = u % 2
                sc.wait_ge(sPE.h, rec_pe_ab[par])
                sc.activation(wab[:, par * 128:(par + 1) * 128],
                              wsl[:, j * 512 + hs * 128: j * 512 + (hs + 1) * 128], AF.Abs)
                rec_ac_ab[u] = sAC.inc(sc.drain(), 1)
                te.wait_ge(sAC.h, rec_ac_ab[u])
                te.matmul(psP2[:, hs * 64:(hs + 1) * 64], wab[:, par * 128:(par + 1) * 128],
                          ones64b[:], start=(u == 0), stop=(u == 31),
                          skip_group_check=True)
                rec_pe_ab[par] = sPE.inc(te.drain(), 1)
            wpe(sc)
            sc.activation(alc[:], psP2[:, 0:256].rearrange("p (h s) -> p h s", h=4)[:, :, 0],
                          AF.Identity, scale=np.float32(1.0 / 1024.0))
            sAC.inc(sc.drain(), 1)
            wac(sy)
            sST.inc(sy.dma_start(
                out=bass.AP(al1d.tensor, slab * 512, [[1, 128], [128, 4]]),
                in_=alc[:], allow_slow_non_contiguous=True), 16)
            wst(sy)
            sLD.inc(sy.dma_start(
                out=abca[:], in_=bass.AP(al1d.tensor, slab * 512, [[0, 128], [1, 512]]),
                allow_slow_non_contiguous=True), 16)
            wst(sc)
            for j in range(8):
                sc.sign(wsg[:, j * 512:(j + 1) * 512], wsl[:, j * 512:(j + 1) * 512])
            sAC.inc(sc.drain(), 1)
            wac(ve)
            wld(ve)
            for j in range(8):
                ve.tensor_tensor(wsg[:, j * 512:(j + 1) * 512],
                                 wsg[:, j * 512:(j + 1) * 512], abca[:], ALU.mult)
            sDV.inc(ve.drain(), 1)
            wdv(sy)
            for j in range(8):
                sST.inc(sy.dma_start(
                    out=bw1d[j * 128:(j + 1) * 128, slab * 512:(slab + 1) * 512],
                    in_=wsg[:, j * 512:(j + 1) * 512]), 16)

        for slab in range(2):
            for half in range(4):
                wst(sy)
                wdv(sy)
                for j in range(8):
                    hc = half * 8 + j
                    sLD.inc(sy.dma_start(
                        out=wsl[:, j * 512:(j + 1) * 512],
                        in_=w2T[hc * 128:(hc + 1) * 128, slab * 512:(slab + 1) * 512]), 16)
                wld(sc)
                wdv(te)
                rec_pe_ab = [sPE.v, sPE.v]
                rec_ac_ab = {}
                for u in range(32):
                    ds_, j = u // 8, u % 8
                    par = u % 2
                    sc.wait_ge(sPE.h, rec_pe_ab[par])
                    sc.activation(wab[:, par * 128:(par + 1) * 128],
                                  wsl[:, j * 512 + ds_ * 128: j * 512 + (ds_ + 1) * 128],
                                  AF.Abs)
                    rec_ac_ab[u] = sAC.inc(sc.drain(), 1)
                    te.wait_ge(sAC.h, rec_ac_ab[u])
                    te.matmul(psP2[:, ds_ * 64:(ds_ + 1) * 64],
                              wab[:, par * 128:(par + 1) * 128], ones64b[:],
                              start=(half == 0 and u == 0),
                              stop=(half == 3 and u == 31),
                              skip_group_check=True)
                    rec_pe_ab[par] = sPE.inc(te.drain(), 1)
                wst(sc)
                for j in range(8):
                    sc.sign(wsg[:, j * 512:(j + 1) * 512], wsl[:, j * 512:(j + 1) * 512])
                sAC.inc(sc.drain(), 1)
                wac(sy)
                for j in range(8):
                    hc = half * 8 + j
                    sST.inc(sy.dma_start(
                        out=bw2d[hc * 128:(hc + 1) * 128, slab * 512:(slab + 1) * 512],
                        in_=wsg[:, j * 512:(j + 1) * 512]), 16)
            wpe(sc)
            sc.activation(alc[:], psP2[:, 0:256].rearrange("p (h s) -> p h s", h=4)[:, :, 0],
                          AF.Identity, scale=np.float32(1.0 / 4096.0))
            sAC.inc(sc.drain(), 1)
            wac(sy)
            sST.inc(sy.dma_start(
                out=bass.AP(al2d.tensor, slab * 512, [[1, 128], [128, 4]]),
                in_=alc[:], allow_slow_non_contiguous=True), 16)
            wst(sy)
            sLD.inc(sy.dma_start(
                out=abca[:], in_=bass.AP(al2d.tensor, slab * 512, [[0, 128], [1, 512]]),
                allow_slow_non_contiguous=True), 16)
            for half in range(4):
                wst(sy)
                wdv(sy)
                for j in range(8):
                    hc = half * 8 + j
                    sLD.inc(sy.dma_start(
                        out=wsg[:, j * 512:(j + 1) * 512],
                        in_=bw2d[hc * 128:(hc + 1) * 128, slab * 512:(slab + 1) * 512]), 16)
                wld(ve)
                for j in range(8):
                    ve.tensor_tensor(wsg[:, j * 512:(j + 1) * 512],
                                     wsg[:, j * 512:(j + 1) * 512], abca[:], ALU.mult)
                sDV.inc(ve.drain(), 1)
                wdv(sy)
                for j in range(8):
                    hc = half * 8 + j
                    sST.inc(sy.dma_start(
                        out=bw2d[hc * 128:(hc + 1) * 128, slab * 512:(slab + 1) * 512],
                        in_=wsg[:, j * 512:(j + 1) * 512]), 16)
        wst(sy)
        barrier()
        pctx.__exit__(None, None, None)
        if STOP_AFTER == "prep":
            if KDBG:
                wst(sy)
                sLD.inc(sy.dma_start(out=wab[:, 0:512], in_=bw1d[0:128, 0:512]), 16)
                wld(ve)
                ve.tensor_copy(wsl[:, 0:512], wab[:, 0:512])
                sDV.inc(ve.drain(), 1)
                wdv(sy)
                sST.inc(sy.dma_start(out=dbg_pp[0:128, 1024:1536], in_=wsl[:, 0:512]), 16)
                wst(sy)
                sLD.inc(sy.dma_start(out=wab[:, 0:512], in_=bw2d[384:512, 512:1024]), 16)
                wld(ve)
                ve.tensor_copy(wsl[:, 0:512], wab[:, 0:512])
                sDV.inc(ve.drain(), 1)
                wdv(sy)
                sST.inc(sy.dma_start(out=dbg_pp[0:128, 1536:2048], in_=wsl[:, 0:512]), 16)
                sy.wait_ge(sST.h, sST.v)
            pctx.__exit__(None, None, None)
            actx.__exit__(None, None, None)
            return

        def make_ws(ctx, tag):
            w = {}
            w["xb"] = mksb(ctx, f"xb{tag}", [128, 4 * D], F32)
            w["sq"] = mksb(ctx, f"sq{tag}", [128, D], F32)
            w["st_s"] = mksb(ctx, f"sts{tag}", [128, 4], F32)
            w["st_q"] = mksb(ctx, f"stq{tag}", [128, 4], F32)
            w["mean4"] = mksb(ctx, f"mean{tag}", [128, 4], F32)
            w["var4"] = mksb(ctx, f"var{tag}", [128, 4], F32)
            w["rstd4"] = mksb(ctx, f"rstd{tag}", [128, 4], F32)
            w["nmr4"] = mksb(ctx, f"nmr{tag}", [128, 4], F32)
            w["nt1"] = mksb(ctx, f"nt1{tag}", [128, 4], F32)
            w["nt2"] = mksb(ctx, f"nt2{tag}", [128, 4], F32)
            w["xn_tm"] = mksb(ctx, f"xntm{tag}", [128, 4 * D], BF16)
            w["xn_fm"] = mksb(ctx, f"xnfm{tag}", [128, 8 * 512], BF16)
            w["h_fm"] = mksb(ctx, f"hfm{tag}", [128, 32 * 512], BF16)
            w["w1s"] = mksb(ctx, f"w1s{tag}", [128, 2, 8 * 512], BF16)
            w["w2s"] = mksb(ctx, f"w2s{tag}", [128, 2, 512], BF16)
            w["psT"] = mkps(ctx, f"psT{tag}", [128, 1024], BF16)
            w["psH"] = [mkps(ctx, f"psH{tag}a", [128, 512], F32),
                        mkps(ctx, f"psH{tag}b", [128, 512], F32)]
            w["psY"] = [mkps(ctx, f"psY{tag}{i}", [128, 512], F32) for i in range(4)]
            return w

        def emit_ln(w, src, dst):
            for t in range(4):
                xt = src[:, t * D:(t + 1) * D]
                ve.reduce_sum(w["st_s"][:, t:t + 1], xt, axis=AX)
                wdv(sc)
                sc.square(w["sq"][:], xt)
                sAC.inc(sc.drain(), 1)
                wac(ve)
                ve.reduce_sum(w["st_q"][:, t:t + 1], w["sq"][:], axis=AX)
                sDV.inc(ve.drain(), 1)
            ve.tensor_scalar_mul(w["mean4"][:], w["st_s"][:], 1.0 / D)
            sDV.inc(ve.drain(), 1)
            wdv(sc)
            sc.square(w["nt1"][:], w["mean4"][:])
            sAC.inc(sc.drain(), 1)
            wac(ve)
            ve.scalar_tensor_tensor(w["var4"][:], w["st_q"][:], 1.0 / D, w["nt1"][:],
                                    ALU.mult, ALU.subtract)
            sDV.inc(ve.drain(), 1)
            wdv(sc)
            sc.activation(w["rstd4"][:], w["var4"][:], AF.Sqrt, bias=np.float32(EPS))
            sAC.inc(sc.drain(), 1)
            wac(ve)
            ve.reciprocal(w["nt2"][:], w["rstd4"][:])
            sDV.inc(ve.drain(), 1)
            wdv(ve)
            ve.scalar_tensor_tensor(w["nmr4"][:], w["mean4"][:], -1.0, w["nt2"][:],
                                    ALU.mult, ALU.mult)
            sDV.inc(ve.drain(), 1)
            wdv(ve)
            wpe(ve)
            for t in range(4):
                ve.tensor_scalar(dst[:, t * D:(t + 1) * D], src[:, t * D:(t + 1) * D],
                                 w["nt2"][:, t:t + 1], w["nmr4"][:, t:t + 1],
                                 ALU.mult, ALU.add)
            sDV.inc(ve.drain(), 1)

        def emit_tp(w, src, dst, ntile=4):
            for t in range(ntile):
                wdv(te)
                for j in range(8):
                    te.transpose(w["psT"][:, j * 128:(j + 1) * 128],
                                 src[:, t * D + j * 128: t * D + (j + 1) * 128],
                                 identb[:])
                sPE.inc(te.drain(), 1)
                wpe(ve)
                dv = dst[:].rearrange("p (j n) -> p j n", j=8)
                sv = w["psT"][:].rearrange("p (j n) -> p j n", j=8)
                ve.tensor_copy(dv[:, :, t * 128:(t + 1) * 128], sv[:, :, :])
                sDV.inc(ve.drain(), 1)

        def emit_ffn(w, xres, yout):
            rec_pe_w1 = [0, 0]
            rec_ac_h = [0, 0]
            rec_sc_w1 = {}

            def load_w1(hg_):
                par_ = hg_ % 2
                sc.wait_ge(sPE.h, rec_pe_w1[par_])
                for j in range(8):
                    sSC.inc(sc.dma_start(
                        out=w["w1s"][:, par_, j * 512:(j + 1) * 512],
                        in_=bw1d[j * 128:(j + 1) * 128, hg_ * 512:(hg_ + 1) * 512]), 16)
                rec_sc_w1[hg_] = sSC.v

            wdv(te)
            load_w1(0)
            for hg in range(8):
                par = hg % 2
                if hg + 1 < 8:
                    load_w1(hg + 1)
                te.wait_ge(sSC.h, rec_sc_w1[hg])
                for ht in range(4):
                    hpar = ht % 2
                    hidx = hg * 4 + ht
                    te.wait_ge(sAC.h, rec_ac_h[hpar])
                    for j in range(8):
                        te.matmul(w["psH"][hpar][:],
                                  w["w1s"][:, par, j * 512 + ht * 128: j * 512 + (ht + 1) * 128],
                                  w["xn_fm"][:, j * 512:(j + 1) * 512],
                                  start=(j == 0), stop=False, skip_group_check=True)
                    te.matmul(w["psH"][hpar][:], b1h[0:1, hidx * 128:(hidx + 1) * 128],
                              onesb[0:1, :], start=False, stop=False, skip_group_check=True)
                    te.matmul(w["psH"][hpar][:], b1l[0:1, hidx * 128:(hidx + 1) * 128],
                              onesb[0:1, :], start=False, stop=True, skip_group_check=True)
                    sPE.inc(te.drain(), 1)
                    wpe(sc)
                    sc.activation(w["h_fm"][:, hidx * 512:(hidx + 1) * 512],
                                  w["psH"][hpar][:], AF.Gelu)
                    rec_ac_h[hpar] = sAC.inc(sc.drain(), 1)
                rec_pe_w1[par] = sPE.v
            wac(te)
            rec_pe_w2 = [0, 0]
            rec_sc_w2 = {}

            def load_w2(idx_):
                dh_, hc_ = idx_ // 32, idx_ % 32
                par_ = idx_ % 2
                sc.wait_ge(sPE.h, rec_pe_w2[par_])
                sSC.inc(sc.dma_start(
                    out=w["w2s"][:, par_, :],
                    in_=bw2d[hc_ * 128:(hc_ + 1) * 128, dh_ * 512:(dh_ + 1) * 512]), 16)
                rec_sc_w2[idx_] = sSC.v

            load_w2(0)
            for dh in range(2):
                for hc in range(32):
                    idx = dh * 32 + hc
                    par = idx % 2
                    if idx + 1 < 64:
                        load_w2(idx + 1)
                    te.wait_ge(sSC.h, rec_sc_w2[idx])
                    for tt in range(4):
                        te.matmul(w["psY"][tt][:],
                                  w["h_fm"][:, hc * 512 + tt * 128: hc * 512 + (tt + 1) * 128],
                                  w["w2s"][:, par, :],
                                  start=(hc == 0), stop=False, skip_group_check=True)
                    rec_pe_w2[par] = sPE.inc(te.drain(), 1)
                for tt in range(4):
                    te.matmul(w["psY"][tt][:], onesb[0:1, 0:128],
                              b2h[0:1, dh * 512:(dh + 1) * 512],
                              start=False, stop=False, skip_group_check=True)
                    te.matmul(w["psY"][tt][:], onesb[0:1, 0:128],
                              b2l[0:1, dh * 512:(dh + 1) * 512],
                              start=False, stop=True, skip_group_check=True)
                sPE.inc(te.drain(), 1)
                wpe(ve)
                for tt in range(4):
                    ve.scalar_tensor_tensor(
                        yout[:, tt * D + dh * 512: tt * D + (dh + 1) * 512],
                        w["psY"][tt][:], 0.5,
                        xres[:, tt * D + dh * 512: tt * D + (dh + 1) * 512],
                        ALU.mult, ALU.add)
                sDV.inc(ve.drain(), 1)
                wdv(te)

        wsA = make_ws(actx, "A")
        x1b = mksb(actx, "x1bA", [128, 4 * D], F32)
        kstage = mksb(actx, "kstA", [128, 2, 512], BF16)
        vstage = mksb(actx, "vstA", [128, 2, 520], BF16)
        qin = mksb(actx, "qinA", [128, D], BF16)
        qin_fm = mksb(actx, "qinfA", [128, 8 * 128], BF16)
        qstage = mksb(actx, "qstA", [128, 8 * 128], BF16)

        for bat in range(NB):
            wdv(sy)
            wac(sy)
            for t in range(4):
                sLD.inc(sy.dma_start(
                    out=wsA["xb"][:, t * D:(t + 1) * D],
                    in_=xin[bat * TB + t * 128: bat * TB + (t + 1) * 128, :]), 16)
            wld(ve)
            wld(sc)
            emit_ln(wsA, wsA["xb"], wsA["xn_tm"])
            emit_tp(wsA, wsA["xn_tm"], wsA["xn_fm"])
            if STOP_AFTER == "A1" and bat == 0:
                break
            emit_ffn(wsA, wsA["xb"], x1b)
            if STOP_AFTER == "A2" and bat == 0:
                break
            wdv(sy)
            for t in range(4):
                sST.inc(sy.dma_start(
                    out=x1d[bat * TB + t * 128: bat * TB + (t + 1) * 128, :],
                    in_=x1b[:, t * D:(t + 1) * D]), 16)
            emit_ln(wsA, x1b, wsA["xn_tm"])
            wdv(sy)
            for t in range(4):
                sST.inc(sy.dma_start(out=xnqd[t],
                                     in_=wsA["xn_tm"][:, t * D:(t + 1) * D]), 16)
            emit_tp(wsA, wsA["xn_tm"], wsA["xn_fm"])
            wdv(te)
            rec_ac_k = [sAC.v, sAC.v]
            rec_st_k = [sST.v, sST.v]
            for mt in range(8):
                par = mt % 2
                te.wait_ge(sAC.h, rec_ac_k[par])
                for j in range(8):
                    te.matmul(wsA["psH"][par][:],
                              wkRT[:, j * D + mt * 128: j * D + (mt + 1) * 128],
                              wsA["xn_fm"][:, j * 512:(j + 1) * 512],
                              start=(j == 0), stop=False, skip_group_check=True)
                te.matmul(wsA["psH"][par][:], bkh[0:1, mt * 128:(mt + 1) * 128],
                          onesb[0:1, :], start=False, stop=False, skip_group_check=True)
                te.matmul(wsA["psH"][par][:], bkl[0:1, mt * 128:(mt + 1) * 128],
                          onesb[0:1, :], start=False, stop=True, skip_group_check=True)
                sPE.inc(te.drain(), 1)
                wpe(sc)
                sc.wait_ge(sST.h, rec_st_k[par])
                sc.activation(kstage[:, par, :], wsA["psH"][par][:], AF.Identity)
                rec_ac_k[par] = sAC.inc(sc.drain(), 1)
                wac(sy)
                rec_st_k[par] = sST.inc(sy.dma_start(
                    out=kd[mt * 128:(mt + 1) * 128, bat * TB:(bat + 1) * TB],
                    in_=kstage[:, par, :]), 16)
            if STOP_AFTER == "A3" and bat == 0:
                break
            wac(te)
            rec_dv_v = [sDV.v, sDV.v]
            rec_st_v = [sST.v, sST.v]
            for tt in range(4):
                for mh in range(2):
                    par = (tt * 2 + mh) % 2
                    te.wait_ge(sDV.h, rec_dv_v[par])
                    for j in range(8):
                        te.matmul(wsA["psH"][par][:],
                                  wsA["xn_fm"][:, j * 512 + tt * 128: j * 512 + (tt + 1) * 128],
                                  wvTb[:, j * D + mh * 512: j * D + (mh + 1) * 512],
                                  start=(j == 0), stop=False, skip_group_check=True)
                    te.matmul(wsA["psH"][par][:], onesb[0:1, 0:128],
                              bvh[0:1, mh * 512:(mh + 1) * 512],
                              start=False, stop=False, skip_group_check=True)
                    te.matmul(wsA["psH"][par][:], onesb[0:1, 0:128],
                              bvl[0:1, mh * 512:(mh + 1) * 512],
                              start=False, stop=True, skip_group_check=True)
                    sPE.inc(te.drain(), 1)
                    gp.wait_ge(sST.h, rec_st_v[par])
                    gp.memset(vstage[:, par, :], 1.0)
                    sGP.inc(gp.drain(), 1)
                    wgp(ve)
                    wpe(ve)
                    for g in range(8):
                        ve.tensor_copy(vstage[:, par, g * 65:g * 65 + 64],
                                       wsA["psH"][par][:, g * 64:(g + 1) * 64])
                    rec_dv_v[par] = sDV.inc(ve.drain(), 1)
                    wdv(sy)
                    rec_st_v[par] = sST.inc(sy.dma_start(
                        out=vd[bat * 4 + tt, :, mh * 520:(mh + 1) * 520],
                        in_=vstage[:, par, :]), 16)
            if STOP_AFTER == "A4" and bat == 0:
                break
            wst(sy)
            sLD.inc(sy.dma_start(
                out=qin[:],
                in_=xnqd[ds(ccv, 1), :, :].rearrange("o p n -> (o p) n")), 16)
            wld(te)
            wdv(te)
            for j in range(8):
                te.transpose(wsA["psT"][:, j * 128:(j + 1) * 128],
                             qin[:, j * 128:(j + 1) * 128], identb[:])
            sPE.inc(te.drain(), 1)
            wpe(ve)
            ve.tensor_copy(qin_fm[:], wsA["psT"][:])
            sDV.inc(ve.drain(), 1)
            wdv(te)
            wst(sc)
            rec_ac_q = [sAC.v, sAC.v]
            for mt in range(8):
                par = mt % 2
                te.wait_ge(sAC.h, rec_ac_q[par])
                for j in range(8):
                    te.matmul(wsA["psH"][par][:, 0:128],
                              wqRT[:, j * D + mt * 128: j * D + (mt + 1) * 128],
                              qin_fm[:, j * 128:(j + 1) * 128],
                              start=(j == 0), stop=False, skip_group_check=True)
                te.matmul(wsA["psH"][par][:, 0:128], bqh[0:1, mt * 128:(mt + 1) * 128],
                          onesb[0:1, 0:128], start=False, stop=False, skip_group_check=True)
                te.matmul(wsA["psH"][par][:, 0:128], bql[0:1, mt * 128:(mt + 1) * 128],
                          onesb[0:1, 0:128], start=False, stop=True, skip_group_check=True)
                sPE.inc(te.drain(), 1)
                wpe(sc)
                sc.activation(qstage[:, mt * 128:(mt + 1) * 128],
                              wsA["psH"][par][:, 0:128], AF.Identity)
                rec_ac_q[par] = sAC.inc(sc.drain(), 1)
            wac(sy)
            sST.inc(sy.dma_start(out=qd[bat], in_=qstage[:]), 16)
            wst(sy)
            if STOP_AFTER == "A5" and bat == 0:
                break
        wst(sy)
        barrier()
        if KDBG and STOP_AFTER == "A2":
            for t in range(4):
                sST.inc(sy.dma_start(out=dbg_x1[t * 128:(t + 1) * 128, :],
                                     in_=x1b[:, t * D:(t + 1) * D]), 16)
            sST.inc(sy.dma_start(out=dbg_k, in_=wsA["h_fm"][:, 0:512]), 16)
            sST.inc(sy.dma_start(out=dbg_q, in_=wsA["h_fm"][:, 512:1536]), 16)
            wst(sy)
        if KDBG and STOP_AFTER == "A1":
            for t in range(4):
                sST.inc(sy.dma_start(out=dbg_x1[t * 128:(t + 1) * 128, :],
                                     in_=wsA["xb"][:, t * D:(t + 1) * D]), 16)
            ve.tensor_copy(x1b[:, 0:D], wsA["xn_tm"][:, 0:D])
            ve.tensor_copy(x1b[:, D:2 * D], wsA["xn_fm"][:, 0:D])
            sDV.inc(ve.drain(), 1)
            wdv(sy)
            sST.inc(sy.dma_start(out=dbg_pp[:, 0:1024], in_=x1b[:, 0:D]), 16)
            sST.inc(sy.dma_start(out=dbg_pp[:, 1024:2048], in_=x1b[:, D:2 * D]), 16)
            sST.inc(sy.dma_start(out=dbg_pp[:, 2048:2052], in_=wsA["st_s"][:]), 16)
            sST.inc(sy.dma_start(out=dbg_pp[:, 2052:2056], in_=wsA["st_q"][:]), 16)
            sST.inc(sy.dma_start(out=dbg_pp[:, 2056:2060], in_=wsA["rstd4"][:]), 16)
            sST.inc(sy.dma_start(out=dbg_pp[:, 2060:2064], in_=wsA["nmr4"][:]), 16)
            wst(sy)
        if KDBG and STOP_AFTER not in ("A1", "A2"):
            for t in range(4):
                sLD.inc(sy.dma_start(out=wsA["xb"][:, t * D:(t + 1) * D],
                                     in_=x1d[t * 128:(t + 1) * 128, :]), 16)
            wld(sy)
            for t in range(4):
                sST.inc(sy.dma_start(out=dbg_x1[t * 128:(t + 1) * 128, :],
                                     in_=wsA["xb"][:, t * D:(t + 1) * D]), 16)
            sLD.inc(sy.dma_start(out=kstage[:, 0, :], in_=kd[0:128, 0:512]), 16)
            sLD.inc(sy.dma_start(out=qstage[:], in_=qd[0]), 16)
            sLD.inc(sy.dma_start(out=vstage[:, 0, 0:520], in_=vd[0][:, 0:520]), 16)
            sLD.inc(sy.dma_start(out=vstage[:, 1, 0:520], in_=vd[0][:, 520:1040]), 16)
            wld(sy)
            sST.inc(sy.dma_start(out=dbg_k, in_=kstage[:, 0, :]), 16)
            sST.inc(sy.dma_start(out=dbg_q, in_=qstage[:]), 16)
            sST.inc(sy.dma_start(out=dbg_v[:, 0:520], in_=vstage[:, 0, 0:520]), 16)
            sST.inc(sy.dma_start(out=dbg_v[:, 520:1040], in_=vstage[:, 1, 0:520]), 16)
            wst(sy)
            barrier()
        actx.__exit__(None, None, None)
        if STOP_AFTER in ("A", "A1", "A2", "A3", "A4", "A5"):
            return

        bctx = ExitStack()
        bctx.__enter__()
        o_fm = mksb(bctx, "o_fm", [128, 8 * 2048], BF16)
        battn = ExitStack()
        battn.__enter__()
        ksl = mksb(battn, "ksl", [128, 2, 8 * 512], BF16)
        vsl = mksb(battn, "vsl", [128, 2, 4 * 1040], BF16)
        qsl = mksb(battn, "qsl", [128, 2, 1024], BF16)
        qdup = mksb(battn, "qdup", [128, 2, 512], BF16)
        expS = mksb(battn, "expS", [128, 2, 1024], BF16)
        recq = mksb(battn, "recq", [1, 512], F32)
        rech = mksb(battn, "rech", [1, 512], BF16)
        bcst = mksb(battn, "bcst", [64, 512], F32)
        psS = mkps(battn, "psS", [128, 2048], F32)
        dbgs = mksb(battn, "dbgsB", [128, 1024], F32) if KDBG else None
        psO = [mkps(battn, "psOa", [128, 512], F32), mkps(battn, "psOb", [128, 512], F32)]
        psB = mkps(battn, "psB", [64, 512], F32)

        for tb_ in (ksl, vsl, qsl, qdup, expS):
            gp.memset(tb_[:], 0.0)
        gp.memset(bcst[:], 0.0)
        sGP.inc(gp.drain(), 1)
        barrier()

        gp_head = []

        def load_head(hd, par):
            sGP.inc(sy.dma_start(out=qsl[:, par, :], in_=qd[hd]), 16)
            for mt in range(8):
                sGP.inc(sy.dma_start(
                    out=ksl[:, par, mt * 512:(mt + 1) * 512],
                    in_=kd[mt * 128:(mt + 1) * 128, hd * TB:(hd + 1) * TB]), 16)
            for tt in range(4):
                sGP.inc(sy.dma_start(
                    out=vsl[:, par, tt * 1040:(tt + 1) * 1040],
                    in_=vd[hd * 4 + tt]), 16)
            gp_head.append(sGP.v)

        rec_pe_head_end = [0, 0]
        rec_dv_norm = [0, 0]
        rec_dv_qdup = {}

        def emit_qdup(hd_, G_):
            par_ = hd_ % 2
            gpar_ = G_ % 2
            ve.wait_ge(sGP.h, gp_head[hd_])
            for gl_ in range(4):
                g_ = G_ * 4 + gl_
                srcq_ = qsl[(g_ % 2) * 64:(g_ % 2) * 64 + 64, par_,
                            (g_ // 2) * 128:(g_ // 2 + 1) * 128]
                ve.tensor_copy(qdup[0:64, gpar_, gl_ * 128:(gl_ + 1) * 128], srcq_)
                ve.tensor_copy(qdup[64:128, gpar_, gl_ * 128:(gl_ + 1) * 128], srcq_)
            return sDV.inc(ve.drain(), 1)

        load_head(0, 0)
        rec_dv_qdup[(0, 0)] = emit_qdup(0, 0)
        for hd in range(NH):
            par = hd % 2
            if hd + 1 < NH:
                sy.wait_ge(sPE.h, rec_pe_head_end[(hd + 1) % 2])
                load_head(hd + 1, (hd + 1) % 2)
            te.wait_ge(sGP.h, gp_head[hd])
            for G in range(4):
                gpar = G % 2
                te.wait_ge(sDV.h, rec_dv_qdup[(hd, G)])
                rec_ac_exp = {}

                def emit_pv(pv):
                    mtp, ttpp = pv // 4, pv % 4
                    pp2 = pv % 2
                    te.wait_ge(sAC.h, rec_ac_exp[pv])
                    if pv == 0:
                        te.wait_ge(sDV.h, rec_dv_norm[gpar])
                    te.matmul(psO[gpar][0:65, :],
                              vsl[:, par, ttpp * 1040 + (2 * mtp) * 65:
                                        ttpp * 1040 + (2 * mtp) * 65 + 65],
                              expS[:, pp2, 0:512],
                              start=(pv == 0), stop=False, skip_group_check=True)
                    te.matmul(psO[gpar][0:65, :],
                              vsl[:, par, ttpp * 1040 + (2 * mtp + 1) * 65:
                                        ttpp * 1040 + (2 * mtp + 1) * 65 + 65],
                              expS[:, pp2, 512:1024],
                              start=False, stop=(pv == 31), skip_group_check=True)

                for pj in range(32):
                    mt, ttp = pj // 4, pj % 4
                    p2 = pj % 2
                    if pj >= 2:
                        te.wait_ge(sAC.h, rec_ac_exp[pj - 2])
                    kslice = ksl[:, par, mt * 512 + ttp * 128: mt * 512 + (ttp + 1) * 128]
                    te.matmul(psS[:, p2 * 1024: p2 * 1024 + 512],
                              kslice[0:64, :], qdup[0:64, gpar, :],
                              start=True, stop=True, tile_position=(0, 0),
                              skip_group_check=True)
                    te.matmul(psS[:, p2 * 1024 + 512: p2 * 1024 + 1024],
                              kslice[64:128, :], qdup[64:128, gpar, :],
                              start=True, stop=True, tile_position=(64, 0),
                              skip_group_check=True)
                    pe_qk = sPE.inc(te.drain(), 1)
                    sc.wait_ge(sPE.h, pe_qk)
                    sc.activation(expS[:, p2, :], psS[:, p2 * 1024:(p2 + 1) * 1024],
                                  AF.Exp, bias=np.float32(-EXPC), scale=np.float32(0.125))
                    rec_ac_exp[pj] = sAC.inc(sc.drain(), 1)
                    if pj >= 1:
                        emit_pv(pj - 1)
                emit_pv(31)
                sPE.inc(te.drain(), 1)
                nxt = (hd, G + 1) if G < 3 else \
                      ((hd + 1, 0) if hd + 1 < NH else None)
                if nxt is not None:
                    rec_dv_qdup[nxt] = emit_qdup(*nxt)
                wpe(ve)
                ve.reciprocal(recq[:], psO[gpar][64:65, :])
                sDV.inc(ve.drain(), 1)
                wdv(ve)
                ve.tensor_copy(rech[:], recq[:])
                sDV.inc(ve.drain(), 1)
                wdv(te)
                te.matmul(psB[0:64, :], onesb[0:1, 0:64], rech[0:1, :],
                          start=True, stop=True, skip_group_check=True)
                sPE.inc(te.drain(), 1)
                wpe(sc)
                sc.wait_ge(sDV.h, rec_dv_norm[1 - gpar])
                sc.activation(bcst[0:64, :], psB[0:64, :], AF.Identity)
                sAC.inc(sc.drain(), 1)
                wac(ve)
                mt_o, basep = hd // 2, (hd % 2) * 64
                ve.tensor_tensor(
                    o_fm[basep:basep + 64,
                         mt_o * 2048 + G * 512: mt_o * 2048 + (G + 1) * 512],
                    psO[gpar][0:64, :], bcst[0:64, :], ALU.mult)
                if KDBG and hd == 0 and G == 0:
                    ve.tensor_copy(dbgs[0:65, 0:512], psO[gpar][0:65, :])
                    ve.tensor_copy(dbgs[0:64, 512:1024],
                                   o_fm[0:64, 0:512])
                rec_dv_norm[gpar] = sDV.inc(ve.drain(), 1)
            rec_pe_head_end[par] = sPE.v
        barrier()
        if KDBG:
            sST.inc(sy.dma_start(out=dbg_o, in_=o_fm[:, 0:2048]), 16)
            sST.inc(sy.dma_start(out=dbg_pp[0:65, 2560:3072], in_=dbgs[0:65, 0:512]), 16)
            sST.inc(sy.dma_start(out=dbg_pp[0:64, 3072:3584], in_=dbgs[0:64, 512:1024]), 16)
            sST.inc(sy.dma_start(out=dbg_bw1, in_=o_fm[0:128, 0:512]), 16)
            wst(sy)
        battn.__exit__(None, None, None)
        if STOP_AFTER == "B":
            bctx.__exit__(None, None, None)
            return

        cctx = ExitStack()
        cctx.__enter__()
        woTb = mksb(cctx, "woTb", [128, 8 * D], BF16)
        wsC = make_ws(cctx, "C")
        x1t = mksb(cctx, "x1tC", [128, D], F32)
        outb = mksb(cctx, "outbC", [128, 4 * D], F32)
        x2b = wsC["xb"]

        for j in range(8):
            wdv(sy)
            sLD.inc(sy.dma_start(out=x2b[:, 0:D], in_=woT[j * 128:(j + 1) * 128, :]), 16)
            wld(ve)
            ve.tensor_copy(woTb[:, j * D:(j + 1) * D], x2b[:, 0:D])
            sDV.inc(ve.drain(), 1)
        wdv(sy)

        for sbi in range(4):
            wac(ve)
            for gl in range(4):
                g = sbi * 4 + gl
                G, g_l = g // 4, g % 4
                wdv(sy)
                sLD.inc(sy.dma_start(
                    out=x1t[:],
                    in_=x1d.rearrange("(a b) d -> a b d", b=16)[ds(ccv128, 128), g, :]), 16)
                for jh in range(2):
                    par = jh % 2
                    wdv(te)
                    for mt in range(8):
                        te.matmul(wsC["psH"][par][:],
                                  o_fm[:, mt * 2048 + G * 512 + g_l * 128:
                                       mt * 2048 + G * 512 + (g_l + 1) * 128],
                                  woTb[:, mt * D + jh * 512: mt * D + (jh + 1) * 512],
                                  start=(mt == 0), stop=False, skip_group_check=True)
                    te.matmul(wsC["psH"][par][:], onesb[0:1, 0:128],
                              boh[0:1, jh * 512:(jh + 1) * 512],
                              start=False, stop=False, skip_group_check=True)
                    te.matmul(wsC["psH"][par][:], onesb[0:1, 0:128],
                              bol[0:1, jh * 512:(jh + 1) * 512],
                              start=False, stop=True, skip_group_check=True)
                    sPE.inc(te.drain(), 1)
                    wpe(ve)
                    wld(ve)
                    ve.tensor_add(x2b[:, gl * D + jh * 512: gl * D + (jh + 1) * 512],
                                  x1t[:, jh * 512:(jh + 1) * 512], wsC["psH"][par][:])
                    sDV.inc(ve.drain(), 1)
            wdv(sc)
            emit_ln(wsC, x2b, wsC["xn_tm"])
            emit_tp(wsC, wsC["xn_tm"], wsC["xn_fm"])
            wst(ve)
            emit_ffn(wsC, x2b, outb)
            wdv(sy)
            for gl in range(4):
                g = sbi * 4 + gl
                sST.inc(sy.dma_start(out=out[g * 128:(g + 1) * 128, :],
                                     in_=outb[:, gl * D:(gl + 1) * D]), 16)
        wst(sy)

        cctx.__exit__(None, None, None)
        bctx.__exit__(None, None, None)

    gctx.__exit__(None, None, None)
    return nc


_NC_CACHE = {}


def kernel(L0_attn_x, L0_attn_wq, L0_attn_bq, L0_attn_wk, L0_attn_bk, L0_attn_wv,
           L0_attn_bv, L0_attn_wo, L0_attn_bo, L0_attn_rot_cos, L0_attn_rot_sin,
           L0_ffn_w1, L0_ffn_b1, L0_ffn_w2, L0_ffn_b2):
    x = np.asarray(L0_attn_x, dtype=np.float32)
    if "nc" not in _NC_CACHE:
        _NC_CACHE["nc"] = build_nc()
    nc = _NC_CACHE["nc"]

    f32 = np.float32
    common = {
        "w1T": np.ascontiguousarray(np.asarray(L0_ffn_w1, f32).T),
        "w2T": np.ascontiguousarray(np.asarray(L0_ffn_w2, f32).T),
        "wqT": np.ascontiguousarray(np.asarray(L0_attn_wq, f32).T),
        "wkT": np.ascontiguousarray(np.asarray(L0_attn_wk, f32).T),
        "wvT": np.ascontiguousarray(np.asarray(L0_attn_wv, f32).T),
        "woT": np.ascontiguousarray(np.asarray(L0_attn_wo, f32).T),
        "b1": np.ascontiguousarray(np.asarray(L0_ffn_b1, f32)),
        "b2": np.ascontiguousarray(np.asarray(L0_ffn_b2, f32)),
        "bq": np.ascontiguousarray(np.asarray(L0_attn_bq, f32)),
        "bk": np.ascontiguousarray(np.asarray(L0_attn_bk, f32)),
        "bv": np.ascontiguousarray(np.asarray(L0_attn_bv, f32)),
        "bo": np.ascontiguousarray(np.asarray(L0_attn_bo, f32)),
        "rotc": np.ascontiguousarray(np.asarray(L0_attn_rot_cos, f32)[4095]),
        "rots": np.ascontiguousarray(np.asarray(L0_attn_rot_sin, f32)[4095]),
    }
    in_maps = []
    for c in range(8):
        m = dict(common)
        m["xin"] = np.ascontiguousarray(x[c // 4])
        in_maps.append(m)

    res = run_bass_kernel_spmd(nc, in_maps, core_ids=list(range(8)))

    outp = np.empty((2, TFULL, D), np.float32)
    for c in range(8):
        bc, cc = c // 4, c % 4
        r = np.asarray(res.results[c]["out"]).reshape(16, 128, D)
        outp[bc, cc * 2048:(cc + 1) * 2048] = r.transpose(1, 0, 2).reshape(2048, D)
    return outp


# version 09 (final)
# speedup vs baseline: 1.4587x; 1.0017x over previous
import os
import numpy as np
import ml_dtypes
import concourse.bass as bass
import concourse.mybir as mybir
from concourse.bass import ds
from concourse.bass_utils import run_bass_kernel_spmd
from contextlib import ExitStack

F32 = mybir.dt.float32
BF16 = mybir.dt.bfloat16
AF = mybir.ActivationFunctionType
ALU = mybir.AluOpType
AX = mybir.AxisListType.X

D = 1024
HF = 4096
NH = 16
TFULL = 8192
NB = 16
TB = 512
EPS = 1e-5
EXPC = 12.0
CONSTS = [EPS, -EXPC, 0.125, 1.0 / 1024.0, 1.0 / 4096.0, 0.5]


STOP_AFTER = os.environ.get("KSTOP", "")
KDBG = os.environ.get("KDBG", "") == "1"



def build_nc():
    nc = bass.Bass("TRN2", target_bir_lowering=False, debug=False)

    xin = nc.dram_tensor("xin", [TFULL, D], F32, kind="ExternalInput").ap()
    w1T = nc.dram_tensor("w1T", [D, HF], F32, kind="ExternalInput").ap()
    w2T = nc.dram_tensor("w2T", [HF, D], F32, kind="ExternalInput").ap()
    wqT = nc.dram_tensor("wqT", [D, D], F32, kind="ExternalInput").ap()
    wkT = nc.dram_tensor("wkT", [D, D], F32, kind="ExternalInput").ap()
    wvT = nc.dram_tensor("wvT", [D, D], F32, kind="ExternalInput").ap()
    woT = nc.dram_tensor("woT", [D, D], F32, kind="ExternalInput").ap()
    b1i = nc.dram_tensor("b1", [HF], F32, kind="ExternalInput").ap()
    b2i = nc.dram_tensor("b2", [D], F32, kind="ExternalInput").ap()
    bqi = nc.dram_tensor("bq", [D], F32, kind="ExternalInput").ap()
    bki = nc.dram_tensor("bk", [D], F32, kind="ExternalInput").ap()
    bvi = nc.dram_tensor("bv", [D], F32, kind="ExternalInput").ap()
    boi = nc.dram_tensor("bo", [D], F32, kind="ExternalInput").ap()
    rci = nc.dram_tensor("rotc", [32], F32, kind="ExternalInput").ap()
    rsi = nc.dram_tensor("rots", [32], F32, kind="ExternalInput").ap()
    out = nc.dram_tensor("out", [2048, D], F32, kind="ExternalOutput").ap()
    if KDBG:
        dbg_pp = nc.dram_tensor("dbg_pp", [128, 4096], F32, kind="ExternalOutput").ap()
        dbg_bw1 = nc.dram_tensor("dbg_bw1", [128, 512], BF16, kind="ExternalOutput").ap()
        dbg_bw2 = nc.dram_tensor("dbg_bw2", [128, 512], BF16, kind="ExternalOutput").ap()
        dbg_x1 = nc.dram_tensor("dbg_x1", [512, D], F32, kind="ExternalOutput").ap()
        dbg_k = nc.dram_tensor("dbg_k", [128, 512], BF16, kind="ExternalOutput").ap()
        dbg_q = nc.dram_tensor("dbg_q", [128, 1024], BF16, kind="ExternalOutput").ap()
        dbg_v = nc.dram_tensor("dbg_v", [128, 1040], BF16, kind="ExternalOutput").ap()
        dbg_o = nc.dram_tensor("dbg_o", [128, 2048], BF16, kind="ExternalOutput").ap()

    bw1d = nc.dram_tensor("bw1d", [D, HF], BF16, kind="Internal").ap()
    bw2d = nc.dram_tensor("bw2d", [HF, D], BF16, kind="Internal").ap()
    kd = nc.dram_tensor("kd", [D, TFULL], BF16, kind="Internal").ap()
    vd = nc.dram_tensor("vd", [64, 128, 16 * 65], BF16, kind="Internal").ap()
    qd = nc.dram_tensor("qd", [NH, 128, 8 * 128], BF16, kind="Internal").ap()
    x1d = nc.dram_tensor("x1d", [TFULL, D], F32, kind="Internal").ap()
    xnqd = nc.dram_tensor("xnqd", [4, 128, D], BF16, kind="Internal").ap()
    rowb = nc.dram_tensor("rowb", [1024], F32, kind="Internal").ap()
    al1d = nc.dram_tensor("al1d", [HF], F32, kind="Internal").ap()
    al2d = nc.dram_tensor("al2d", [D], F32, kind="Internal").ap()
    recb = nc.dram_tensor("recb", [512], F32, kind="Internal").ap()

    gctx = ExitStack()
    gctx.__enter__()

    def mksb(ctx, name, shape, dt):
        return ctx.enter_context(nc.sbuf_tensor(name, shape, dt))

    def mkps(ctx, name, shape, dt):
        return ctx.enter_context(nc.psum_tensor(name, shape, dt))

    class Ctr:
        def __init__(self, name):
            self.h = gctx.enter_context(nc.semaphore(name))
            self.v = 0

        def inc(self, inst, n=1):
            inst.then_inc(self.h, n)
            self.v += n
            return self.v

    sLD = Ctr("sLD")
    sSC = Ctr("sSC")
    sGP = Ctr("sGP")
    sPE = Ctr("sPE")
    sAC = Ctr("sAC")
    sDV = Ctr("sDV")
    sST = Ctr("sST")
    sBR = Ctr("sBR")

    identb = mksb(gctx, "identb", [128, 128], BF16)
    onesb = mksb(gctx, "onesb", [1, 512], BF16)
    ones64b = mksb(gctx, "ones64b", [128, 64], BF16)
    b1h = mksb(gctx, "b1h", [1, HF], BF16)
    b1l = mksb(gctx, "b1l", [1, HF], BF16)
    b2h = mksb(gctx, "b2h", [1, D], BF16)
    b2l = mksb(gctx, "b2l", [1, D], BF16)
    bvh = mksb(gctx, "bvh", [1, D], BF16)
    bvl = mksb(gctx, "bvl", [1, D], BF16)
    boh = mksb(gctx, "boh", [1, D], BF16)
    bol = mksb(gctx, "bol", [1, D], BF16)
    bqh = mksb(gctx, "bqh", [1, D], BF16)
    bql = mksb(gctx, "bql", [1, D], BF16)
    bkh = mksb(gctx, "bkh", [1, D], BF16)
    bkl = mksb(gctx, "bkl", [1, D], BF16)

    blk = gctx.enter_context(nc.Block())

    @blk.sync
    def _(sync):
        te, ve, sc, gp, sy = nc.tensor, nc.vector, nc.scalar, nc.gpsimd, nc.sync
        pid = sy.partition_id()
        ccv = sy.compute_val(pid % 4)
        ccv128 = sy.compute_val(pid % 4 * 128)

        def wpe(e):
            e.wait_ge(sPE.h, sPE.v)

        def wac(e):
            e.wait_ge(sAC.h, sAC.v)

        def wdv(e):
            e.wait_ge(sDV.h, sDV.v)

        def wld(e):
            e.wait_ge(sLD.h, sLD.v)

        def wsc(e):
            e.wait_ge(sSC.h, sSC.v)

        def wst(e):
            e.wait_ge(sST.h, sST.v)

        def wgp(e):
            e.wait_ge(sGP.h, sGP.v)

        def barrier():
            for e in (te, ve, sc, gp, sy):
                sBR.inc(e.drain(), 1)
            tgt = sBR.v
            for e in (te, ve, sc, gp, sy):
                e.wait_ge(sBR.h, tgt)

        actx = ExitStack()
        actx.__enter__()
        wqRT = mksb(actx, "wqRT", [128, 8 * D], BF16)
        wkRT = mksb(actx, "wkRT", [128, 8 * D], BF16)
        wvTb = mksb(actx, "wvTb", [128, 8 * D], BF16)

        pctx = ExitStack()
        pctx.__enter__()
        crow = mksb(pctx, "crow", [1, 128], F32)
        srow = mksb(pctx, "srow", [1, 128], F32)
        ssrow = mksb(pctx, "ssrow", [1, 128], F32)
        signp = mksb(pctx, "signp", [1, 128], F32)
        cbt = mksb(pctx, "cbt", [128, 128], F32)
        ssbt = mksb(pctx, "ssbt", [128, 128], F32)
        brow_f = mksb(pctx, "brow_f", [1, 6 * D + HF], F32)
        btmp = mksb(pctx, "btmp", [1, 2 * D], F32)
        wld_t = mksb(pctx, "wld_t", [128, D], F32)
        wtmp = mksb(pctx, "wtmp", [128, 256], F32)
        wsl = mksb(pctx, "wsl", [128, 8 * 512], F32)
        wab = mksb(pctx, "wab", [128, 512], BF16)
        alc = mksb(pctx, "alc", [128, 4], F32)
        abca = mksb(pctx, "abca", [128, 512], F32)
        wsg = mksb(pctx, "wsg", [128, 8 * 512], BF16)
        psP2 = mkps(pctx, "psP2", [128, 512], F32)

        gp.memset(onesb[:], 1.0)
        gp.memset(ones64b[:], 1.0)
        gp.memset(identb[:], 0.0)
        gp.affine_select(out=identb[:], in_=identb[:], compare_op=ALU.not_equal,
                         fill=1.0, base=0, pattern=[[-1, 128]], channel_multiplier=1)
        gp.memset(signp[:], 1.0)
        sGP.inc(gp.drain(), 1)
        wgp(ve)
        spv = signp[:].rearrange("p (i e) -> p i e", e=2)
        ve.tensor_single_scalar(spv[:, :, 0], spv[:, :, 0], -1.0, ALU.mult)
        sDV.inc(ve.drain(), 1)
        for e in (te, ve, sc, sy):
            wgp(e)

        OB1, OB2, OBV, OBO, OBQ, OBK = 0, HF, HF + D, HF + 2 * D, HF + 3 * D, HF + 4 * D
        for off, t in ((OB1, b1i), (OB2, b2i), (OBV, bvi), (OBO, boi),
                       (OBQ, bqi), (OBK, bki)):
            sLD.inc(sy.dma_start(out=brow_f[0:1, off:off + t.shape[0]],
                                 in_=t[None, :]), 16)
        rr = btmp
        sLD.inc(sy.dma_start(out=rr[0:1, 0:32], in_=rci[None, :]), 16)
        sLD.inc(sy.dma_start(out=rr[0:1, 32:64], in_=rsi[None, :]), 16)
        wld(ve)
        for b in range(2):
            for e in range(2):
                cv = crow[0:1, b * 64:(b + 1) * 64].rearrange("p (i e) -> p i e", e=2)
                sv = srow[0:1, b * 64:(b + 1) * 64].rearrange("p (i e) -> p i e", e=2)
                ve.tensor_copy(cv[:, :, e], rr[0:1, 0:32])
                ve.tensor_copy(sv[:, :, e], rr[0:1, 32:64])
        ve.tensor_mul(ssrow[:], srow[:], signp[:])
        sDV.inc(ve.drain(), 1)
        wdv(sy)
        sST.inc(sy.dma_start(out=rowb[None, 0:128], in_=crow[:]), 16)
        sST.inc(sy.dma_start(out=rowb[None, 128:256], in_=ssrow[:]), 16)
        wst(sy)
        sLD.inc(sy.dma_start(out=cbt[:], in_=bass.AP(rowb.tensor, 0, [[0, 128], [1, 128]]),
                             allow_slow_non_contiguous=True), 16)
        sLD.inc(sy.dma_start(out=ssbt[:], in_=bass.AP(rowb.tensor, 128, [[0, 128], [1, 128]]),
                             allow_slow_non_contiguous=True), 16)

        wld(ve)
        for off in (OBQ, OBK):
            for ch in range(8):
                s0 = brow_f[0:1, off + ch * 128: off + (ch + 1) * 128]
                t0 = btmp[0:1, ch * 128:(ch + 1) * 128]
                sv = s0.rearrange("p (i e) -> p i e", e=2)
                tv = t0.rearrange("p (i e) -> p i e", e=2)
                ssv = ssrow[0:1, :].rearrange("p (i e) -> p i e", e=2)
                ve.tensor_tensor(tv[:, :, 0], sv[:, :, 1], ssv[:, :, 0], ALU.mult)
                ve.tensor_tensor(tv[:, :, 1], sv[:, :, 0], ssv[:, :, 1], ALU.mult)
                ve.tensor_tensor(s0, s0, crow[0:1, :], ALU.mult)
                ve.tensor_tensor(s0, s0, t0, ALU.add)
        SPLITS = ((b1h, b1l, OB1, HF), (b2h, b2l, OB2, D),
                  (bvh, bvl, OBV, D), (boh, bol, OBO, D),
                  (bqh, bql, OBQ, D), (bkh, bkl, OBK, D))
        for hi, lo, off, ln in SPLITS:
            ve.tensor_copy(hi[:], brow_f[0:1, off:off + ln])
        sDV.inc(ve.drain(), 1)
        wdv(ve)
        for hi, lo, off, ln in SPLITS:
            ve.tensor_tensor(brow_f[0:1, off:off + ln],
                             brow_f[0:1, off:off + ln], hi[:], ALU.subtract)
        sDV.inc(ve.drain(), 1)
        wdv(ve)
        for hi, lo, off, ln in SPLITS:
            ve.tensor_copy(lo[:], brow_f[0:1, off:off + ln])
        sDV.inc(ve.drain(), 1)

        for wsrc, wdst, roped in ((wqT, wqRT, True), (wkT, wkRT, True),
                                  (wvT, wvTb, False)):
            for j in range(8):
                wdv(sy)
                sLD.inc(sy.dma_start(out=wld_t[:], in_=wsrc[j * 128:(j + 1) * 128, :]), 16)
                wld(ve)
                if roped:
                    for ch in range(8):
                        s0 = wld_t[:, ch * 128:(ch + 1) * 128]
                        sv = s0.rearrange("p (i e) -> p i e", e=2)
                        t1 = wtmp[:, 0:128]
                        t2 = wtmp[:, 128:256]
                        t2v = t2.rearrange("p (i e) -> p i e", e=2)
                        ssv = ssbt[:].rearrange("p (i e) -> p i e", e=2)
                        ve.tensor_tensor(t2v[:, :, 0], sv[:, :, 1], ssv[:, :, 0], ALU.mult)
                        ve.tensor_tensor(t2v[:, :, 1], sv[:, :, 0], ssv[:, :, 1], ALU.mult)
                        ve.tensor_tensor(t1, s0, cbt[:], ALU.mult)
                        ve.tensor_tensor(wdst[:, j * D + ch * 128: j * D + (ch + 1) * 128],
                                         t1, t2, ALU.add)
                else:
                    ve.tensor_copy(wdst[:, j * D:(j + 1) * D], wld_t[:])
                sDV.inc(ve.drain(), 1)

        if STOP_AFTER == "prep3":
            if KDBG:
                sST.inc(sy.dma_start(out=dbg_pp[0:1, 0:3072], in_=brow_f[0:1, HF:HF + 3 * D]), 16)
                sST.inc(sy.dma_start(out=dbg_pp[1:2, 0:128], in_=crow[:]), 16)
                sST.inc(sy.dma_start(out=dbg_pp[1:2, 128:256], in_=ssrow[:]), 16)
                sST.inc(sy.dma_start(out=dbg_pp[2:34, 0:128], in_=cbt[0:32, :]), 16)
                wdv(ve)

                ve.tensor_copy(brow_f[0:1, 0:1024], bass.AP(wqRT, 0, [[8192, 1], [1, 1024]]))
                sDV.inc(ve.drain(), 1)
                wdv(sy)
                sST.inc(sy.dma_start(out=dbg_pp[40:41, 0:1024], in_=brow_f[0:1, 0:1024]), 16)
                sy.wait_ge(sST.h, sST.v)
            pctx.__exit__(None, None, None)
            actx.__exit__(None, None, None)
            return
        for slab in range(8):
            wst(sy)
            wdv(sy)
            for j in range(8):
                sLD.inc(sy.dma_start(
                    out=wsl[:, j * 512:(j + 1) * 512],
                    in_=w1T[j * 128:(j + 1) * 128, slab * 512:(slab + 1) * 512]), 16)
            wld(sc)
            wdv(te)
            rec_pe_ab = [sPE.v, sPE.v]
            rec_ac_ab = {}
            for u in range(32):
                hs, j = u // 8, u % 8
                par = u % 2
                sc.wait_ge(sPE.h, rec_pe_ab[par])
                sc.activation(wab[:, par * 128:(par + 1) * 128],
                              wsl[:, j * 512 + hs * 128: j * 512 + (hs + 1) * 128], AF.Abs)
                rec_ac_ab[u] = sAC.inc(sc.drain(), 1)
                te.wait_ge(sAC.h, rec_ac_ab[u])
                te.matmul(psP2[:, hs * 64:(hs + 1) * 64], wab[:, par * 128:(par + 1) * 128],
                          ones64b[:], start=(u == 0), stop=(u == 31),
                          skip_group_check=True)
                rec_pe_ab[par] = sPE.inc(te.drain(), 1)
            wpe(sc)
            sc.activation(alc[:], psP2[:, 0:256].rearrange("p (h s) -> p h s", h=4)[:, :, 0],
                          AF.Identity, scale=np.float32(1.0 / 1024.0))
            sAC.inc(sc.drain(), 1)
            wac(sy)
            sST.inc(sy.dma_start(
                out=bass.AP(al1d.tensor, slab * 512, [[1, 128], [128, 4]]),
                in_=alc[:], allow_slow_non_contiguous=True), 16)
            wst(sy)
            sLD.inc(sy.dma_start(
                out=abca[:], in_=bass.AP(al1d.tensor, slab * 512, [[0, 128], [1, 512]]),
                allow_slow_non_contiguous=True), 16)
            wst(sc)
            for j in range(8):
                sc.sign(wsg[:, j * 512:(j + 1) * 512], wsl[:, j * 512:(j + 1) * 512])
            sAC.inc(sc.drain(), 1)
            wac(ve)
            wld(ve)
            for j in range(8):
                ve.tensor_tensor(wsg[:, j * 512:(j + 1) * 512],
                                 wsg[:, j * 512:(j + 1) * 512], abca[:], ALU.mult)
            sDV.inc(ve.drain(), 1)
            wdv(sy)
            for j in range(8):
                sST.inc(sy.dma_start(
                    out=bw1d[j * 128:(j + 1) * 128, slab * 512:(slab + 1) * 512],
                    in_=wsg[:, j * 512:(j + 1) * 512]), 16)

        for slab in range(2):
            for half in range(4):
                wst(sy)
                wdv(sy)
                for j in range(8):
                    hc = half * 8 + j
                    sLD.inc(sy.dma_start(
                        out=wsl[:, j * 512:(j + 1) * 512],
                        in_=w2T[hc * 128:(hc + 1) * 128, slab * 512:(slab + 1) * 512]), 16)
                wld(sc)
                wdv(te)
                rec_pe_ab = [sPE.v, sPE.v]
                rec_ac_ab = {}
                for u in range(32):
                    ds_, j = u // 8, u % 8
                    par = u % 2
                    sc.wait_ge(sPE.h, rec_pe_ab[par])
                    sc.activation(wab[:, par * 128:(par + 1) * 128],
                                  wsl[:, j * 512 + ds_ * 128: j * 512 + (ds_ + 1) * 128],
                                  AF.Abs)
                    rec_ac_ab[u] = sAC.inc(sc.drain(), 1)
                    te.wait_ge(sAC.h, rec_ac_ab[u])
                    te.matmul(psP2[:, ds_ * 64:(ds_ + 1) * 64],
                              wab[:, par * 128:(par + 1) * 128], ones64b[:],
                              start=(half == 0 and u == 0),
                              stop=(half == 3 and u == 31),
                              skip_group_check=True)
                    rec_pe_ab[par] = sPE.inc(te.drain(), 1)
                wst(sc)
                for j in range(8):
                    sc.sign(wsg[:, j * 512:(j + 1) * 512], wsl[:, j * 512:(j + 1) * 512])
                sAC.inc(sc.drain(), 1)
                wac(sy)
                for j in range(8):
                    hc = half * 8 + j
                    sST.inc(sy.dma_start(
                        out=bw2d[hc * 128:(hc + 1) * 128, slab * 512:(slab + 1) * 512],
                        in_=wsg[:, j * 512:(j + 1) * 512]), 16)
            wpe(sc)
            sc.activation(alc[:], psP2[:, 0:256].rearrange("p (h s) -> p h s", h=4)[:, :, 0],
                          AF.Identity, scale=np.float32(1.0 / 4096.0))
            sAC.inc(sc.drain(), 1)
            wac(sy)
            sST.inc(sy.dma_start(
                out=bass.AP(al2d.tensor, slab * 512, [[1, 128], [128, 4]]),
                in_=alc[:], allow_slow_non_contiguous=True), 16)
            wst(sy)
            sLD.inc(sy.dma_start(
                out=abca[:], in_=bass.AP(al2d.tensor, slab * 512, [[0, 128], [1, 512]]),
                allow_slow_non_contiguous=True), 16)
            for half in range(4):
                wst(sy)
                wdv(sy)
                for j in range(8):
                    hc = half * 8 + j
                    sLD.inc(sy.dma_start(
                        out=wsg[:, j * 512:(j + 1) * 512],
                        in_=bw2d[hc * 128:(hc + 1) * 128, slab * 512:(slab + 1) * 512]), 16)
                wld(ve)
                for j in range(8):
                    ve.tensor_tensor(wsg[:, j * 512:(j + 1) * 512],
                                     wsg[:, j * 512:(j + 1) * 512], abca[:], ALU.mult)
                sDV.inc(ve.drain(), 1)
                wdv(sy)
                for j in range(8):
                    hc = half * 8 + j
                    sST.inc(sy.dma_start(
                        out=bw2d[hc * 128:(hc + 1) * 128, slab * 512:(slab + 1) * 512],
                        in_=wsg[:, j * 512:(j + 1) * 512]), 16)
        wst(sy)
        barrier()
        pctx.__exit__(None, None, None)
        if STOP_AFTER == "prep":
            if KDBG:
                wst(sy)
                sLD.inc(sy.dma_start(out=wab[:, 0:512], in_=bw1d[0:128, 0:512]), 16)
                wld(ve)
                ve.tensor_copy(wsl[:, 0:512], wab[:, 0:512])
                sDV.inc(ve.drain(), 1)
                wdv(sy)
                sST.inc(sy.dma_start(out=dbg_pp[0:128, 1024:1536], in_=wsl[:, 0:512]), 16)
                wst(sy)
                sLD.inc(sy.dma_start(out=wab[:, 0:512], in_=bw2d[384:512, 512:1024]), 16)
                wld(ve)
                ve.tensor_copy(wsl[:, 0:512], wab[:, 0:512])
                sDV.inc(ve.drain(), 1)
                wdv(sy)
                sST.inc(sy.dma_start(out=dbg_pp[0:128, 1536:2048], in_=wsl[:, 0:512]), 16)
                sy.wait_ge(sST.h, sST.v)
            pctx.__exit__(None, None, None)
            actx.__exit__(None, None, None)
            return

        def make_ws(ctx, tag):
            w = {}
            w["xb"] = mksb(ctx, f"xb{tag}", [128, 4 * D], F32)
            w["sq"] = mksb(ctx, f"sq{tag}", [128, D], F32)
            w["st_s"] = mksb(ctx, f"sts{tag}", [128, 4], F32)
            w["st_q"] = mksb(ctx, f"stq{tag}", [128, 4], F32)
            w["mean4"] = mksb(ctx, f"mean{tag}", [128, 4], F32)
            w["var4"] = mksb(ctx, f"var{tag}", [128, 4], F32)
            w["rstd4"] = mksb(ctx, f"rstd{tag}", [128, 4], F32)
            w["nmr4"] = mksb(ctx, f"nmr{tag}", [128, 4], F32)
            w["nt1"] = mksb(ctx, f"nt1{tag}", [128, 4], F32)
            w["nt2"] = mksb(ctx, f"nt2{tag}", [128, 4], F32)
            w["xn_tm"] = mksb(ctx, f"xntm{tag}", [128, 4 * D], BF16)
            w["xn_fm"] = mksb(ctx, f"xnfm{tag}", [128, 8 * 512], BF16)
            w["h_fm"] = mksb(ctx, f"hfm{tag}", [128, 32 * 512], BF16)
            w["w1s"] = mksb(ctx, f"w1s{tag}", [128, 2, 8 * 512], BF16)
            w["w2s"] = mksb(ctx, f"w2s{tag}", [128, 2, 512], BF16)
            w["psT"] = mkps(ctx, f"psT{tag}", [128, 1024], BF16)
            w["psH"] = [mkps(ctx, f"psH{tag}a", [128, 512], F32),
                        mkps(ctx, f"psH{tag}b", [128, 512], F32)]
            w["psY"] = [mkps(ctx, f"psY{tag}{i}", [128, 512], F32) for i in range(4)]
            return w

        def emit_ln(w, src, dst):
            for t in range(4):
                xt = src[:, t * D:(t + 1) * D]
                ve.reduce_sum(w["st_s"][:, t:t + 1], xt, axis=AX)
                wdv(sc)
                sc.square(w["sq"][:], xt)
                sAC.inc(sc.drain(), 1)
                wac(ve)
                ve.reduce_sum(w["st_q"][:, t:t + 1], w["sq"][:], axis=AX)
                sDV.inc(ve.drain(), 1)
            ve.tensor_scalar_mul(w["mean4"][:], w["st_s"][:], 1.0 / D)
            sDV.inc(ve.drain(), 1)
            wdv(sc)
            sc.square(w["nt1"][:], w["mean4"][:])
            sAC.inc(sc.drain(), 1)
            wac(ve)
            ve.scalar_tensor_tensor(w["var4"][:], w["st_q"][:], 1.0 / D, w["nt1"][:],
                                    ALU.mult, ALU.subtract)
            sDV.inc(ve.drain(), 1)
            wdv(sc)
            sc.activation(w["rstd4"][:], w["var4"][:], AF.Sqrt, bias=np.float32(EPS))
            sAC.inc(sc.drain(), 1)
            wac(ve)
            ve.reciprocal(w["nt2"][:], w["rstd4"][:])
            sDV.inc(ve.drain(), 1)
            wdv(ve)
            ve.scalar_tensor_tensor(w["nmr4"][:], w["mean4"][:], -1.0, w["nt2"][:],
                                    ALU.mult, ALU.mult)
            sDV.inc(ve.drain(), 1)
            wdv(ve)
            wpe(ve)
            for t in range(4):
                ve.tensor_scalar(dst[:, t * D:(t + 1) * D], src[:, t * D:(t + 1) * D],
                                 w["nt2"][:, t:t + 1], w["nmr4"][:, t:t + 1],
                                 ALU.mult, ALU.add)
            sDV.inc(ve.drain(), 1)

        def emit_tp(w, src, dst, ntile=4):
            for t in range(ntile):
                wdv(te)
                for j in range(8):
                    te.transpose(w["psT"][:, j * 128:(j + 1) * 128],
                                 src[:, t * D + j * 128: t * D + (j + 1) * 128],
                                 identb[:])
                sPE.inc(te.drain(), 1)
                wpe(ve)
                dv = dst[:].rearrange("p (j n) -> p j n", j=8)
                sv = w["psT"][:].rearrange("p (j n) -> p j n", j=8)
                ve.tensor_copy(dv[:, :, t * 128:(t + 1) * 128], sv[:, :, :])
                sDV.inc(ve.drain(), 1)

        def emit_ffn(w, xres, yout):
            rec_pe_w1 = [0, 0]
            rec_ac_h = [0, 0]
            rec_sc_w1 = {}

            def load_w1(hg_):
                par_ = hg_ % 2
                sc.wait_ge(sPE.h, rec_pe_w1[par_])
                for j in range(8):
                    sSC.inc(sc.dma_start(
                        out=w["w1s"][:, par_, j * 512:(j + 1) * 512],
                        in_=bw1d[j * 128:(j + 1) * 128, hg_ * 512:(hg_ + 1) * 512]), 16)
                rec_sc_w1[hg_] = sSC.v

            wdv(te)
            load_w1(0)
            for hg in range(8):
                par = hg % 2
                if hg + 1 < 8:
                    load_w1(hg + 1)
                te.wait_ge(sSC.h, rec_sc_w1[hg])
                for ht in range(4):
                    hpar = ht % 2
                    hidx = hg * 4 + ht
                    te.wait_ge(sAC.h, rec_ac_h[hpar])
                    for j in range(8):
                        te.matmul(w["psH"][hpar][:],
                                  w["w1s"][:, par, j * 512 + ht * 128: j * 512 + (ht + 1) * 128],
                                  w["xn_fm"][:, j * 512:(j + 1) * 512],
                                  start=(j == 0), stop=False, skip_group_check=True)
                    te.matmul(w["psH"][hpar][:], b1h[0:1, hidx * 128:(hidx + 1) * 128],
                              onesb[0:1, :], start=False, stop=False, skip_group_check=True)
                    te.matmul(w["psH"][hpar][:], b1l[0:1, hidx * 128:(hidx + 1) * 128],
                              onesb[0:1, :], start=False, stop=True, skip_group_check=True)
                    sPE.inc(te.drain(), 1)
                    wpe(sc)
                    sc.activation(w["h_fm"][:, hidx * 512:(hidx + 1) * 512],
                                  w["psH"][hpar][:], AF.Gelu)
                    rec_ac_h[hpar] = sAC.inc(sc.drain(), 1)
                rec_pe_w1[par] = sPE.v
            wac(te)
            rec_pe_w2 = [0, 0]
            rec_sc_w2 = {}

            def load_w2(idx_):
                dh_, hc_ = idx_ // 32, idx_ % 32
                par_ = idx_ % 2
                sc.wait_ge(sPE.h, rec_pe_w2[par_])
                sSC.inc(sc.dma_start(
                    out=w["w2s"][:, par_, :],
                    in_=bw2d[hc_ * 128:(hc_ + 1) * 128, dh_ * 512:(dh_ + 1) * 512]), 16)
                rec_sc_w2[idx_] = sSC.v

            load_w2(0)
            for dh in range(2):
                for hc in range(32):
                    idx = dh * 32 + hc
                    par = idx % 2
                    if idx + 1 < 64:
                        load_w2(idx + 1)
                    te.wait_ge(sSC.h, rec_sc_w2[idx])
                    for tt in range(4):
                        te.matmul(w["psY"][tt][:],
                                  w["h_fm"][:, hc * 512 + tt * 128: hc * 512 + (tt + 1) * 128],
                                  w["w2s"][:, par, :],
                                  start=(hc == 0), stop=False, skip_group_check=True)
                    rec_pe_w2[par] = sPE.inc(te.drain(), 1)
                for tt in range(4):
                    te.matmul(w["psY"][tt][:], onesb[0:1, 0:128],
                              b2h[0:1, dh * 512:(dh + 1) * 512],
                              start=False, stop=False, skip_group_check=True)
                    te.matmul(w["psY"][tt][:], onesb[0:1, 0:128],
                              b2l[0:1, dh * 512:(dh + 1) * 512],
                              start=False, stop=True, skip_group_check=True)
                sPE.inc(te.drain(), 1)
                wpe(ve)
                for tt in range(4):
                    ve.scalar_tensor_tensor(
                        yout[:, tt * D + dh * 512: tt * D + (dh + 1) * 512],
                        w["psY"][tt][:], 0.5,
                        xres[:, tt * D + dh * 512: tt * D + (dh + 1) * 512],
                        ALU.mult, ALU.add)
                sDV.inc(ve.drain(), 1)
                wdv(te)

        wsA = make_ws(actx, "A")
        x1b = mksb(actx, "x1bA", [128, 4 * D], F32)
        kstage = mksb(actx, "kstA", [128, 2, 512], BF16)
        vstage = mksb(actx, "vstA", [128, 2, 520], BF16)
        qin = mksb(actx, "qinA", [128, D], BF16)
        qin_fm = mksb(actx, "qinfA", [128, 8 * 128], BF16)
        qstage = mksb(actx, "qstA", [128, 8 * 128], BF16)

        gp.memset(vstage[:, 0, :], 1.0)
        gp.memset(vstage[:, 1, :], 1.0)
        sGP.inc(gp.drain(), 1)
        wgp(ve)
        for bat in range(NB):
            wdv(sy)
            wac(sy)
            for t in range(4):
                sLD.inc(sy.dma_start(
                    out=wsA["xb"][:, t * D:(t + 1) * D],
                    in_=xin[bat * TB + t * 128: bat * TB + (t + 1) * 128, :]), 16)
            wld(ve)
            wld(sc)
            emit_ln(wsA, wsA["xb"], wsA["xn_tm"])
            emit_tp(wsA, wsA["xn_tm"], wsA["xn_fm"])
            if STOP_AFTER == "A1" and bat == 0:
                break
            emit_ffn(wsA, wsA["xb"], x1b)
            if STOP_AFTER == "A2" and bat == 0:
                break
            wdv(sy)
            for t in range(4):
                sST.inc(sy.dma_start(
                    out=x1d[bat * TB + t * 128: bat * TB + (t + 1) * 128, :],
                    in_=x1b[:, t * D:(t + 1) * D]), 16)
            emit_ln(wsA, x1b, wsA["xn_tm"])
            wdv(sy)
            for t in range(4):
                sST.inc(sy.dma_start(out=xnqd[t],
                                     in_=wsA["xn_tm"][:, t * D:(t + 1) * D]), 16)
            emit_tp(wsA, wsA["xn_tm"], wsA["xn_fm"])
            wdv(te)
            rec_ac_k = [sAC.v, sAC.v]
            rec_st_k = [sST.v, sST.v]
            for mt in range(8):
                par = mt % 2
                te.wait_ge(sAC.h, rec_ac_k[par])
                for j in range(8):
                    te.matmul(wsA["psH"][par][:],
                              wkRT[:, j * D + mt * 128: j * D + (mt + 1) * 128],
                              wsA["xn_fm"][:, j * 512:(j + 1) * 512],
                              start=(j == 0), stop=False, skip_group_check=True)
                te.matmul(wsA["psH"][par][:], bkh[0:1, mt * 128:(mt + 1) * 128],
                          onesb[0:1, :], start=False, stop=False, skip_group_check=True)
                te.matmul(wsA["psH"][par][:], bkl[0:1, mt * 128:(mt + 1) * 128],
                          onesb[0:1, :], start=False, stop=True, skip_group_check=True)
                sPE.inc(te.drain(), 1)
                wpe(sc)
                sc.wait_ge(sST.h, rec_st_k[par])
                sc.activation(kstage[:, par, :], wsA["psH"][par][:], AF.Identity)
                rec_ac_k[par] = sAC.inc(sc.drain(), 1)
                wac(sy)
                rec_st_k[par] = sST.inc(sy.dma_start(
                    out=kd[mt * 128:(mt + 1) * 128, bat * TB:(bat + 1) * TB],
                    in_=kstage[:, par, :]), 16)
            if STOP_AFTER == "A3" and bat == 0:
                break
            wac(te)
            rec_dv_v = [sDV.v, sDV.v]
            rec_st_v = [sST.v, sST.v]
            for tt in range(4):
                for mh in range(2):
                    par = (tt * 2 + mh) % 2
                    te.wait_ge(sDV.h, rec_dv_v[par])
                    for j in range(8):
                        te.matmul(wsA["psH"][par][:],
                                  wsA["xn_fm"][:, j * 512 + tt * 128: j * 512 + (tt + 1) * 128],
                                  wvTb[:, j * D + mh * 512: j * D + (mh + 1) * 512],
                                  start=(j == 0), stop=False, skip_group_check=True)
                    te.matmul(wsA["psH"][par][:], onesb[0:1, 0:128],
                              bvh[0:1, mh * 512:(mh + 1) * 512],
                              start=False, stop=False, skip_group_check=True)
                    te.matmul(wsA["psH"][par][:], onesb[0:1, 0:128],
                              bvl[0:1, mh * 512:(mh + 1) * 512],
                              start=False, stop=True, skip_group_check=True)
                    sPE.inc(te.drain(), 1)
                    ve.wait_ge(sST.h, rec_st_v[par])
                    wpe(ve)
                    for g in range(8):
                        ve.tensor_copy(vstage[:, par, g * 65:g * 65 + 64],
                                       wsA["psH"][par][:, g * 64:(g + 1) * 64])
                    rec_dv_v[par] = sDV.inc(ve.drain(), 1)
                    wdv(sy)
                    rec_st_v[par] = sST.inc(sy.dma_start(
                        out=vd[bat * 4 + tt, :, mh * 520:(mh + 1) * 520],
                        in_=vstage[:, par, :]), 16)
            if STOP_AFTER == "A4" and bat == 0:
                break
            wst(sy)
            sLD.inc(sy.dma_start(
                out=qin[:],
                in_=xnqd[ds(ccv, 1), :, :].rearrange("o p n -> (o p) n")), 16)
            wld(te)
            wdv(te)
            for j in range(8):
                te.transpose(wsA["psT"][:, j * 128:(j + 1) * 128],
                             qin[:, j * 128:(j + 1) * 128], identb[:])
            sPE.inc(te.drain(), 1)
            wpe(ve)
            ve.tensor_copy(qin_fm[:], wsA["psT"][:])
            sDV.inc(ve.drain(), 1)
            wdv(te)
            wst(sc)
            rec_ac_q = [sAC.v, sAC.v]
            for mt in range(8):
                par = mt % 2
                te.wait_ge(sAC.h, rec_ac_q[par])
                for j in range(8):
                    te.matmul(wsA["psH"][par][:, 0:128],
                              wqRT[:, j * D + mt * 128: j * D + (mt + 1) * 128],
                              qin_fm[:, j * 128:(j + 1) * 128],
                              start=(j == 0), stop=False, skip_group_check=True)
                te.matmul(wsA["psH"][par][:, 0:128], bqh[0:1, mt * 128:(mt + 1) * 128],
                          onesb[0:1, 0:128], start=False, stop=False, skip_group_check=True)
                te.matmul(wsA["psH"][par][:, 0:128], bql[0:1, mt * 128:(mt + 1) * 128],
                          onesb[0:1, 0:128], start=False, stop=True, skip_group_check=True)
                sPE.inc(te.drain(), 1)
                wpe(sc)
                sc.activation(qstage[:, mt * 128:(mt + 1) * 128],
                              wsA["psH"][par][:, 0:128], AF.Identity)
                rec_ac_q[par] = sAC.inc(sc.drain(), 1)
            wac(sy)
            sST.inc(sy.dma_start(out=qd[bat], in_=qstage[:]), 16)
            wst(sy)
            if STOP_AFTER == "A5" and bat == 0:
                break
        wst(sy)
        barrier()
        if KDBG and STOP_AFTER == "A2":
            for t in range(4):
                sST.inc(sy.dma_start(out=dbg_x1[t * 128:(t + 1) * 128, :],
                                     in_=x1b[:, t * D:(t + 1) * D]), 16)
            sST.inc(sy.dma_start(out=dbg_k, in_=wsA["h_fm"][:, 0:512]), 16)
            sST.inc(sy.dma_start(out=dbg_q, in_=wsA["h_fm"][:, 512:1536]), 16)
            wst(sy)
        if KDBG and STOP_AFTER == "A1":
            for t in range(4):
                sST.inc(sy.dma_start(out=dbg_x1[t * 128:(t + 1) * 128, :],
                                     in_=wsA["xb"][:, t * D:(t + 1) * D]), 16)
            ve.tensor_copy(x1b[:, 0:D], wsA["xn_tm"][:, 0:D])
            ve.tensor_copy(x1b[:, D:2 * D], wsA["xn_fm"][:, 0:D])
            sDV.inc(ve.drain(), 1)
            wdv(sy)
            sST.inc(sy.dma_start(out=dbg_pp[:, 0:1024], in_=x1b[:, 0:D]), 16)
            sST.inc(sy.dma_start(out=dbg_pp[:, 1024:2048], in_=x1b[:, D:2 * D]), 16)
            sST.inc(sy.dma_start(out=dbg_pp[:, 2048:2052], in_=wsA["st_s"][:]), 16)
            sST.inc(sy.dma_start(out=dbg_pp[:, 2052:2056], in_=wsA["st_q"][:]), 16)
            sST.inc(sy.dma_start(out=dbg_pp[:, 2056:2060], in_=wsA["rstd4"][:]), 16)
            sST.inc(sy.dma_start(out=dbg_pp[:, 2060:2064], in_=wsA["nmr4"][:]), 16)
            wst(sy)
        if KDBG and STOP_AFTER not in ("A1", "A2"):
            for t in range(4):
                sLD.inc(sy.dma_start(out=wsA["xb"][:, t * D:(t + 1) * D],
                                     in_=x1d[t * 128:(t + 1) * 128, :]), 16)
            wld(sy)
            for t in range(4):
                sST.inc(sy.dma_start(out=dbg_x1[t * 128:(t + 1) * 128, :],
                                     in_=wsA["xb"][:, t * D:(t + 1) * D]), 16)
            sLD.inc(sy.dma_start(out=kstage[:, 0, :], in_=kd[0:128, 0:512]), 16)
            sLD.inc(sy.dma_start(out=qstage[:], in_=qd[0]), 16)
            sLD.inc(sy.dma_start(out=vstage[:, 0, 0:520], in_=vd[0][:, 0:520]), 16)
            sLD.inc(sy.dma_start(out=vstage[:, 1, 0:520], in_=vd[0][:, 520:1040]), 16)
            wld(sy)
            sST.inc(sy.dma_start(out=dbg_k, in_=kstage[:, 0, :]), 16)
            sST.inc(sy.dma_start(out=dbg_q, in_=qstage[:]), 16)
            sST.inc(sy.dma_start(out=dbg_v[:, 0:520], in_=vstage[:, 0, 0:520]), 16)
            sST.inc(sy.dma_start(out=dbg_v[:, 520:1040], in_=vstage[:, 1, 0:520]), 16)
            wst(sy)
            barrier()
        actx.__exit__(None, None, None)
        if STOP_AFTER in ("A", "A1", "A2", "A3", "A4", "A5"):
            return

        bctx = ExitStack()
        bctx.__enter__()
        o_fm = mksb(bctx, "o_fm", [128, 8 * 2048], BF16)
        battn = ExitStack()
        battn.__enter__()
        ksl = mksb(battn, "ksl", [128, 2, 8 * 512], BF16)
        vsl = mksb(battn, "vsl", [128, 2, 4 * 1040], BF16)
        qsl = mksb(battn, "qsl", [128, 2, 1024], BF16)
        qdup = mksb(battn, "qdup", [128, 2, 512], BF16)
        expS = mksb(battn, "expS", [128, 2, 1024], BF16)
        recq = mksb(battn, "recq", [1, 512], F32)
        rech = mksb(battn, "rech", [1, 512], BF16)
        bcst = mksb(battn, "bcst", [64, 512], F32)
        psS = mkps(battn, "psS", [128, 2048], F32)
        dbgs = mksb(battn, "dbgsB", [128, 1024], F32) if KDBG else None
        psO = [mkps(battn, "psOa", [128, 512], F32), mkps(battn, "psOb", [128, 512], F32)]
        psB = mkps(battn, "psB", [64, 512], F32)

        for tb_ in (ksl, vsl, qsl, qdup, expS):
            gp.memset(tb_[:], 0.0)
        gp.memset(bcst[:], 0.0)
        sGP.inc(gp.drain(), 1)
        barrier()

        gp_head = []

        def load_head(hd, par):
            sGP.inc(sy.dma_start(out=qsl[:, par, :], in_=qd[hd]), 16)
            for mt in range(8):
                sGP.inc(sy.dma_start(
                    out=ksl[:, par, mt * 512:(mt + 1) * 512],
                    in_=kd[mt * 128:(mt + 1) * 128, hd * TB:(hd + 1) * TB]), 16)
            for tt in range(4):
                sGP.inc(sy.dma_start(
                    out=vsl[:, par, tt * 1040:(tt + 1) * 1040],
                    in_=vd[hd * 4 + tt]), 16)
            gp_head.append(sGP.v)

        rec_pe_head_end = [0, 0]
        rec_dv_norm = [0, 0]
        rec_dv_qdup = {}

        def emit_qdup(hd_, G_):
            par_ = hd_ % 2
            gpar_ = G_ % 2
            ve.wait_ge(sGP.h, gp_head[hd_])
            for gl_ in range(4):
                g_ = G_ * 4 + gl_
                srcq_ = qsl[(g_ % 2) * 64:(g_ % 2) * 64 + 64, par_,
                            (g_ // 2) * 128:(g_ // 2 + 1) * 128]
                ve.tensor_copy(qdup[0:64, gpar_, gl_ * 128:(gl_ + 1) * 128], srcq_)
                ve.tensor_copy(qdup[64:128, gpar_, gl_ * 128:(gl_ + 1) * 128], srcq_)
            return sDV.inc(ve.drain(), 1)

        load_head(0, 0)
        rec_dv_qdup[(0, 0)] = emit_qdup(0, 0)
        for hd in range(NH):
            par = hd % 2
            if hd + 1 < NH:
                sy.wait_ge(sPE.h, rec_pe_head_end[(hd + 1) % 2])
                load_head(hd + 1, (hd + 1) % 2)
            te.wait_ge(sGP.h, gp_head[hd])
            for G in range(4):
                gpar = G % 2
                te.wait_ge(sDV.h, rec_dv_qdup[(hd, G)])
                rec_ac_exp = {}

                def emit_pv(pv):
                    mtp, ttpp = pv // 4, pv % 4
                    pp2 = pv % 2
                    te.wait_ge(sAC.h, rec_ac_exp[pv])
                    if pv == 0:
                        te.wait_ge(sDV.h, rec_dv_norm[gpar])
                    te.matmul(psO[gpar][0:65, :],
                              vsl[:, par, ttpp * 1040 + (2 * mtp) * 65:
                                        ttpp * 1040 + (2 * mtp) * 65 + 65],
                              expS[:, pp2, 0:512],
                              start=(pv == 0), stop=False, skip_group_check=True)
                    te.matmul(psO[gpar][0:65, :],
                              vsl[:, par, ttpp * 1040 + (2 * mtp + 1) * 65:
                                        ttpp * 1040 + (2 * mtp + 1) * 65 + 65],
                              expS[:, pp2, 512:1024],
                              start=False, stop=(pv == 31), skip_group_check=True)

                for pj in range(32):
                    mt, ttp = pj // 4, pj % 4
                    p2 = pj % 2
                    if pj >= 2:
                        te.wait_ge(sAC.h, rec_ac_exp[pj - 2])
                    kslice = ksl[:, par, mt * 512 + ttp * 128: mt * 512 + (ttp + 1) * 128]
                    te.matmul(psS[:, p2 * 1024: p2 * 1024 + 512],
                              kslice[0:64, :], qdup[0:64, gpar, :],
                              start=True, stop=True, tile_position=(0, 0),
                              skip_group_check=True)
                    te.matmul(psS[:, p2 * 1024 + 512: p2 * 1024 + 1024],
                              kslice[64:128, :], qdup[64:128, gpar, :],
                              start=True, stop=True, tile_position=(64, 0),
                              skip_group_check=True)
                    pe_qk = sPE.inc(te.drain(), 1)
                    sc.wait_ge(sPE.h, pe_qk)
                    sc.activation(expS[:, p2, :], psS[:, p2 * 1024:(p2 + 1) * 1024],
                                  AF.Exp, bias=np.float32(-EXPC), scale=np.float32(0.125))
                    rec_ac_exp[pj] = sAC.inc(sc.drain(), 1)
                    if pj >= 1:
                        emit_pv(pj - 1)
                emit_pv(31)
                sPE.inc(te.drain(), 1)
                nxt = (hd, G + 1) if G < 3 else \
                      ((hd + 1, 0) if hd + 1 < NH else None)
                if nxt is not None:
                    rec_dv_qdup[nxt] = emit_qdup(*nxt)
                wpe(ve)
                ve.reciprocal(recq[:], psO[gpar][64:65, :])
                sDV.inc(ve.drain(), 1)
                wdv(ve)
                ve.tensor_copy(rech[:], recq[:])
                sDV.inc(ve.drain(), 1)
                wdv(te)
                te.matmul(psB[0:64, :], onesb[0:1, 0:64], rech[0:1, :],
                          start=True, stop=True, skip_group_check=True)
                sPE.inc(te.drain(), 1)
                wpe(sc)
                sc.wait_ge(sDV.h, rec_dv_norm[1 - gpar])
                sc.activation(bcst[0:64, :], psB[0:64, :], AF.Identity)
                sAC.inc(sc.drain(), 1)
                wac(ve)
                mt_o, basep = hd // 2, (hd % 2) * 64
                ve.tensor_tensor(
                    o_fm[basep:basep + 64,
                         mt_o * 2048 + G * 512: mt_o * 2048 + (G + 1) * 512],
                    psO[gpar][0:64, :], bcst[0:64, :], ALU.mult)
                if KDBG and hd == 0 and G == 0:
                    ve.tensor_copy(dbgs[0:65, 0:512], psO[gpar][0:65, :])
                    ve.tensor_copy(dbgs[0:64, 512:1024],
                                   o_fm[0:64, 0:512])
                rec_dv_norm[gpar] = sDV.inc(ve.drain(), 1)
            rec_pe_head_end[par] = sPE.v
        barrier()
        if KDBG:
            sST.inc(sy.dma_start(out=dbg_o, in_=o_fm[:, 0:2048]), 16)
            sST.inc(sy.dma_start(out=dbg_pp[0:65, 2560:3072], in_=dbgs[0:65, 0:512]), 16)
            sST.inc(sy.dma_start(out=dbg_pp[0:64, 3072:3584], in_=dbgs[0:64, 512:1024]), 16)
            sST.inc(sy.dma_start(out=dbg_bw1, in_=o_fm[0:128, 0:512]), 16)
            wst(sy)
        battn.__exit__(None, None, None)
        if STOP_AFTER == "B":
            bctx.__exit__(None, None, None)
            return

        cctx = ExitStack()
        cctx.__enter__()
        woTb = mksb(cctx, "woTb", [128, 8 * D], BF16)
        wsC = make_ws(cctx, "C")
        x1t = mksb(cctx, "x1tC", [128, D], F32)
        outb = mksb(cctx, "outbC", [128, 4 * D], F32)
        x2b = wsC["xb"]

        for j in range(8):
            wdv(sy)
            sLD.inc(sy.dma_start(out=x2b[:, 0:D], in_=woT[j * 128:(j + 1) * 128, :]), 16)
            wld(ve)
            ve.tensor_copy(woTb[:, j * D:(j + 1) * D], x2b[:, 0:D])
            sDV.inc(ve.drain(), 1)
        wdv(sy)

        for sbi in range(4):
            wac(ve)
            for gl in range(4):
                g = sbi * 4 + gl
                G, g_l = g // 4, g % 4
                wdv(sy)
                sLD.inc(sy.dma_start(
                    out=x1t[:],
                    in_=x1d.rearrange("(a b) d -> a b d", b=16)[ds(ccv128, 128), g, :]), 16)
                for jh in range(2):
                    par = jh % 2
                    wdv(te)
                    for mt in range(8):
                        te.matmul(wsC["psH"][par][:],
                                  o_fm[:, mt * 2048 + G * 512 + g_l * 128:
                                       mt * 2048 + G * 512 + (g_l + 1) * 128],
                                  woTb[:, mt * D + jh * 512: mt * D + (jh + 1) * 512],
                                  start=(mt == 0), stop=False, skip_group_check=True)
                    te.matmul(wsC["psH"][par][:], onesb[0:1, 0:128],
                              boh[0:1, jh * 512:(jh + 1) * 512],
                              start=False, stop=False, skip_group_check=True)
                    te.matmul(wsC["psH"][par][:], onesb[0:1, 0:128],
                              bol[0:1, jh * 512:(jh + 1) * 512],
                              start=False, stop=True, skip_group_check=True)
                    sPE.inc(te.drain(), 1)
                    wpe(ve)
                    wld(ve)
                    ve.tensor_add(x2b[:, gl * D + jh * 512: gl * D + (jh + 1) * 512],
                                  x1t[:, jh * 512:(jh + 1) * 512], wsC["psH"][par][:])
                    sDV.inc(ve.drain(), 1)
            wdv(sc)
            emit_ln(wsC, x2b, wsC["xn_tm"])
            emit_tp(wsC, wsC["xn_tm"], wsC["xn_fm"])
            wst(ve)
            emit_ffn(wsC, x2b, outb)
            wdv(sy)
            for gl in range(4):
                g = sbi * 4 + gl
                sST.inc(sy.dma_start(out=out[g * 128:(g + 1) * 128, :],
                                     in_=outb[:, gl * D:(gl + 1) * D]), 16)
        wst(sy)

        cctx.__exit__(None, None, None)
        bctx.__exit__(None, None, None)

    gctx.__exit__(None, None, None)
    return nc


_NC_CACHE = {}


def kernel(L0_attn_x, L0_attn_wq, L0_attn_bq, L0_attn_wk, L0_attn_bk, L0_attn_wv,
           L0_attn_bv, L0_attn_wo, L0_attn_bo, L0_attn_rot_cos, L0_attn_rot_sin,
           L0_ffn_w1, L0_ffn_b1, L0_ffn_w2, L0_ffn_b2):
    x = np.asarray(L0_attn_x, dtype=np.float32)
    if "nc" not in _NC_CACHE:
        _NC_CACHE["nc"] = build_nc()
    nc = _NC_CACHE["nc"]

    f32 = np.float32
    common = {
        "w1T": np.ascontiguousarray(np.asarray(L0_ffn_w1, f32).T),
        "w2T": np.ascontiguousarray(np.asarray(L0_ffn_w2, f32).T),
        "wqT": np.ascontiguousarray(np.asarray(L0_attn_wq, f32).T),
        "wkT": np.ascontiguousarray(np.asarray(L0_attn_wk, f32).T),
        "wvT": np.ascontiguousarray(np.asarray(L0_attn_wv, f32).T),
        "woT": np.ascontiguousarray(np.asarray(L0_attn_wo, f32).T),
        "b1": np.ascontiguousarray(np.asarray(L0_ffn_b1, f32)),
        "b2": np.ascontiguousarray(np.asarray(L0_ffn_b2, f32)),
        "bq": np.ascontiguousarray(np.asarray(L0_attn_bq, f32)),
        "bk": np.ascontiguousarray(np.asarray(L0_attn_bk, f32)),
        "bv": np.ascontiguousarray(np.asarray(L0_attn_bv, f32)),
        "bo": np.ascontiguousarray(np.asarray(L0_attn_bo, f32)),
        "rotc": np.ascontiguousarray(np.asarray(L0_attn_rot_cos, f32)[4095]),
        "rots": np.ascontiguousarray(np.asarray(L0_attn_rot_sin, f32)[4095]),
    }
    in_maps = []
    for c in range(8):
        m = dict(common)
        m["xin"] = np.ascontiguousarray(x[c // 4])
        in_maps.append(m)

    res = run_bass_kernel_spmd(nc, in_maps, core_ids=list(range(8)))

    outp = np.empty((2, TFULL, D), np.float32)
    for c in range(8):
        bc, cc = c // 4, c % 4
        r = np.asarray(res.results[c]["out"]).reshape(16, 128, D)
        outp[bc, cc * 2048:(cc + 1) * 2048] = r.transpose(1, 0, 2).reshape(2048, D)
    return outp
